# version 10
# baseline (speedup 1.0000x reference)
import numpy as np
import ml_dtypes
from contextlib import ExitStack

import concourse.bass as bass
import concourse.mybir as mybir
from concourse.ap import AP
from concourse.bass_utils import run_bass_kernel_spmd

BF16 = mybir.dt.bfloat16
F32 = mybir.dt.float32
AF = mybir.ActivationFunctionType
ALU = mybir.AluOpType

D = 1024
I2 = 2048
DFF = 4096
NG = 8
KTAP = 4
HALO = 32
HT = 1024
TH = HALO + HT
NCORES = 8
EPS = 1e-5
TILES = [(0, HALO), (HALO, 512), (HALO + 512, 512)]
OWN_TILES = TILES[1:]
DC, IC = 8, 16


def _tiles(WT):
    return {(k, m): WT[128 * k:128 * (k + 1), 128 * m:128 * (m + 1)]
            for k in range(WT.shape[0] // 128) for m in range(WT.shape[1] // 128)}


def _blob(entries):
    ix, arrs = {}, []
    for i, (key, a) in enumerate(entries):
        ix[key] = i
        arrs.append(np.asarray(a, np.float32))
    return np.ascontiguousarray(np.concatenate(arrs, axis=1)), ix


def _col_pack(vec, nch):
    return np.ascontiguousarray(np.asarray(vec, np.float32).reshape(nch, 128).T)


def build_host(inputs):
    f32 = lambda k: np.ascontiguousarray(np.asarray(inputs[k], np.float32))
    ipw = f32("L0_attn_in_proj_w")
    WhT = np.ascontiguousarray(ipw[:I2, :].T)
    WgT = np.ascontiguousarray(ipw[I2:, :].T)
    cw = f32("L0_attn_conv_w")
    fcT = np.ascontiguousarray(f32("L0_attn_fc_w").T)
    cpT = np.ascontiguousarray(f32("L0_attn_cproj_w").T)
    owT = np.ascontiguousarray(f32("L0_attn_out_w").T)

    tU1, tV1 = _tiles(f32("L0_ffn_U1")), _tiles(f32("L0_ffn_V1"))
    tU2, tV2 = _tiles(f32("L0_ffn_U2")), _tiles(f32("L0_ffn_V2"))
    ents = ([(("U1", k, m), tU1[(k, m)]) for m in range(2) for k in range(8)]
            + [(("V1", k, m), tV1[(k, m)]) for m in range(32) for k in range(2)]
            + [(("U2", k, m), tU2[(k, m)]) for m in range(2) for k in range(32)]
            + [(("V2", k, m), tV2[(k, m)]) for m in range(8) for k in range(2)])
    ffn_blob, ffn_ix = _blob(ents)

    tWh, tWg = _tiles(WhT), _tiles(WgT)
    ip_chunks, ip_ix = [], None
    for g in range(NG):
        ents = ([(("Wh", k, ml), tWh[(k, 2 * g + ml)])
                 for ml in range(2) for k in range(8)]
                + [(("Wg", k, ml), tWg[(k, 2 * g + ml)])
                   for ml in range(2) for k in range(8)])
        for mc in range(2):
            for tap in range(KTAP):
                CT = np.ascontiguousarray(cw[256 * g:256 * (g + 1), :, tap].T)
                for kc in range(2):
                    ents.append((("cv", tap, kc, mc),
                                 CT[128 * kc:128 * (kc + 1),
                                    128 * mc:128 * (mc + 1)]))
        blob, ip_ix = _blob(ents)
        ip_chunks.append(blob)
    ip_blob = np.ascontiguousarray(np.concatenate(ip_chunks, axis=1))

    tfc, tcp, tow = _tiles(fcT), _tiles(cpT), _tiles(owT)
    fc_chunks, cp_chunks, ow_chunks = [], [], []
    for c in range(4):
        b, fc_ix = _blob([(("fc", ml, k), tfc[(k, 2 * c + ml)])
                          for ml in range(2) for k in range(16)])
        fc_chunks.append(b)
    for c in range(4):
        b, cp_ix = _blob([(("cp", ml, k), tcp[(k, 4 * c + ml)])
                          for ml in range(4) for k in range(8)])
        cp_chunks.append(b)
    for c in range(4):
        b, ow_ix = _blob([(("ow", ml, k), tow[(k, 2 * c + ml)])
                          for ml in range(2) for k in range(16)])
        ow_chunks.append(b)
    fc_blob = np.ascontiguousarray(np.concatenate(fc_chunks, axis=1))
    cp_blob = np.ascontiguousarray(np.concatenate(cp_chunks, axis=1))
    ow_blob = np.ascontiguousarray(np.concatenate(ow_chunks, axis=1))

    cols, parts = {}, []

    def add(name, arr):
        cols[name] = sum(p.shape[1] for p in parts)
        parts.append(arr)

    add("b1", _col_pack(inputs["L0_ffn_b1"], 32))
    add("b2", _col_pack(inputs["L0_ffn_b2"], 8))
    add("fc_b", _col_pack(inputs["L0_attn_fc_b"], 8))
    add("cproj_b", _col_pack(inputs["L0_attn_cproj_b"], 16))
    add("conv_b", _col_pack(inputs["L0_attn_conv_b"], 16))
    add("ln_g", _col_pack(inputs["L0_attn_ln_g"], 16))
    add("ln_b", _col_pack(inputs["L0_attn_ln_b"], 16))
    add("eps", np.full((128, 1), EPS, np.float32))
    add("mask", np.zeros((128, 2), np.float32))
    const_blob = np.ascontiguousarray(np.concatenate(parts, axis=1))

    blobs = dict(ffn_blob=ffn_blob, ip_blob=ip_blob, fc_blob=fc_blob,
                 cp_blob=cp_blob, ow_blob=ow_blob)
    ixs = dict(ffn=ffn_ix, ip=ip_ix, fc=fc_ix, cp=cp_ix, ow=ow_ix)
    return blobs, ixs, const_blob, cols


def shard_x(x_in):
    outs = []
    for core in range(NCORES):
        b, chunk = core // 2, core % 2
        t0 = chunk * 2048
        arr = np.zeros((128, 2 * DC * TH), np.float32)
        for half in range(2):
            base = t0 + half * HT
            xs = np.zeros((TH, D), np.float32)
            lo = base - HALO
            if lo >= 0:
                xs[:HALO] = x_in[b, lo:base]
            xs[HALO:] = x_in[b, base:base + HT]
            fm = xs.T.reshape(DC, 128, TH).transpose(1, 0, 2)
            arr[:, half * DC * TH:(half + 1) * DC * TH] = fm.reshape(128, DC * TH)
        outs.append(np.ascontiguousarray(arr))
    return outs


def gather_y(results):
    out = np.zeros((4, 4096, D), np.float32)
    for core in range(NCORES):
        b, chunk = core // 2, core % 2
        a = np.asarray(results[core]["y"], np.float32).reshape(128, DC, 2048)
        out[b, chunk * 2048:(chunk + 1) * 2048] = a.transpose(2, 1, 0).reshape(2048, D)
    return out


class Buf:
    __slots__ = ("name", "w", "r", "psum")

    def __init__(self, name, psum=False):
        self.name, self.w, self.r, self.psum = name, None, {}, psum


class Em:
    ENG_SEM = {"t": "tc", "v": "vc", "s": "sc", "q": "sd", "g": "gd"}

    def __init__(self, nc, es):
        self.nc = nc
        self.eng = {"t": nc.tensor, "v": nc.vector, "s": nc.scalar,
                    "q": nc.sync, "g": nc.gpsimd}
        self.sems, self.cnt, self.wm = {}, {}, {}
        for sn in ("tc", "vc", "sc", "sd", "gd"):
            self.sems[sn] = es.enter_context(nc.semaphore(sn))
            self.cnt[sn] = 0

    def wait(self, en, sn, val):
        if not val or val <= 0:
            return
        assert val <= self.cnt[sn], (en, sn, val, self.cnt[sn])
        if self.wm.get((en, sn), 0) >= val:
            return
        self.eng[en].wait_ge(self.sems[sn], val)
        self.wm[(en, sn)] = val

    def inc(self, en, instr, n=1):
        sn = self.ENG_SEM[en]
        if en in ("q", "g"):
            n = 16
        instr.then_inc(self.sems[sn], n)
        self.cnt[sn] += n
        return self.cnt[sn]

    def r_wait(self, en, buf):
        if buf.w is not None and buf.w[0] != en:
            self.wait(en, self.ENG_SEM[buf.w[0]], buf.w[1])
        if buf.psum and en in ("v", "s"):
            other = "s" if en == "v" else "v"
            if other in buf.r:
                self.wait(en, self.ENG_SEM[other], buf.r[other])

    def w_wait(self, en, buf):
        if buf.w is not None and buf.w[0] != en:
            self.wait(en, self.ENG_SEM[buf.w[0]], buf.w[1])
        for ren, rc in buf.r.items():
            if ren != en:
                self.wait(en, self.ENG_SEM[ren], rc)

    def note_r(self, en, buf, cnt):
        buf.r[en] = max(buf.r.get(en, 0), cnt)

    def note_w(self, en, buf, cnt):
        buf.w = (en, cnt)
        buf.r = {}

    def barrier(self, ens=("t", "v", "s"), dma=False):
        snap = {sn: self.cnt[sn] for sn in ("tc", "vc", "sc")}
        for en in ens:
            for sn, vv in snap.items():
                if self.ENG_SEM[en] != sn:
                    self.wait(en, sn, vv)
            if dma:
                self.wait(en, "sd", self.cnt["sd"])
                self.wait(en, "gd", self.cnt["gd"])


def ap3(base_ap, extra_off, chunk_stride, nch, width):
    apl = [list(p) for p in base_ap.ap]
    return AP(tensor=base_ap.tensor, offset=base_ap.offset + extra_off,
              ap=[apl[0], [chunk_stride, nch], [1, width]])


def bc_ap(psum_ap, reps, width):
    apl = [list(p) for p in psum_ap.ap]
    return AP(tensor=psum_ap.tensor, offset=psum_ap.offset,
              ap=[apl[0], [0, reps], [apl[1][0], width]])


def build_nc(ixs, ccols, nconst):
    nc = bass.Bass("TRN2", target_bir_lowering=False)
    x_h = nc.dram_tensor("x", [128, 2 * DC * TH], F32, kind="ExternalInput").ap()
    ffn_h = nc.dram_tensor("ffn_blob", [128, 160 * 128], F32, kind="ExternalInput").ap()
    ip_h = nc.dram_tensor("ip_blob", [128, NG * 48 * 128], F32, kind="ExternalInput").ap()
    fc_h = nc.dram_tensor("fc_blob", [128, 8 * 16 * 128], F32, kind="ExternalInput").ap()
    cp_h = nc.dram_tensor("cp_blob", [128, 16 * 8 * 128], F32, kind="ExternalInput").ap()
    ow_h = nc.dram_tensor("ow_blob", [128, 8 * 16 * 128], F32, kind="ExternalInput").ap()
    const_h = nc.dram_tensor("const_blob", [128, nconst], F32, kind="ExternalInput").ap()
    onec_h = nc.dram_tensor("onec", [128, 1], F32, kind="ExternalInput").ap()
    oner_h = nc.dram_tensor("oner", [1, 128], F32, kind="ExternalInput").ap()
    y_h = nc.dram_tensor("y", [128, DC * 2048], F32, kind="ExternalOutput").ap()

    with ExitStack() as es:
        E = Em(nc, es)
        t, v, s, q, g = nc.tensor, nc.vector, nc.scalar, nc.sync, nc.gpsimd

        def sbl(name, shape, dt):
            return es.enter_context(nc.sbuf_tensor(name, shape, dt, side="left"))

        consts = sbl("consts_sb", [128, nconst], F32)
        onec = sbl("onec_sb", [128, 1], BF16)
        oner = sbl("oner_sb", [1, 128], BF16)
        ffn_sb = sbl("ffn_sb", [128, 160 * 128], BF16)
        wslot = [sbl(f"wslot{i}", [128, 48 * 128], BF16) for i in range(2)]
        mA = sbl("mA", [1, TH], BF16)
        qA = sbl("qA", [1, TH], BF16)
        mB = sbl("mB", [1, TH], BF16)
        qB = sbl("qB", [1, TH], BF16)
        rstd = sbl("rstd", [1, TH], BF16)
        varb = sbl("varb", [1, TH], F32)
        b2h = sbl("b2h", [128, 8], F32)
        lnb_bf = sbl("lnb_bf", [128, 16], BF16)
        fcb = sbl("fcb", [128, 8], F32)
        tsl = sbl("tsl", [128, 2 * 512], F32)
        xb3 = sbl("xb3", [128, 4 * 512], BF16)
        sq3 = sbl("sq3", [128, 2 * 512], BF16)
        mS = xb3[:, 2 * 512:3 * 512]
        rS = xb3[:, 3 * 512:4 * 512]
        outb = sbl("outb", [128, 2 * 512], F32)

        B = {n: Buf(n) for n in
             ("consts onec oner ffn mA qA mB qB rstd varb b2h lnb fcb "
              "mS rS xcur h x2 xn ph gate hn m1 xb sq xc3 Pb2 Hq2 Qb2 "
              "Pb Hq Qb").split()}
        B_wslot = [Buf("ws0"), Buf("ws1")]
        B_tsl = [Buf("tsl0"), Buf("tsl1")]
        B_xb3 = [Buf(f"xb3{i}") for i in range(2)]
        B_sq3 = [Buf(f"sq3{i}") for i in range(2)]
        B_tail = [Buf(f"tail{i}") for i in range(4)]
        B_outb = [Buf("ob0"), Buf("ob1")]
        banks = [Buf(f"bk{i}", psum=True) for i in range(8)]
        pt = [es.enter_context(nc.psum_tensor(f"pb{i}", [128, 512], F32))
              for i in range(8)]

        def ccol(name, j=0, n=1):
            return consts[:, ccols[name] + j:ccols[name] + j + n]

        def fftile(name, k, m):
            i = ixs["ffn"][(name, k, m)]
            return ffn_sb[:, 128 * i:128 * (i + 1)]

        c = E.inc("q", q.dma_start(consts[:, :], const_h))
        E.note_w("q", B["consts"], c)
        c = E.inc("g", g.dma_start(onec[:, :], onec_h))
        E.note_w("g", B["onec"], c)
        c = E.inc("g", g.dma_start(oner[:, :], oner_h))
        E.note_w("g", B["oner"], c)
        c = E.inc("g", g.dma_start(ffn_sb[:, :], ffn_h))
        E.note_w("g", B["ffn"], c)

        E.r_wait("s", B["consts"])
        c = E.inc("s", s.mul(b2h[:, :], ccol("b2", 0, 8), 0.5))
        E.note_w("s", B["b2h"], c)
        c = E.inc("s", s.copy(lnb_bf[:, :], ccol("ln_b", 0, 16)))
        E.note_w("s", B["lnb"], c)

        sched = []
        for half in range(2):
            sched += [("ip", gi, 6144, ip_h[:, gi * 6144:(gi + 1) * 6144])
                      for gi in range(NG)]
            sched += [("fc", c, 4096, fc_h[:, c * 4096:(c + 1) * 4096])
                      for c in range(4)]
            sched += [("cp", c, 4096, cp_h[:, c * 4096:(c + 1) * 4096])
                      for c in range(4)]
            sched += [("ow", c, 4096, ow_h[:, c * 4096:(c + 1) * 4096])
                      for c in range(4)]
        ws = {"issued": 0, "dma": [None] * len(sched)}

        def ws_issue():
            i = ws["issued"]
            if i >= len(sched):
                return
            _, _, ncols, src = sched[i]
            E.w_wait("g", B_wslot[i % 2])
            c2 = E.inc("g", g.dma_start(wslot[i % 2][:, :ncols], src))
            E.note_w("g", B_wslot[i % 2], c2)
            ws["dma"][i] = c2
            ws["issued"] = i + 1

        def ws_tile(ci, which, key):
            j = ixs[which][key]
            return wslot[ci % 2][:, j * 128:(j + 1) * 128]

        def ws_ensure(ci):
            E.wait("t", "gd", ws["dma"][ci])

        def ws_done(ci):
            E.note_r("t", B_wslot[ci % 2], E.cnt["tc"])
            ws_issue()

        ws_issue()
        ws_issue()

        def bcast_to(src_ap, w, bi, B_src, dst_sb, B_dst):
            bk = banks[bi]
            E.w_wait("t", bk)
            E.r_wait("t", B["oner"])
            E.r_wait("t", B_src)
            ii = t.matmul(pt[bi][:, :w], oner[:, :], src_ap, start=True, stop=True)
            cnt = E.inc("t", ii)
            E.note_w("t", bk, cnt)
            E.note_r("t", B_src, cnt)
            E.r_wait("s", bk)
            E.w_wait("s", B_dst)
            c2 = E.inc("s", s.copy(dst_sb[:, 0:w], pt[bi][:, :w]))
            E.note_r("s", bk, c2)
            E.note_w("s", B_dst, c2)

        def rstd_batch(m_sb, q_sb, B_m, B_q):
            E.r_wait("v", B_m)
            E.w_wait("v", B["varb"])
            c2 = E.inc("v", v.tensor_mul(varb[0:1, :], m_sb[0:1, :], m_sb[0:1, :]))
            E.note_r("v", B_m, c2)
            E.note_w("v", B["varb"], c2)
            E.r_wait("v", B_q)
            c2 = E.inc("v", v.tensor_tensor(varb[0:1, :], q_sb[0:1, :],
                                            varb[0:1, :], ALU.subtract))
            E.note_r("v", B_q, c2)
            E.note_w("v", B["varb"], c2)
            E.r_wait("v", B["consts"])
            c2 = E.inc("v", v.tensor_scalar_add(
                varb[0:1, :], varb[0:1, :],
                consts[0:1, ccols["eps"]:ccols["eps"] + 1]))
            E.note_w("v", B["varb"], c2)
            c2 = E.inc("v", v.reciprocal(varb[0:1, :], varb[0:1, :]))
            E.note_w("v", B["varb"], c2)
            E.r_wait("s", B["varb"])
            E.w_wait("s", B["rstd"])
            c2 = E.inc("s", s.activation(rstd[0:1, :], varb[0:1, :], AF.Sqrt))
            E.note_r("s", B["varb"], c2)
            E.note_w("s", B["rstd"], c2)

        store_last = [0]

        for half in range(2):
            with ExitStack() as hs:
                def rsb(name, shape, dt, stack):
                    return stack.enter_context(
                        nc.sbuf_tensor(f"{name}{half}", shape, dt, side="right"))

                xcur = rsb("xcur", [128, DC * TH], F32, hs)
                h = rsb("h", [128, IC * HT], BF16, hs)
                B_xcur, B_h = B["xcur"], B["h"]

                hap = h[:, :]
                XB, SQ = 0, 4096
                Pb = h[:, 8192:9216]
                Hq = h[:, 9216:13312]
                Qb = h[:, 13312:14336]
                TAIL = 14336
                B_xb, B_sq = B["xb"], B["sq"]
                B_Pb, B_Hq, B_Qb = B["Pb"], B["Hq"], B["Qb"]

                def hfl(base, w):
                    return h[:, base:base + w]

                def xcur_v(tcol, w):
                    return ap3(xcur[:, :], tcol, TH, DC, w)

                E.barrier(("t", "v", "s"), dma=True)
                E.wait("q", "tc", E.cnt["tc"])
                E.wait("q", "vc", E.cnt["vc"])
                E.wait("q", "sc", E.cnt["sc"])
                if half == 0:
                    E.w_wait("q", B_xcur)
                    c = E.inc("q", q.dma_start(
                        xcur[:, :], x_h[:, 0:DC * TH]))
                    E.note_w("q", B_xcur, c)

                rot = [0]

                def nb(held=()):
                    while True:
                        rot[0] = (rot[0] + 1) % 8
                        if rot[0] not in held:
                            return rot[0]

                Pb_c, Qb_c = [Pb], [Qb]
                B_Pb_c, B_Qb_c = [B_Pb], [B_Qb]
                HqD = [[Hq, Hq]]
                B_HqD = [[B_Hq, B_Hq]]

                def ffn_tile(xsrc, B_xsrc, dst_add, w, after_u1=None,
                             avoid=()):
                    for m in range(2):
                        bi = nb(held=avoid)
                        bk = banks[bi]
                        E.w_wait("t", bk)
                        for k in range(8):
                            E.r_wait("t", B_xsrc)
                            ii = t.matmul(pt[bi][:, :w], fftile("U1", k, m),
                                          xsrc(k), start=(k == 0), stop=(k == 7))
                        cnt = E.inc("t", ii)
                        E.note_w("t", bk, cnt)
                        E.note_r("t", B_xsrc, cnt)
                        E.r_wait("s", bk)
                        E.w_wait("s", B_Pb_c[0])
                        c2 = E.inc("s", s.copy(Pb_c[0][:, m * 512:m * 512 + w],
                                               pt[bi][:, :w]))
                        E.note_r("s", bk, c2)
                        E.note_w("s", B_Pb_c[0], c2)
                    if after_u1 is not None:
                        after_u1()
                    u2a = nb(held=avoid)
                    u2b_ = nb(held=(u2a,) + tuple(avoid))
                    u2 = (u2a, u2b_)
                    for mu in range(2):
                        E.w_wait("t", banks[u2[mu]])
                    for qq in range(4):
                        Hqb = HqD[0][qq % 2]
                        B_Hqb = B_HqD[0][qq % 2]
                        for mq in range(8):
                            m = qq * 8 + mq
                            bi = nb(held=u2 + tuple(avoid))
                            bk = banks[bi]
                            E.w_wait("t", bk)
                            for k in range(2):
                                E.r_wait("t", B_Pb_c[0])
                                ii = t.matmul(pt[bi][:, :w], fftile("V1", k, m),
                                              Pb_c[0][:, k * 512:k * 512 + w],
                                              start=(k == 0), stop=(k == 1))
                            cnt = E.inc("t", ii)
                            E.note_w("t", bk, cnt)
                            E.note_r("t", B_Pb_c[0], cnt)
                            E.r_wait("s", bk)
                            E.w_wait("s", B_Hqb)
                            E.r_wait("s", B["consts"])
                            c2 = E.inc("s", s.activation(
                                Hqb[:, mq * 512:mq * 512 + w], pt[bi][:, :w],
                                AF.Gelu_apprx_tanh, bias=ccol("b1", m), scale=1.0))
                            E.note_r("s", bk, c2)
                            E.note_w("s", B_Hqb, c2)
                        for mu in range(2):
                            for kq in range(8):
                                k = qq * 8 + kq
                                E.r_wait("t", B_Hqb)
                                ii = t.matmul(pt[u2[mu]][:, :w], fftile("U2", k, mu),
                                              Hqb[:, kq * 512:kq * 512 + w],
                                              start=(qq == 0 and kq == 0),
                                              stop=(qq == 3 and kq == 7))
                            cnt = E.inc("t", ii)
                            E.note_r("t", B_Hqb, cnt)
                            if qq == 3:
                                E.note_w("t", banks[u2[mu]], cnt)
                    for mu in range(2):
                        bk = banks[u2[mu]]
                        E.r_wait("s", bk)
                        E.w_wait("s", B_Qb_c[0])
                        c2 = E.inc("s", s.copy(Qb_c[0][:, mu * 512:mu * 512 + w],
                                               pt[u2[mu]][:, :w]))
                        E.note_r("s", bk, c2)
                        E.note_w("s", B_Qb_c[0], c2)
                    for m in range(8):
                        bi = nb(held=avoid)
                        bk = banks[bi]
                        E.w_wait("t", bk)
                        for k in range(2):
                            E.r_wait("t", B_Qb_c[0])
                            ii = t.matmul(pt[bi][:, :w], fftile("V2", k, m),
                                          Qb_c[0][:, k * 512:k * 512 + w],
                                          start=(k == 0), stop=(k == 1))
                        cnt = E.inc("t", ii)
                        E.note_w("t", bk, cnt)
                        E.note_r("t", B_Qb_c[0], cnt)
                        dst_add(m, bi, bk)

                for si, (tcol, w) in enumerate(TILES):
                    bm = 4 if si % 2 == 0 else 6
                    bq = bm + 1
                    E.w_wait("t", banks[bm])
                    E.w_wait("t", banks[bq])
                    for cch in range(DC):
                        sl = cch % 2
                        xsl = TAIL + sl * 512
                        qsl = TAIL + 1024 + sl * 512
                        E.r_wait("s", B_xcur)
                        E.w_wait("s", B_tail[sl])
                        c2 = E.inc("s", s.copy(
                            hfl(xsl, w), xcur[:, cch * TH + tcol:cch * TH + tcol + w]))
                        E.note_r("s", B_xcur, c2)
                        E.note_w("s", B_tail[sl], c2)
                        E.r_wait("v", B_tail[sl])
                        E.w_wait("v", B_tail[2 + sl])
                        c2 = E.inc("v", v.tensor_mul(hfl(qsl, w), hfl(xsl, w),
                                                     hfl(xsl, w)))
                        E.note_r("v", B_tail[sl], c2)
                        E.note_w("v", B_tail[2 + sl], c2)
                        E.r_wait("t", B_tail[sl])
                        E.r_wait("t", B["onec"])
                        ii = t.matmul(pt[bm][0:1, :w], onec[:, :], hfl(xsl, w),
                                      start=(cch == 0), stop=(cch == DC - 1))
                        cnt = E.inc("t", ii)
                        E.note_r("t", B_tail[sl], cnt)
                        if cch == DC - 1:
                            E.note_w("t", banks[bm], cnt)
                        E.r_wait("t", B_tail[2 + sl])
                        ii = t.matmul(pt[bq][0:1, :w], onec[:, :], hfl(qsl, w),
                                      start=(cch == 0), stop=(cch == DC - 1))
                        cnt = E.inc("t", ii)
                        E.note_r("t", B_tail[2 + sl], cnt)
                        if cch == DC - 1:
                            E.note_w("t", banks[bq], cnt)
                    for bb, dst, bfd in ((bm, mA, B["mA"]), (bq, qA, B["qA"])):
                        E.r_wait("s", banks[bb])
                        E.w_wait("s", bfd)
                        c2 = E.inc("s", s.activation(dst[0:1, tcol:tcol + w],
                                                     pt[bb][0:1, :w], AF.Copy,
                                                     scale=1.0 / D))
                        E.note_r("s", banks[bb], c2)
                        E.note_w("s", bfd, c2)
                rstd_batch(mA, qA, B["mA"], B["qA"])

                E.barrier(("t", "v", "s"))

                for ti, (tcol, w) in enumerate(TILES):
                    bcast_to(mA[0:1, tcol:tcol + w], w, 6, B["mA"], mS, B["mS"])
                    bcast_to(rstd[0:1, tcol:tcol + w], w, 7, B["rstd"], rS, B["rS"])
                    E.r_wait("v", B["mS"])
                    E.r_wait("v", B_xcur)
                    E.w_wait("v", B_xb)
                    c2 = E.inc("v", v.tensor_tensor(ap3(hap, XB, 512, DC, w),
                                                    xcur_v(tcol, w),
                                                    bc_ap(mS[:, 0:w], DC, w),
                                                    ALU.subtract))
                    E.note_r("v", B["mS"], c2)
                    E.note_r("v", B_xcur, c2)
                    E.note_w("v", B_xb, c2)
                    E.r_wait("v", B["rS"])
                    E.w_wait("v", B_sq)
                    c2 = E.inc("v", v.tensor_tensor(ap3(hap, SQ, 512, DC, w),
                                                    ap3(hap, XB, 512, DC, w),
                                                    bc_ap(rS[:, 0:w], DC, w),
                                                    ALU.mult))
                    E.note_r("v", B["rS"], c2)
                    E.note_r("v", B_xb, c2)
                    E.note_w("v", B_sq, c2)

                    def dst_add(m, bi, bk, _tcol=tcol, _w=w):
                        sl = m % 2
                        E.r_wait("s", bk)
                        E.w_wait("s", B_tsl[sl])
                        c3 = E.inc("s", s.mul(tsl[:, sl * 512:sl * 512 + _w],
                                              pt[bi][:, :_w], 0.5))
                        E.note_r("s", bk, c3)
                        E.note_w("s", B_tsl[sl], c3)
                        E.r_wait("v", B_tsl[sl])
                        E.r_wait("v", B["b2h"])
                        dstc = xcur[:, m * TH + _tcol:m * TH + _tcol + _w]
                        c3 = E.inc("v", v.scalar_tensor_tensor(
                            dstc, tsl[:, sl * 512:sl * 512 + _w],
                            b2h[:, m:m + 1], dstc, ALU.add, ALU.add))
                        E.note_r("v", B_tsl[sl], c3)
                        E.note_r("v", B["b2h"], c3)
                        E.note_w("v", B_xcur, c3)

                    ffn_tile(lambda k: hfl(SQ + k * 512, w), B_sq, dst_add, w)

                    E.r_wait("s", B_xcur)
                    E.w_wait("s", B_xb)
                    c2 = E.inc("s", s.copy(ap3(hap, XB, 512, DC, w),
                                           xcur_v(tcol, w)))
                    E.note_r("s", B_xcur, c2)
                    E.note_w("s", B_xb, c2)
                    E.r_wait("v", B_xb)
                    E.w_wait("v", B_sq)
                    c2 = E.inc("v", v.tensor_mul(ap3(hap, SQ, 512, DC, w),
                                                 ap3(hap, XB, 512, DC, w),
                                                 ap3(hap, XB, 512, DC, w)))
                    E.note_r("v", B_xb, c2)
                    E.note_w("v", B_sq, c2)
                    E.r_wait("t", B_xb)
                    E.r_wait("t", B_sq)
                    E.w_wait("t", banks[4])
                    E.r_wait("t", B["onec"])
                    for k in range(DC):
                        ii = t.matmul(pt[4][0:1, :w], onec[:, :], hfl(XB + k * 512, w),
                                      start=(k == 0), stop=(k == DC - 1))
                    cnt = E.inc("t", ii)
                    E.note_w("t", banks[4], cnt)
                    E.note_r("t", B_xb, cnt)
                    E.w_wait("t", banks[5])
                    for k in range(DC):
                        ii = t.matmul(pt[5][0:1, :w], onec[:, :], hfl(SQ + k * 512, w),
                                      start=(k == 0), stop=(k == DC - 1))
                    cnt = E.inc("t", ii)
                    E.note_w("t", banks[5], cnt)
                    E.note_r("t", B_sq, cnt)
                    for bb, dst, bfd in ((4, mB, B["mB"]), (5, qB, B["qB"])):
                        E.r_wait("s", banks[bb])
                        E.w_wait("s", bfd)
                        c2 = E.inc("s", s.activation(dst[0:1, tcol:tcol + w],
                                                     pt[bb][0:1, :w], AF.Copy,
                                                     scale=1.0 / D))
                        E.note_r("s", banks[bb], c2)
                        E.note_w("s", bfd, c2)
                rstd_batch(mB, qB, B["mB"], B["qB"])

                with ExitStack() as p1s:
                    xn = rsb("xn", [128, DC * TH], BF16, p1s)
                    ph = rsb("ph", [128, 2 * (3 + TH)], BF16, p1s)
                    gate = rsb("gate", [128, 2 * 512], BF16, p1s)
                    B_xn, B_ph, B_gate = B["xn"], B["ph"], B["gate"]
                    PHW = 3 + TH

                    for (tcol, w) in TILES:
                        bcast_to(mB[0:1, tcol:tcol + w], w, 6, B["mB"], mS, B["mS"])
                        bcast_to(rstd[0:1, tcol:tcol + w], w, 7, B["rstd"], rS, B["rS"])
                        E.r_wait("v", B["mS"])
                        E.r_wait("v", B_xcur)
                        E.w_wait("v", B_xb)
                        c2 = E.inc("v", v.tensor_tensor(
                            ap3(hap, XB, 512, DC, w), xcur_v(tcol, w),
                            bc_ap(mS[:, 0:w], DC, w), ALU.subtract))
                        E.note_r("v", B["mS"], c2)
                        E.note_r("v", B_xcur, c2)
                        E.note_w("v", B_xb, c2)
                        E.r_wait("v", B["rS"])
                        E.w_wait("v", B_xn)
                        c2 = E.inc("v", v.tensor_tensor(
                            ap3(xn[:, :], tcol, TH, DC, w),
                            ap3(hap, XB, 512, DC, w),
                            bc_ap(rS[:, 0:w], DC, w), ALU.mult))
                        E.note_r("v", B["rS"], c2)
                        E.note_r("v", B_xb, c2)
                        E.note_w("v", B_xn, c2)

                    E.barrier(("v",))

                    HELD = (2, 3, 4, 5)
                    for pr in HELD:
                        E.w_wait("t", banks[pr])
                    pending_stats = []

                    def flush_stats():
                        for fn in pending_stats:
                            fn()
                        pending_stats.clear()

                    for gi in range(NG):
                        ci = half * 20 + gi
                        ws_ensure(ci)
                        for tix, (tcol, w) in enumerate(TILES):
                            flush_stats()
                            for ml in range(2):
                                bi = nb(held=HELD)
                                bk = banks[bi]
                                E.w_wait("t", bk)
                                for k in range(8):
                                    E.r_wait("t", B_xn)
                                    ii = t.matmul(
                                        pt[bi][:, :w],
                                        ws_tile(ci, "ip", ("Wh", k, ml)),
                                        xn[:, k * TH + tcol:k * TH + tcol + w],
                                        start=(k == 0), stop=(k == 7))
                                cnt = E.inc("t", ii)
                                E.note_w("t", bk, cnt)
                                E.note_r("t", B_xn, cnt)
                                E.r_wait("s", bk)
                                E.w_wait("s", B_ph)
                                c2 = E.inc("s", s.copy(
                                    ph[:, ml * PHW + 3 + tcol:
                                       ml * PHW + 3 + tcol + w], pt[bi][:, :w]))
                                E.note_r("s", bk, c2)
                                E.note_w("s", B_ph, c2)
                            if tcol == 0:
                                E.r_wait("v", B_ph)
                                E.r_wait("v", B["consts"])
                                mv = ap3(ph[:, :], 32, PHW, 2, 3)
                                c2 = E.inc("v", v.tensor_scalar_mul(
                                    mv, mv, ccol("mask", half, 1)))
                                E.note_r("v", B["consts"], c2)
                                E.note_w("v", B_ph, c2)
                                continue
                            for ml in range(2):
                                bi = nb(held=HELD)
                                bk = banks[bi]
                                E.w_wait("t", bk)
                                for k in range(8):
                                    ii = t.matmul(
                                        pt[bi][:, :w],
                                        ws_tile(ci, "ip", ("Wg", k, ml)),
                                        xn[:, k * TH + tcol:k * TH + tcol + w],
                                        start=(k == 0), stop=(k == 7))
                                cnt = E.inc("t", ii)
                                E.note_w("t", bk, cnt)
                                E.note_r("t", B_xn, cnt)
                                E.r_wait("s", bk)
                                E.w_wait("s", B_gate)
                                c2 = E.inc("s", s.activation(
                                    gate[:, ml * 512:ml * 512 + w],
                                    pt[bi][:, :w], AF.Silu))
                                E.note_r("s", bk, c2)
                                E.note_w("s", B_gate, c2)
                            for mc in range(2):
                                bi = nb(held=HELD)
                                bk = banks[bi]
                                E.w_wait("t", bk)
                                first = True
                                for tap in range(KTAP):
                                    for kc in range(2):
                                        E.r_wait("t", B_ph)
                                        ii = t.matmul(
                                            pt[bi][:, :w],
                                            ws_tile(ci, "ip", ("cv", tap, kc, mc)),
                                            ph[:, kc * PHW + tcol + tap:
                                               kc * PHW + tcol + tap + w],
                                            start=first,
                                            stop=(tap == 3 and kc == 1))
                                        first = False
                                cnt = E.inc("t", ii)
                                E.note_w("t", bk, cnt)
                                E.note_r("t", B_ph, cnt)
                                cch = 2 * gi + mc
                                base = cch * HT + (tcol - HALO)
                                E.r_wait("v", bk)
                                E.r_wait("v", B_gate)
                                E.r_wait("v", B["consts"])
                                E.w_wait("v", B_h)
                                c2 = E.inc("v", v.scalar_tensor_tensor(
                                    h[:, base:base + w], pt[bi][:, :w],
                                    ccol("conv_b", cch, 1),
                                    gate[:, mc * 512:mc * 512 + w],
                                    ALU.add, ALU.mult))
                                E.note_r("v", bk, c2)
                                E.note_r("v", B_gate, c2)
                                E.note_w("v", B_h, c2)
                                si2 = tix - 1
                                bm, bq = (2, 3) if si2 == 0 else (4, 5)
                                gsl = gate[:, mc * 512:mc * 512 + w]
                                E.r_wait("v", B_h)
                                E.w_wait("v", B_gate)
                                c2 = E.inc("v", v.tensor_mul(
                                    gsl, h[:, base:base + w],
                                    h[:, base:base + w]))
                                E.note_r("v", B_h, c2)
                                E.note_w("v", B_gate, c2)

                                def _stats(_bm=bm, _bq=bq, _base=base, _w=w,
                                           _gsl=gsl, _gi=gi, _mc=mc):
                                    E.r_wait("t", B_h)
                                    E.r_wait("t", B["onec"])
                                    ii2 = t.matmul(pt[_bm][0:1, :_w], onec[:, :],
                                                   h[:, _base:_base + _w],
                                                   start=(_gi == 0 and _mc == 0),
                                                   stop=(_gi == NG - 1 and _mc == 1))
                                    cn = E.inc("t", ii2)
                                    E.note_r("t", B_h, cn)
                                    if _gi == NG - 1 and _mc == 1:
                                        E.note_w("t", banks[_bm], cn)
                                    E.r_wait("t", B_gate)
                                    ii2 = t.matmul(pt[_bq][0:1, :_w], onec[:, :],
                                                   _gsl,
                                                   start=(_gi == 0 and _mc == 0),
                                                   stop=(_gi == NG - 1 and _mc == 1))
                                    cn = E.inc("t", ii2)
                                    E.note_r("t", B_gate, cn)
                                    if _gi == NG - 1 and _mc == 1:
                                        E.note_w("t", banks[_bq], cn)
                                pending_stats.append(_stats)
                        ws_done(ci)
                    flush_stats()
                    for si2, (tcol, w) in enumerate(OWN_TILES):
                        bm, bq = (2, 3) if si2 == 0 else (4, 5)
                        for bb, dst, bfd in ((bm, mA, B["mA"]), (bq, qA, B["qA"])):
                            E.r_wait("s", banks[bb])
                            E.w_wait("s", bfd)
                            c2 = E.inc("s", s.activation(
                                dst[0:1, tcol:tcol + w], pt[bb][0:1, :w],
                                AF.Copy, scale=1.0 / I2))
                            E.note_r("s", banks[bb], c2)
                            E.note_w("s", bfd, c2)
                rstd_batch(mA, qA, B["mA"], B["qA"])

                with ExitStack() as p2s:
                    hn = rsb("hn", [128, IC * HT], BF16, p2s)
                    m1 = rsb("m1", [128, DC * HT], BF16, p2s)
                    B_hn, B_m1 = B["hn"], B["m1"]

                    def h_v(t0o, w):
                        return ap3(h[:, :], t0o, HT, IC, w)

                    def hn_v(t0o, w):
                        return ap3(hn[:, :], t0o, HT, IC, w)

                    for (tcol, w) in OWN_TILES:
                        t0o = tcol - HALO
                        bcast_to(mA[0:1, tcol:tcol + w], w, 6, B["mA"], mS, B["mS"])
                        bcast_to(rstd[0:1, tcol:tcol + w], w, 7, B["rstd"], rS, B["rS"])
                        E.r_wait("v", B["mS"])
                        E.r_wait("v", B_h)
                        E.w_wait("v", B_hn)
                        c2 = E.inc("v", v.tensor_tensor(
                            hn_v(t0o, w), h_v(t0o, w),
                            bc_ap(mS[:, 0:w], IC, w), ALU.subtract))
                        E.note_r("v", B["mS"], c2)
                        E.note_r("v", B_h, c2)
                        E.note_w("v", B_hn, c2)
                        E.r_wait("v", B["rS"])
                        E.r_wait("v", B["consts"])
                        for cch in range(IC):
                            base = cch * HT + t0o
                            c2 = E.inc("v", v.scalar_tensor_tensor(
                                hn[:, base:base + w], hn[:, base:base + w],
                                ccol("ln_g", cch, 1), rS[:, 0:w],
                                ALU.mult, ALU.mult))
                        E.note_r("v", B["rS"], c2)
                        E.note_w("v", B_hn, c2)

                    for m in range(8):
                        ci = half * 20 + 8 + m // 2
                        if m % 2 == 0:
                            ws_ensure(ci)
                        if half == 0:
                            bi = nb()
                            bk = banks[bi]
                            E.w_wait("t", bk)
                            E.r_wait("t", B["lnb"])
                            for k in range(16):
                                ii = t.matmul(pt[bi][:, 0:1],
                                              ws_tile(ci, "fc", ("fc", m % 2, k)),
                                              lnb_bf[:, k:k + 1],
                                              start=(k == 0), stop=(k == 15))
                            cnt = E.inc("t", ii)
                            E.note_w("t", bk, cnt)
                            E.r_wait("v", bk)
                            E.w_wait("v", B["fcb"])
                            E.r_wait("v", B["consts"])
                            c2 = E.inc("v", v.tensor_scalar_add(
                                fcb[:, m:m + 1], pt[bi][:, 0:1],
                                ccol("fc_b", m, 1)))
                            E.note_r("v", bk, c2)
                            E.note_w("v", B["fcb"], c2)
                        for (tcol, w) in OWN_TILES:
                            t0o = tcol - HALO
                            bi = nb()
                            bk = banks[bi]
                            E.w_wait("t", bk)
                            for k in range(16):
                                E.r_wait("t", B_hn)
                                ii = t.matmul(pt[bi][:, :w],
                                              ws_tile(ci, "fc", ("fc", m % 2, k)),
                                              hn[:, k * HT + t0o:k * HT + t0o + w],
                                              start=(k == 0), stop=(k == 15))
                            cnt = E.inc("t", ii)
                            E.note_w("t", bk, cnt)
                            E.note_r("t", B_hn, cnt)
                            E.r_wait("s", bk)
                            E.w_wait("s", B_m1)
                            E.r_wait("s", B["fcb"])
                            c2 = E.inc("s", s.activation(
                                m1[:, m * HT + t0o:m * HT + t0o + w],
                                pt[bi][:, :w], AF.Silu,
                                bias=fcb[:, m:m + 1], scale=1.0))
                            E.note_r("s", bk, c2)
                            E.note_w("s", B_m1, c2)
                        if m % 2 == 1:
                            ws_done(ci)

                    for m in range(16):
                        ci = half * 20 + 12 + m // 4
                        if m % 4 == 0:
                            ws_ensure(ci)
                        for (tcol, w) in OWN_TILES:
                            t0o = tcol - HALO
                            bi = nb()
                            bk = banks[bi]
                            E.w_wait("t", bk)
                            for k in range(8):
                                E.r_wait("t", B_m1)
                                ii = t.matmul(pt[bi][:, :w],
                                              ws_tile(ci, "cp", ("cp", m % 4, k)),
                                              m1[:, k * HT + t0o:k * HT + t0o + w],
                                              start=(k == 0), stop=(k == 7))
                            cnt = E.inc("t", ii)
                            E.note_w("t", bk, cnt)
                            E.note_r("t", B_m1, cnt)
                            base = m * HT + t0o
                            E.r_wait("v", bk)
                            E.r_wait("v", B["consts"])
                            E.w_wait("v", B_h)
                            c2 = E.inc("v", v.scalar_tensor_tensor(
                                h[:, base:base + w], pt[bi][:, :w],
                                ccol("cproj_b", m, 1), h[:, base:base + w],
                                ALU.add, ALU.add))
                            E.note_r("v", bk, c2)
                            E.note_w("v", B_h, c2)
                        if m % 4 == 3:
                            ws_done(ci)

                x2 = hs.enter_context(
                    nc.sbuf_tensor(f"x2_{half}", [128, DC * HT], F32, side="left"))
                B_x2 = B["x2"]
                stat_banks = {0: (4, 5), 1: (2, 3)}
                mm_banks = {0: (0, 6), 1: (1, 7)}
                for pr in stat_banks.values():
                    for bi in pr:
                        E.w_wait("t", banks[bi])
                pending3 = []

                def flush3():
                    for fn in pending3:
                        fn()
                    pending3.clear()

                for m in range(8):
                    ci = half * 20 + 16 + m // 2
                    if m % 2 == 0:
                        ws_ensure(ci)
                    for si, (tcol, w) in enumerate(OWN_TILES):
                        flush3()
                        t0o = tcol - HALO
                        bi = mm_banks[si][m % 2]
                        bk = banks[bi]
                        E.w_wait("t", bk)
                        for k in range(16):
                            E.r_wait("t", B_h)
                            ii = t.matmul(pt[bi][:, :w],
                                          ws_tile(ci, "ow", ("ow", m % 2, k)),
                                          h[:, k * HT + t0o:k * HT + t0o + w],
                                          start=(k == 0), stop=(k == 15))
                        cnt = E.inc("t", ii)
                        E.note_w("t", bk, cnt)
                        E.note_r("t", B_h, cnt)
                        base = m * HT + t0o
                        E.r_wait("v", bk)
                        E.r_wait("v", B_xcur)
                        E.w_wait("v", B_x2)
                        c2 = E.inc("v", v.tensor_add(
                            x2[:, base:base + w], pt[bi][:, :w],
                            xcur[:, m * TH + tcol:m * TH + tcol + w]))
                        E.note_r("v", bk, c2)
                        E.note_r("v", B_xcur, c2)
                        E.note_w("v", B_x2, c2)
                        sl = si
                        E.r_wait("s", B_x2)
                        E.w_wait("s", B_xb3[sl])
                        c2 = E.inc("s", s.copy(xb3[:, sl * 512:sl * 512 + w],
                                               x2[:, base:base + w]))
                        E.note_r("s", B_x2, c2)
                        E.note_w("s", B_xb3[sl], c2)
                        E.r_wait("v", B_xb3[sl])
                        E.w_wait("v", B_sq3[sl])
                        c2 = E.inc("v", v.tensor_mul(sq3[:, sl * 512:sl * 512 + w],
                                                     xb3[:, sl * 512:sl * 512 + w],
                                                     xb3[:, sl * 512:sl * 512 + w]))
                        E.note_r("v", B_xb3[sl], c2)
                        E.note_w("v", B_sq3[sl], c2)
                        bmi, bqi = stat_banks[si]

                        def _st3(_bmi=bmi, _bqi=bqi, _sl=sl, _m=m, _w=w):
                            E.r_wait("t", B_xb3[_sl])
                            ii2 = t.matmul(pt[_bmi][0:1, :_w], onec[:, :],
                                           xb3[:, _sl * 512:_sl * 512 + _w],
                                           start=(_m == 0), stop=(_m == 7))
                            cn = E.inc("t", ii2)
                            E.note_r("t", B_xb3[_sl], cn)
                            if _m == 7:
                                E.note_w("t", banks[_bmi], cn)
                            E.r_wait("t", B_sq3[_sl])
                            ii2 = t.matmul(pt[_bqi][0:1, :_w], onec[:, :],
                                           sq3[:, _sl * 512:_sl * 512 + _w],
                                           start=(_m == 0), stop=(_m == 7))
                            cn = E.inc("t", ii2)
                            E.note_r("t", B_sq3[_sl], cn)
                            if _m == 7:
                                E.note_w("t", banks[_bqi], cn)
                        pending3.append(_st3)
                    if m % 2 == 1:
                        ws_done(ci)
                flush3()
                if half == 0:
                    E.w_wait("q", B_xcur)
                    c = E.inc("q", q.dma_start(
                        xcur[:, :], x_h[:, DC * TH:2 * DC * TH]))
                    E.note_w("q", B_xcur, c)
                for si, (tcol, w) in enumerate(OWN_TILES):
                    bmi, bqi = stat_banks[si]
                    for bb, dst, bfd in ((bmi, mA, B["mA"]), (bqi, qA, B["qA"])):
                        E.r_wait("s", banks[bb])
                        E.w_wait("s", bfd)
                        c2 = E.inc("s", s.activation(dst[0:1, tcol:tcol + w],
                                                     pt[bb][0:1, :w], AF.Copy,
                                                     scale=1.0 / D))
                        E.note_r("s", banks[bb], c2)
                        E.note_w("s", bfd, c2)
                rstd_batch(mA, qA, B["mA"], B["qA"])

                xc3 = h[:, 4096:8192]
                HqB2 = h[:, 0:4096]
                HqD[0] = [Hq, HqB2]
                B_HqD[0] = [B_Hq, B["Hq2"]]
                Pb_c[0], Qb_c[0] = Pb, Qb
                B_xc3 = B["xc3"]
                E.barrier(("v", "s"))

                for si, (tcol, w) in enumerate(OWN_TILES):
                    t0o = tcol - HALO
                    bcast_to(mA[0:1, tcol:tcol + w], w, 6, B["mA"], mS, B["mS"])
                    bcast_to(rstd[0:1, tcol:tcol + w], w, 7, B["rstd"], rS, B["rS"])
                    E.r_wait("v", B["mS"])
                    E.r_wait("v", B_x2)
                    E.w_wait("v", B_xc3)
                    xc3d = ap3(xc3, 0, 512, DC, w)
                    c2 = E.inc("v", v.tensor_tensor(
                        xc3d, ap3(x2[:, :], t0o, HT, DC, w),
                        bc_ap(mS[:, 0:w], DC, w), ALU.subtract))
                    E.note_r("v", B["mS"], c2)
                    E.note_r("v", B_x2, c2)
                    E.note_w("v", B_xc3, c2)
                    E.r_wait("v", B["rS"])
                    c2 = E.inc("v", v.tensor_tensor(
                        xc3d, xc3d, bc_ap(rS[:, 0:w], DC, w), ALU.mult))
                    E.note_r("v", B["rS"], c2)
                    E.note_w("v", B_xc3, c2)

                    def dst_add2(m, bi, bk, _si=si, _t0o=t0o, _tcol=tcol, _w=w):
                        sl = m % 2
                        E.r_wait("s", bk)
                        E.w_wait("s", B_tsl[sl])
                        c3 = E.inc("s", s.mul(tsl[:, sl * 512:sl * 512 + _w],
                                              pt[bi][:, :_w], 0.5))
                        E.note_r("s", bk, c3)
                        E.note_w("s", B_tsl[sl], c3)
                        ob = B_outb[sl]
                        E.r_wait("v", B_tsl[sl])
                        E.r_wait("v", B_x2)
                        E.r_wait("v", B["b2h"])
                        E.w_wait("v", ob)
                        c3 = E.inc("v", v.scalar_tensor_tensor(
                            outb[:, sl * 512:sl * 512 + _w],
                            tsl[:, sl * 512:sl * 512 + _w],
                            b2h[:, m:m + 1],
                            x2[:, m * HT + _t0o:m * HT + _t0o + _w],
                            ALU.add, ALU.add))
                        E.note_r("v", B_tsl[sl], c3)
                        E.note_r("v", B_x2, c3)
                        E.note_w("v", ob, c3)
                        E.r_wait("q", ob)
                        c3 = E.inc("q", q.dma_start(
                            y_h[:, m * 2048 + half * HT + _t0o:
                                m * 2048 + half * HT + _t0o + _w],
                            outb[:, sl * 512:sl * 512 + _w]))
                        E.note_r("q", ob, c3)
                        store_last[0] = c3

                    ffn_tile(lambda k: xc3[:, k * 512:k * 512 + w],
                             B_xc3, dst_add2, w, avoid=(6, 7))

        E.wait("q", "sd", store_last[0])
    return nc


def kernel(**inputs):
    blobs, ixs, const_blob, ccols = build_host(inputs)
    x_in = np.asarray(inputs["L0_attn_hidden_states"], np.float32)
    xs = shard_x(x_in)
    nc = build_nc(ixs, ccols, const_blob.shape[1])
    onec = np.ones((128, 1), np.float32)
    oner = np.ones((1, 128), np.float32)
    in_maps = []
    for core in range(NCORES):
        chunk = core % 2
        cb = const_blob.copy()
        cb[:, ccols["mask"] + 0] = 1.0 if chunk == 1 else 0.0
        cb[:, ccols["mask"] + 1] = 1.0
        in_maps.append(dict(x=xs[core], const_blob=cb, onec=onec, oner=oner,
                            **blobs))
    res = run_bass_kernel_spmd(nc, in_maps, core_ids=list(range(NCORES)))
    return gather_y(res.results)


# version 11
# speedup vs baseline: 1.0620x; 1.0620x over previous
import numpy as np
import ml_dtypes
from contextlib import ExitStack

import concourse.bass as bass
import concourse.mybir as mybir
from concourse.ap import AP
from concourse.bass_utils import run_bass_kernel_spmd

BF16 = mybir.dt.bfloat16
F32 = mybir.dt.float32
AF = mybir.ActivationFunctionType
ALU = mybir.AluOpType

D = 1024
I2 = 2048
DFF = 4096
NG = 8
KTAP = 4
HALO = 32
HT = 1024
TH = HALO + HT
NCORES = 8
EPS = 1e-5
TILES = [(0, HALO), (HALO, 512), (HALO + 512, 512)]
OWN_TILES = TILES[1:]
DC, IC = 8, 16


def _tiles(WT):
    return {(k, m): WT[128 * k:128 * (k + 1), 128 * m:128 * (m + 1)]
            for k in range(WT.shape[0] // 128) for m in range(WT.shape[1] // 128)}


def _blob(entries):
    ix, arrs = {}, []
    for i, (key, a) in enumerate(entries):
        ix[key] = i
        arrs.append(np.asarray(a, np.float32))
    return np.ascontiguousarray(np.concatenate(arrs, axis=1)), ix


def _col_pack(vec, nch):
    return np.ascontiguousarray(np.asarray(vec, np.float32).reshape(nch, 128).T)


def build_host(inputs):
    f32 = lambda k: np.ascontiguousarray(np.asarray(inputs[k], np.float32))
    ipw = f32("L0_attn_in_proj_w")
    WhT = np.ascontiguousarray(ipw[:I2, :].T)
    WgT = np.ascontiguousarray(ipw[I2:, :].T)
    cw = f32("L0_attn_conv_w")
    fcT = np.ascontiguousarray(f32("L0_attn_fc_w").T)
    cpT = np.ascontiguousarray(f32("L0_attn_cproj_w").T)
    owT = np.ascontiguousarray(f32("L0_attn_out_w").T)

    tU1, tV1 = _tiles(f32("L0_ffn_U1")), _tiles(f32("L0_ffn_V1"))
    tU2, tV2 = _tiles(f32("L0_ffn_U2")), _tiles(f32("L0_ffn_V2"))
    ents = ([(("U1", k, m), tU1[(k, m)]) for m in range(2) for k in range(8)]
            + [(("V1", k, m), tV1[(k, m)]) for m in range(32) for k in range(2)]
            + [(("U2", k, m), tU2[(k, m)]) for m in range(2) for k in range(32)]
            + [(("V2", k, m), tV2[(k, m)]) for m in range(8) for k in range(2)])
    ffn_blob, ffn_ix = _blob(ents)

    tWh, tWg = _tiles(WhT), _tiles(WgT)
    ip_chunks, ip_ix = [], None
    for g in range(NG):
        ents = ([(("Wh", k, ml), tWh[(k, 2 * g + ml)])
                 for ml in range(2) for k in range(8)]
                + [(("Wg", k, ml), tWg[(k, 2 * g + ml)])
                   for ml in range(2) for k in range(8)])
        for mc in range(2):
            for tap in range(KTAP):
                CT = np.ascontiguousarray(cw[256 * g:256 * (g + 1), :, tap].T)
                for kc in range(2):
                    ents.append((("cv", tap, kc, mc),
                                 CT[128 * kc:128 * (kc + 1),
                                    128 * mc:128 * (mc + 1)]))
        blob, ip_ix = _blob(ents)
        ip_chunks.append(blob)
    ip_blob = np.ascontiguousarray(np.concatenate(ip_chunks, axis=1))

    tfc, tcp, tow = _tiles(fcT), _tiles(cpT), _tiles(owT)
    fc_chunks, cp_chunks, ow_chunks = [], [], []
    for c in range(4):
        b, fc_ix = _blob([(("fc", ml, k), tfc[(k, 2 * c + ml)])
                          for ml in range(2) for k in range(16)])
        fc_chunks.append(b)
    for c in range(4):
        b, cp_ix = _blob([(("cp", ml, k), tcp[(k, 4 * c + ml)])
                          for ml in range(4) for k in range(8)])
        cp_chunks.append(b)
    for c in range(4):
        b, ow_ix = _blob([(("ow", ml, k), tow[(k, 2 * c + ml)])
                          for ml in range(2) for k in range(16)])
        ow_chunks.append(b)
    fc_blob = np.ascontiguousarray(np.concatenate(fc_chunks, axis=1))
    cp_blob = np.ascontiguousarray(np.concatenate(cp_chunks, axis=1))
    ow_blob = np.ascontiguousarray(np.concatenate(ow_chunks, axis=1))

    cols, parts = {}, []

    def add(name, arr):
        cols[name] = sum(p.shape[1] for p in parts)
        parts.append(arr)

    add("b1", _col_pack(inputs["L0_ffn_b1"], 32))
    add("b2", _col_pack(inputs["L0_ffn_b2"], 8))
    add("fc_b", _col_pack(inputs["L0_attn_fc_b"], 8))
    add("cproj_b", _col_pack(inputs["L0_attn_cproj_b"], 16))
    add("conv_b", _col_pack(inputs["L0_attn_conv_b"], 16))
    add("ln_g", _col_pack(inputs["L0_attn_ln_g"], 16))
    add("ln_b", _col_pack(inputs["L0_attn_ln_b"], 16))
    add("eps", np.full((128, 1), EPS, np.float32))
    add("mask", np.zeros((128, 2), np.float32))
    const_blob = np.ascontiguousarray(np.concatenate(parts, axis=1))

    blobs = dict(ffn_blob=ffn_blob, ip_blob=ip_blob, fc_blob=fc_blob,
                 cp_blob=cp_blob, ow_blob=ow_blob)
    ixs = dict(ffn=ffn_ix, ip=ip_ix, fc=fc_ix, cp=cp_ix, ow=ow_ix)
    return blobs, ixs, const_blob, cols


def shard_x(x_in):
    outs = []
    for core in range(NCORES):
        b, chunk = core // 2, core % 2
        t0 = chunk * 2048
        arr = np.zeros((128, 2 * DC * TH), np.float32)
        for half in range(2):
            base = t0 + half * HT
            xs = np.zeros((TH, D), np.float32)
            lo = base - HALO
            if lo >= 0:
                xs[:HALO] = x_in[b, lo:base]
            xs[HALO:] = x_in[b, base:base + HT]
            fm = xs.T.reshape(DC, 128, TH).transpose(1, 0, 2)
            arr[:, half * DC * TH:(half + 1) * DC * TH] = fm.reshape(128, DC * TH)
        outs.append(np.ascontiguousarray(arr))
    return outs


def gather_y(results):
    out = np.zeros((4, 4096, D), np.float32)
    for core in range(NCORES):
        b, chunk = core // 2, core % 2
        a = np.asarray(results[core]["y"], np.float32).reshape(128, DC, 2048)
        out[b, chunk * 2048:(chunk + 1) * 2048] = a.transpose(2, 1, 0).reshape(2048, D)
    return out


class Buf:
    __slots__ = ("name", "w", "r", "psum")

    def __init__(self, name, psum=False):
        self.name, self.w, self.r, self.psum = name, None, {}, psum


class Em:
    ENG_SEM = {"t": "tc", "v": "vc", "s": "sc", "q": "sd", "g": "gd"}

    def __init__(self, nc, es):
        self.nc = nc
        self.eng = {"t": nc.tensor, "v": nc.vector, "s": nc.scalar,
                    "q": nc.sync, "g": nc.gpsimd}
        self.sems, self.cnt, self.wm = {}, {}, {}
        for sn in ("tc", "vc", "sc", "sd", "gd"):
            self.sems[sn] = es.enter_context(nc.semaphore(sn))
            self.cnt[sn] = 0

    def wait(self, en, sn, val):
        if not val or val <= 0:
            return
        assert val <= self.cnt[sn], (en, sn, val, self.cnt[sn])
        if self.wm.get((en, sn), 0) >= val:
            return
        self.eng[en].wait_ge(self.sems[sn], val)
        self.wm[(en, sn)] = val

    def inc(self, en, instr, n=1):
        sn = self.ENG_SEM[en]
        if en in ("q", "g"):
            n = 16
        instr.then_inc(self.sems[sn], n)
        self.cnt[sn] += n
        return self.cnt[sn]

    def r_wait(self, en, buf):
        if buf.w is not None and buf.w[0] != en:
            self.wait(en, self.ENG_SEM[buf.w[0]], buf.w[1])
        if buf.psum and en in ("v", "s"):
            other = "s" if en == "v" else "v"
            if other in buf.r:
                self.wait(en, self.ENG_SEM[other], buf.r[other])

    def w_wait(self, en, buf):
        if buf.w is not None and buf.w[0] != en:
            self.wait(en, self.ENG_SEM[buf.w[0]], buf.w[1])
        for ren, rc in buf.r.items():
            if ren != en:
                self.wait(en, self.ENG_SEM[ren], rc)

    def note_r(self, en, buf, cnt):
        buf.r[en] = max(buf.r.get(en, 0), cnt)

    def note_w(self, en, buf, cnt):
        buf.w = (en, cnt)
        buf.r = {}

    def barrier(self, ens=("t", "v", "s"), dma=False):
        snap = {sn: self.cnt[sn] for sn in ("tc", "vc", "sc")}
        for en in ens:
            for sn, vv in snap.items():
                if self.ENG_SEM[en] != sn:
                    self.wait(en, sn, vv)
            if dma:
                self.wait(en, "sd", self.cnt["sd"])
                self.wait(en, "gd", self.cnt["gd"])


def ap3(base_ap, extra_off, chunk_stride, nch, width):
    apl = [list(p) for p in base_ap.ap]
    return AP(tensor=base_ap.tensor, offset=base_ap.offset + extra_off,
              ap=[apl[0], [chunk_stride, nch], [1, width]])


def bc_ap(psum_ap, reps, width):
    apl = [list(p) for p in psum_ap.ap]
    return AP(tensor=psum_ap.tensor, offset=psum_ap.offset,
              ap=[apl[0], [0, reps], [apl[1][0], width]])


def build_nc(ixs, ccols, nconst):
    nc = bass.Bass("TRN2", target_bir_lowering=False)
    x_h = nc.dram_tensor("x", [128, 2 * DC * TH], F32, kind="ExternalInput").ap()
    ffn_h = nc.dram_tensor("ffn_blob", [128, 160 * 128], F32, kind="ExternalInput").ap()
    ip_h = nc.dram_tensor("ip_blob", [128, NG * 48 * 128], F32, kind="ExternalInput").ap()
    fc_h = nc.dram_tensor("fc_blob", [128, 8 * 16 * 128], F32, kind="ExternalInput").ap()
    cp_h = nc.dram_tensor("cp_blob", [128, 16 * 8 * 128], F32, kind="ExternalInput").ap()
    ow_h = nc.dram_tensor("ow_blob", [128, 8 * 16 * 128], F32, kind="ExternalInput").ap()
    const_h = nc.dram_tensor("const_blob", [128, nconst], F32, kind="ExternalInput").ap()
    onec_h = nc.dram_tensor("onec", [128, 1], F32, kind="ExternalInput").ap()
    oner_h = nc.dram_tensor("oner", [1, 128], F32, kind="ExternalInput").ap()
    y_h = nc.dram_tensor("y", [128, DC * 2048], F32, kind="ExternalOutput").ap()

    with ExitStack() as es:
        E = Em(nc, es)
        t, v, s, q, g = nc.tensor, nc.vector, nc.scalar, nc.sync, nc.gpsimd

        def sbl(name, shape, dt):
            return es.enter_context(nc.sbuf_tensor(name, shape, dt, side="left"))

        consts = sbl("consts_sb", [128, nconst], F32)
        onec = sbl("onec_sb", [128, 1], BF16)
        oner = sbl("oner_sb", [1, 128], BF16)
        ffn_sb = sbl("ffn_sb", [128, 160 * 128], BF16)
        wslot = [sbl(f"wslot{i}", [128, 48 * 128], BF16) for i in range(2)]
        mA = sbl("mA", [1, TH], BF16)
        qA = sbl("qA", [1, TH], BF16)
        mB = sbl("mB", [1, TH], BF16)
        qB = sbl("qB", [1, TH], BF16)
        rstd = sbl("rstd", [1, TH], BF16)
        varb = sbl("varb", [1, TH], F32)
        b2h = sbl("b2h", [128, 8], F32)
        lnb_bf = sbl("lnb_bf", [128, 16], BF16)
        fcb = sbl("fcb", [128, 8], F32)
        tsl = sbl("tsl", [128, 2 * 512], F32)
        xb3 = sbl("xb3", [128, 4 * 512], BF16)
        sq3 = sbl("sq3", [128, 2 * 512], BF16)
        mS = xb3[:, 2 * 512:3 * 512]
        rS = xb3[:, 3 * 512:4 * 512]
        outb = sbl("outb", [128, 2 * 512], F32)

        B = {n: Buf(n) for n in
             ("consts onec oner ffn mA qA mB qB rstd varb b2h lnb fcb "
              "mS rS xcur h x2 xn ph gate hn m1 xb sq xc3 Pb2 Hq2 Qb2 "
              "Pb Hq Qb").split()}
        B_wslot = [Buf("ws0"), Buf("ws1")]
        B_tsl = [Buf("tsl0"), Buf("tsl1")]
        B_xb3 = [Buf(f"xb3{i}") for i in range(2)]
        B_sq3 = [Buf(f"sq3{i}") for i in range(2)]
        B_tail = [Buf(f"tail{i}") for i in range(4)]
        B_outb = [Buf("ob0"), Buf("ob1")]
        banks = [Buf(f"bk{i}", psum=True) for i in range(8)]
        pt = [es.enter_context(nc.psum_tensor(f"pb{i}", [128, 512], F32))
              for i in range(8)]

        def ccol(name, j=0, n=1):
            return consts[:, ccols[name] + j:ccols[name] + j + n]

        def fftile(name, k, m):
            i = ixs["ffn"][(name, k, m)]
            return ffn_sb[:, 128 * i:128 * (i + 1)]

        c = E.inc("q", q.dma_start(consts[:, :], const_h))
        E.note_w("q", B["consts"], c)
        c = E.inc("g", g.dma_start(onec[:, :], onec_h))
        E.note_w("g", B["onec"], c)
        c = E.inc("g", g.dma_start(oner[:, :], oner_h))
        E.note_w("g", B["oner"], c)
        c = E.inc("g", g.dma_start(ffn_sb[:, :], ffn_h))
        E.note_w("g", B["ffn"], c)

        E.r_wait("s", B["consts"])
        c = E.inc("s", s.mul(b2h[:, :], ccol("b2", 0, 8), 0.5))
        E.note_w("s", B["b2h"], c)
        c = E.inc("s", s.copy(lnb_bf[:, :], ccol("ln_b", 0, 16)))
        E.note_w("s", B["lnb"], c)

        sched = []
        for half in range(2):
            sched += [("ip", gi, 6144, ip_h[:, gi * 6144:(gi + 1) * 6144])
                      for gi in range(NG)]
            sched += [("fc", c, 4096, fc_h[:, c * 4096:(c + 1) * 4096])
                      for c in range(4)]
            sched += [("cp", c, 4096, cp_h[:, c * 4096:(c + 1) * 4096])
                      for c in range(4)]
            sched += [("ow", c, 4096, ow_h[:, c * 4096:(c + 1) * 4096])
                      for c in range(4)]
        ws = {"issued": 0, "dma": [None] * len(sched)}

        def ws_issue():
            i = ws["issued"]
            if i >= len(sched):
                return
            _, _, ncols, src = sched[i]
            E.w_wait("g", B_wslot[i % 2])
            c2 = E.inc("g", g.dma_start(wslot[i % 2][:, :ncols], src))
            E.note_w("g", B_wslot[i % 2], c2)
            ws["dma"][i] = c2
            ws["issued"] = i + 1

        def ws_tile(ci, which, key):
            j = ixs[which][key]
            return wslot[ci % 2][:, j * 128:(j + 1) * 128]

        def ws_ensure(ci):
            E.wait("t", "gd", ws["dma"][ci])

        def ws_done(ci):
            E.note_r("t", B_wslot[ci % 2], E.cnt["tc"])
            ws_issue()

        ws_issue()
        ws_issue()

        def bcast_to(src_ap, w, bi, B_src, dst_sb, B_dst):
            bk = banks[bi]
            E.w_wait("t", bk)
            E.r_wait("t", B["oner"])
            E.r_wait("t", B_src)
            ii = t.matmul(pt[bi][:, :w], oner[:, :], src_ap, start=True, stop=True)
            cnt = E.inc("t", ii)
            E.note_w("t", bk, cnt)
            E.note_r("t", B_src, cnt)
            E.r_wait("s", bk)
            E.w_wait("s", B_dst)
            c2 = E.inc("s", s.copy(dst_sb[:, 0:w], pt[bi][:, :w]))
            E.note_r("s", bk, c2)
            E.note_w("s", B_dst, c2)

        def rstd_batch(m_sb, q_sb, B_m, B_q):
            E.r_wait("v", B_m)
            E.w_wait("v", B["varb"])
            c2 = E.inc("v", v.tensor_mul(varb[0:1, :], m_sb[0:1, :], m_sb[0:1, :]))
            E.note_r("v", B_m, c2)
            E.note_w("v", B["varb"], c2)
            E.r_wait("v", B_q)
            c2 = E.inc("v", v.tensor_tensor(varb[0:1, :], q_sb[0:1, :],
                                            varb[0:1, :], ALU.subtract))
            E.note_r("v", B_q, c2)
            E.note_w("v", B["varb"], c2)
            E.r_wait("v", B["consts"])
            c2 = E.inc("v", v.tensor_scalar_add(
                varb[0:1, :], varb[0:1, :],
                consts[0:1, ccols["eps"]:ccols["eps"] + 1]))
            E.note_w("v", B["varb"], c2)
            c2 = E.inc("v", v.reciprocal(varb[0:1, :], varb[0:1, :]))
            E.note_w("v", B["varb"], c2)
            E.r_wait("s", B["varb"])
            E.w_wait("s", B["rstd"])
            c2 = E.inc("s", s.activation(rstd[0:1, :], varb[0:1, :], AF.Sqrt))
            E.note_r("s", B["varb"], c2)
            E.note_w("s", B["rstd"], c2)

        store_last = [0]

        for half in range(2):
            with ExitStack() as hs:
                def rsb(name, shape, dt, stack):
                    return stack.enter_context(
                        nc.sbuf_tensor(f"{name}{half}", shape, dt, side="right"))

                xcur = rsb("xcur", [128, DC * TH], F32, hs)
                h = rsb("h", [128, IC * HT], BF16, hs)
                B_xcur, B_h = B["xcur"], B["h"]

                hap = h[:, :]
                XB, SQ = 0, 4096
                Pb = h[:, 8192:9216]
                Hq = h[:, 9216:13312]
                Qb = h[:, 13312:14336]
                TAIL = 14336
                B_xb, B_sq = B["xb"], B["sq"]
                B_Pb, B_Hq, B_Qb = B["Pb"], B["Hq"], B["Qb"]

                def hfl(base, w):
                    return h[:, base:base + w]

                def xcur_v(tcol, w):
                    return ap3(xcur[:, :], tcol, TH, DC, w)

                E.barrier(("t", "v", "s"), dma=True)
                E.wait("q", "tc", E.cnt["tc"])
                E.wait("q", "vc", E.cnt["vc"])
                E.wait("q", "sc", E.cnt["sc"])
                if half == 0:
                    E.w_wait("q", B_xcur)
                    c = E.inc("q", q.dma_start(
                        xcur[:, :], x_h[:, 0:DC * TH]))
                    E.note_w("q", B_xcur, c)

                rot = [0]

                def nb(held=()):
                    while True:
                        rot[0] = (rot[0] + 1) % 8
                        if rot[0] not in held:
                            return rot[0]

                Pb_c, Qb_c = [Pb], [Qb]
                B_Pb_c, B_Qb_c = [B_Pb], [B_Qb]
                HqD = [[Hq, Hq]]
                B_HqD = [[B_Hq, B_Hq]]

                def ffn_tile(xsrc, B_xsrc, dst_add, w, after_u1=None,
                             avoid=()):
                    for m in range(2):
                        bi = nb(held=avoid)
                        bk = banks[bi]
                        E.w_wait("t", bk)
                        for k in range(8):
                            E.r_wait("t", B_xsrc)
                            ii = t.matmul(pt[bi][:, :w], fftile("U1", k, m),
                                          xsrc(k), start=(k == 0), stop=(k == 7))
                        cnt = E.inc("t", ii)
                        E.note_w("t", bk, cnt)
                        E.note_r("t", B_xsrc, cnt)
                        E.r_wait("s", bk)
                        E.w_wait("s", B_Pb_c[0])
                        c2 = E.inc("s", s.copy(Pb_c[0][:, m * 512:m * 512 + w],
                                               pt[bi][:, :w]))
                        E.note_r("s", bk, c2)
                        E.note_w("s", B_Pb_c[0], c2)
                    if after_u1 is not None:
                        after_u1()
                    u2a = nb(held=avoid)
                    u2b_ = nb(held=(u2a,) + tuple(avoid))
                    u2 = (u2a, u2b_)
                    for mu in range(2):
                        E.w_wait("t", banks[u2[mu]])
                    for qq in range(4):
                        Hqb = HqD[0][qq % 2]
                        B_Hqb = B_HqD[0][qq % 2]
                        for mq in range(8):
                            m = qq * 8 + mq
                            bi = nb(held=u2 + tuple(avoid))
                            bk = banks[bi]
                            E.w_wait("t", bk)
                            for k in range(2):
                                E.r_wait("t", B_Pb_c[0])
                                ii = t.matmul(pt[bi][:, :w], fftile("V1", k, m),
                                              Pb_c[0][:, k * 512:k * 512 + w],
                                              start=(k == 0), stop=(k == 1))
                            cnt = E.inc("t", ii)
                            E.note_w("t", bk, cnt)
                            E.note_r("t", B_Pb_c[0], cnt)
                            E.r_wait("s", bk)
                            E.w_wait("s", B_Hqb)
                            E.r_wait("s", B["consts"])
                            c2 = E.inc("s", s.activation(
                                Hqb[:, mq * 512:mq * 512 + w], pt[bi][:, :w],
                                AF.Gelu_apprx_tanh, bias=ccol("b1", m), scale=1.0))
                            E.note_r("s", bk, c2)
                            E.note_w("s", B_Hqb, c2)
                        for mu in range(2):
                            for kq in range(8):
                                k = qq * 8 + kq
                                E.r_wait("t", B_Hqb)
                                ii = t.matmul(pt[u2[mu]][:, :w], fftile("U2", k, mu),
                                              Hqb[:, kq * 512:kq * 512 + w],
                                              start=(qq == 0 and kq == 0),
                                              stop=(qq == 3 and kq == 7))
                            cnt = E.inc("t", ii)
                            E.note_r("t", B_Hqb, cnt)
                            if qq == 3:
                                E.note_w("t", banks[u2[mu]], cnt)
                    for mu in range(2):
                        bk = banks[u2[mu]]
                        E.r_wait("s", bk)
                        E.w_wait("s", B_Qb_c[0])
                        c2 = E.inc("s", s.copy(Qb_c[0][:, mu * 512:mu * 512 + w],
                                               pt[u2[mu]][:, :w]))
                        E.note_r("s", bk, c2)
                        E.note_w("s", B_Qb_c[0], c2)
                    for m in range(8):
                        bi = nb(held=avoid)
                        bk = banks[bi]
                        E.w_wait("t", bk)
                        for k in range(2):
                            E.r_wait("t", B_Qb_c[0])
                            ii = t.matmul(pt[bi][:, :w], fftile("V2", k, m),
                                          Qb_c[0][:, k * 512:k * 512 + w],
                                          start=(k == 0), stop=(k == 1))
                        cnt = E.inc("t", ii)
                        E.note_w("t", bk, cnt)
                        E.note_r("t", B_Qb_c[0], cnt)
                        dst_add(m, bi, bk)

                for si, (tcol, w) in enumerate(TILES):
                    bm = 4 if si % 2 == 0 else 6
                    bq = bm + 1
                    E.w_wait("t", banks[bm])
                    E.w_wait("t", banks[bq])
                    for cch in range(DC):
                        sl = cch % 2
                        xsl = TAIL + sl * 512
                        qsl = TAIL + 1024 + sl * 512
                        E.r_wait("s", B_xcur)
                        E.w_wait("s", B_tail[sl])
                        c2 = E.inc("s", s.copy(
                            hfl(xsl, w), xcur[:, cch * TH + tcol:cch * TH + tcol + w]))
                        E.note_r("s", B_xcur, c2)
                        E.note_w("s", B_tail[sl], c2)
                        E.r_wait("v", B_tail[sl])
                        E.w_wait("v", B_tail[2 + sl])
                        c2 = E.inc("v", v.tensor_mul(hfl(qsl, w), hfl(xsl, w),
                                                     hfl(xsl, w)))
                        E.note_r("v", B_tail[sl], c2)
                        E.note_w("v", B_tail[2 + sl], c2)
                        E.r_wait("t", B_tail[sl])
                        E.r_wait("t", B["onec"])
                        ii = t.matmul(pt[bm][0:1, :w], onec[:, :], hfl(xsl, w),
                                      start=(cch == 0), stop=(cch == DC - 1))
                        cnt = E.inc("t", ii)
                        E.note_r("t", B_tail[sl], cnt)
                        if cch == DC - 1:
                            E.note_w("t", banks[bm], cnt)
                        E.r_wait("t", B_tail[2 + sl])
                        ii = t.matmul(pt[bq][0:1, :w], onec[:, :], hfl(qsl, w),
                                      start=(cch == 0), stop=(cch == DC - 1))
                        cnt = E.inc("t", ii)
                        E.note_r("t", B_tail[2 + sl], cnt)
                        if cch == DC - 1:
                            E.note_w("t", banks[bq], cnt)
                    for bb, dst, bfd in ((bm, mA, B["mA"]), (bq, qA, B["qA"])):
                        E.r_wait("s", banks[bb])
                        E.w_wait("s", bfd)
                        c2 = E.inc("s", s.activation(dst[0:1, tcol:tcol + w],
                                                     pt[bb][0:1, :w], AF.Copy,
                                                     scale=1.0 / D))
                        E.note_r("s", banks[bb], c2)
                        E.note_w("s", bfd, c2)
                rstd_batch(mA, qA, B["mA"], B["qA"])

                E.barrier(("t", "v", "s"))

                for ti, (tcol, w) in enumerate(TILES):
                    bcast_to(mA[0:1, tcol:tcol + w], w, 6, B["mA"], mS, B["mS"])
                    bcast_to(rstd[0:1, tcol:tcol + w], w, 7, B["rstd"], rS, B["rS"])
                    E.r_wait("v", B["mS"])
                    E.r_wait("v", B_xcur)
                    E.w_wait("v", B_xb)
                    c2 = E.inc("v", v.tensor_tensor(ap3(hap, XB, 512, DC, w),
                                                    xcur_v(tcol, w),
                                                    bc_ap(mS[:, 0:w], DC, w),
                                                    ALU.subtract))
                    E.note_r("v", B["mS"], c2)
                    E.note_r("v", B_xcur, c2)
                    E.note_w("v", B_xb, c2)
                    E.r_wait("v", B["rS"])
                    E.w_wait("v", B_sq)
                    c2 = E.inc("v", v.tensor_tensor(ap3(hap, SQ, 512, DC, w),
                                                    ap3(hap, XB, 512, DC, w),
                                                    bc_ap(rS[:, 0:w], DC, w),
                                                    ALU.mult))
                    E.note_r("v", B["rS"], c2)
                    E.note_r("v", B_xb, c2)
                    E.note_w("v", B_sq, c2)

                    def dst_add(m, bi, bk, _tcol=tcol, _w=w):
                        sl = m % 2
                        E.r_wait("s", bk)
                        E.w_wait("s", B_tsl[sl])
                        c3 = E.inc("s", s.mul(tsl[:, sl * 512:sl * 512 + _w],
                                              pt[bi][:, :_w], 0.5))
                        E.note_r("s", bk, c3)
                        E.note_w("s", B_tsl[sl], c3)
                        E.r_wait("v", B_tsl[sl])
                        E.r_wait("v", B["b2h"])
                        dstc = xcur[:, m * TH + _tcol:m * TH + _tcol + _w]
                        c3 = E.inc("v", v.scalar_tensor_tensor(
                            dstc, tsl[:, sl * 512:sl * 512 + _w],
                            b2h[:, m:m + 1], dstc, ALU.add, ALU.add))
                        E.note_r("v", B_tsl[sl], c3)
                        E.note_r("v", B["b2h"], c3)
                        E.note_w("v", B_xcur, c3)

                    ffn_tile(lambda k: hfl(SQ + k * 512, w), B_sq, dst_add, w)

                    E.r_wait("s", B_xcur)
                    E.w_wait("s", B_xb)
                    c2 = E.inc("s", s.copy(ap3(hap, XB, 512, DC, w),
                                           xcur_v(tcol, w)))
                    E.note_r("s", B_xcur, c2)
                    E.note_w("s", B_xb, c2)
                    E.r_wait("v", B_xb)
                    E.w_wait("v", B_sq)
                    c2 = E.inc("v", v.tensor_mul(ap3(hap, SQ, 512, DC, w),
                                                 ap3(hap, XB, 512, DC, w),
                                                 ap3(hap, XB, 512, DC, w)))
                    E.note_r("v", B_xb, c2)
                    E.note_w("v", B_sq, c2)
                    E.r_wait("t", B_xb)
                    E.r_wait("t", B_sq)
                    E.w_wait("t", banks[4])
                    E.r_wait("t", B["onec"])
                    for k in range(DC):
                        ii = t.matmul(pt[4][0:1, :w], onec[:, :], hfl(XB + k * 512, w),
                                      start=(k == 0), stop=(k == DC - 1))
                    cnt = E.inc("t", ii)
                    E.note_w("t", banks[4], cnt)
                    E.note_r("t", B_xb, cnt)
                    E.w_wait("t", banks[5])
                    for k in range(DC):
                        ii = t.matmul(pt[5][0:1, :w], onec[:, :], hfl(SQ + k * 512, w),
                                      start=(k == 0), stop=(k == DC - 1))
                    cnt = E.inc("t", ii)
                    E.note_w("t", banks[5], cnt)
                    E.note_r("t", B_sq, cnt)
                    for bb, dst, bfd in ((4, mB, B["mB"]), (5, qB, B["qB"])):
                        E.r_wait("s", banks[bb])
                        E.w_wait("s", bfd)
                        c2 = E.inc("s", s.activation(dst[0:1, tcol:tcol + w],
                                                     pt[bb][0:1, :w], AF.Copy,
                                                     scale=1.0 / D))
                        E.note_r("s", banks[bb], c2)
                        E.note_w("s", bfd, c2)
                rstd_batch(mB, qB, B["mB"], B["qB"])

                with ExitStack() as p1s:
                    xn = rsb("xn", [128, DC * TH], BF16, p1s)
                    ph = rsb("ph", [128, 2 * (3 + TH)], BF16, p1s)
                    gate = rsb("gate", [128, 2 * 512], BF16, p1s)
                    B_xn, B_ph, B_gate = B["xn"], B["ph"], B["gate"]
                    PHW = 3 + TH

                    for (tcol, w) in TILES:
                        bcast_to(mB[0:1, tcol:tcol + w], w, 6, B["mB"], mS, B["mS"])
                        bcast_to(rstd[0:1, tcol:tcol + w], w, 7, B["rstd"], rS, B["rS"])
                        E.r_wait("v", B["mS"])
                        E.r_wait("v", B_xcur)
                        E.w_wait("v", B_xb)
                        c2 = E.inc("v", v.tensor_tensor(
                            ap3(hap, XB, 512, DC, w), xcur_v(tcol, w),
                            bc_ap(mS[:, 0:w], DC, w), ALU.subtract))
                        E.note_r("v", B["mS"], c2)
                        E.note_r("v", B_xcur, c2)
                        E.note_w("v", B_xb, c2)
                        E.r_wait("v", B["rS"])
                        E.w_wait("v", B_xn)
                        c2 = E.inc("v", v.tensor_tensor(
                            ap3(xn[:, :], tcol, TH, DC, w),
                            ap3(hap, XB, 512, DC, w),
                            bc_ap(rS[:, 0:w], DC, w), ALU.mult))
                        E.note_r("v", B["rS"], c2)
                        E.note_r("v", B_xb, c2)
                        E.note_w("v", B_xn, c2)

                    E.barrier(("v",))

                    HELD = (2, 3, 4, 5)
                    for pr in HELD:
                        E.w_wait("t", banks[pr])
                    pending_stats = []

                    def flush_stats():
                        for fn in pending_stats:
                            fn()
                        pending_stats.clear()

                    for gi in range(NG):
                        ci = half * 20 + gi
                        ws_ensure(ci)
                        for tix, (tcol, w) in enumerate(TILES):
                            for ml in range(2):
                                bi = nb(held=HELD)
                                bk = banks[bi]
                                E.w_wait("t", bk)
                                for k in range(8):
                                    E.r_wait("t", B_xn)
                                    ii = t.matmul(
                                        pt[bi][:, :w],
                                        ws_tile(ci, "ip", ("Wh", k, ml)),
                                        xn[:, k * TH + tcol:k * TH + tcol + w],
                                        start=(k == 0), stop=(k == 7))
                                cnt = E.inc("t", ii)
                                E.note_w("t", bk, cnt)
                                E.note_r("t", B_xn, cnt)
                                if ml == 0:
                                    flush_stats()
                                E.r_wait("s", bk)
                                E.w_wait("s", B_ph)
                                c2 = E.inc("s", s.copy(
                                    ph[:, ml * PHW + 3 + tcol:
                                       ml * PHW + 3 + tcol + w], pt[bi][:, :w]))
                                E.note_r("s", bk, c2)
                                E.note_w("s", B_ph, c2)
                            if tcol == 0:
                                E.r_wait("v", B_ph)
                                E.r_wait("v", B["consts"])
                                mv = ap3(ph[:, :], 32, PHW, 2, 3)
                                c2 = E.inc("v", v.tensor_scalar_mul(
                                    mv, mv, ccol("mask", half, 1)))
                                E.note_r("v", B["consts"], c2)
                                E.note_w("v", B_ph, c2)
                                continue
                            for ml in range(2):
                                bi = nb(held=HELD)
                                bk = banks[bi]
                                E.w_wait("t", bk)
                                for k in range(8):
                                    ii = t.matmul(
                                        pt[bi][:, :w],
                                        ws_tile(ci, "ip", ("Wg", k, ml)),
                                        xn[:, k * TH + tcol:k * TH + tcol + w],
                                        start=(k == 0), stop=(k == 7))
                                cnt = E.inc("t", ii)
                                E.note_w("t", bk, cnt)
                                E.note_r("t", B_xn, cnt)
                                E.r_wait("s", bk)
                                E.w_wait("s", B_gate)
                                c2 = E.inc("s", s.activation(
                                    gate[:, ml * 512:ml * 512 + w],
                                    pt[bi][:, :w], AF.Silu))
                                E.note_r("s", bk, c2)
                                E.note_w("s", B_gate, c2)
                            for mc in range(2):
                                bi = nb(held=HELD)
                                bk = banks[bi]
                                E.w_wait("t", bk)
                                first = True
                                for tap in range(KTAP):
                                    for kc in range(2):
                                        E.r_wait("t", B_ph)
                                        ii = t.matmul(
                                            pt[bi][:, :w],
                                            ws_tile(ci, "ip", ("cv", tap, kc, mc)),
                                            ph[:, kc * PHW + tcol + tap:
                                               kc * PHW + tcol + tap + w],
                                            start=first,
                                            stop=(tap == 3 and kc == 1))
                                        first = False
                                cnt = E.inc("t", ii)
                                E.note_w("t", bk, cnt)
                                E.note_r("t", B_ph, cnt)
                                cch = 2 * gi + mc
                                base = cch * HT + (tcol - HALO)
                                E.r_wait("v", bk)
                                E.r_wait("v", B_gate)
                                E.r_wait("v", B["consts"])
                                E.w_wait("v", B_h)
                                c2 = E.inc("v", v.scalar_tensor_tensor(
                                    h[:, base:base + w], pt[bi][:, :w],
                                    ccol("conv_b", cch, 1),
                                    gate[:, mc * 512:mc * 512 + w],
                                    ALU.add, ALU.mult))
                                E.note_r("v", bk, c2)
                                E.note_r("v", B_gate, c2)
                                E.note_w("v", B_h, c2)
                                si2 = tix - 1
                                bm, bq = (2, 3) if si2 == 0 else (4, 5)
                                gsl = gate[:, mc * 512:mc * 512 + w]
                                E.r_wait("v", B_h)
                                E.w_wait("v", B_gate)
                                c2 = E.inc("v", v.tensor_mul(
                                    gsl, h[:, base:base + w],
                                    h[:, base:base + w]))
                                E.note_r("v", B_h, c2)
                                E.note_w("v", B_gate, c2)

                                def _stats(_bm=bm, _bq=bq, _base=base, _w=w,
                                           _gsl=gsl, _gi=gi, _mc=mc):
                                    E.r_wait("t", B_h)
                                    E.r_wait("t", B["onec"])
                                    ii2 = t.matmul(pt[_bm][0:1, :_w], onec[:, :],
                                                   h[:, _base:_base + _w],
                                                   start=(_gi == 0 and _mc == 0),
                                                   stop=(_gi == NG - 1 and _mc == 1))
                                    cn = E.inc("t", ii2)
                                    E.note_r("t", B_h, cn)
                                    if _gi == NG - 1 and _mc == 1:
                                        E.note_w("t", banks[_bm], cn)
                                    E.r_wait("t", B_gate)
                                    ii2 = t.matmul(pt[_bq][0:1, :_w], onec[:, :],
                                                   _gsl,
                                                   start=(_gi == 0 and _mc == 0),
                                                   stop=(_gi == NG - 1 and _mc == 1))
                                    cn = E.inc("t", ii2)
                                    E.note_r("t", B_gate, cn)
                                    if _gi == NG - 1 and _mc == 1:
                                        E.note_w("t", banks[_bq], cn)
                                pending_stats.append(_stats)
                        ws_done(ci)
                    flush_stats()
                    for si2, (tcol, w) in enumerate(OWN_TILES):
                        bm, bq = (2, 3) if si2 == 0 else (4, 5)
                        for bb, dst, bfd in ((bm, mA, B["mA"]), (bq, qA, B["qA"])):
                            E.r_wait("s", banks[bb])
                            E.w_wait("s", bfd)
                            c2 = E.inc("s", s.activation(
                                dst[0:1, tcol:tcol + w], pt[bb][0:1, :w],
                                AF.Copy, scale=1.0 / I2))
                            E.note_r("s", banks[bb], c2)
                            E.note_w("s", bfd, c2)
                rstd_batch(mA, qA, B["mA"], B["qA"])

                with ExitStack() as p2s:
                    hn = rsb("hn", [128, IC * HT], BF16, p2s)
                    m1 = rsb("m1", [128, DC * HT], BF16, p2s)
                    B_hn, B_m1 = B["hn"], B["m1"]

                    def h_v(t0o, w):
                        return ap3(h[:, :], t0o, HT, IC, w)

                    def hn_v(t0o, w):
                        return ap3(hn[:, :], t0o, HT, IC, w)

                    mS2 = xb3[:, 0:512]
                    rS2 = xb3[:, 512:1024]
                    (t1c, t1w), (t2c, t2w) = OWN_TILES
                    bcast_to(mA[0:1, t1c:t1c + t1w], t1w, 6, B["mA"], mS, B["mS"])
                    bcast_to(rstd[0:1, t1c:t1c + t1w], t1w, 7, B["rstd"], rS, B["rS"])
                    bcast_to(mA[0:1, t2c:t2c + t2w], t2w, 6, B["mA"], mS2, B_xb3[0])
                    bcast_to(rstd[0:1, t2c:t2c + t2w], t2w, 7, B["rstd"], rS2, B_xb3[1])
                    hn_ready = [0] * IC
                    for k in range(IC):
                        for si, (tcol, w) in enumerate(OWN_TILES):
                            t0o = tcol - HALO
                            base = k * HT + t0o
                            msl = mS if si == 0 else mS2
                            rsl = rS if si == 0 else rS2
                            bmsl = B["mS"] if si == 0 else B_xb3[0]
                            brsl = B["rS"] if si == 0 else B_xb3[1]
                            E.r_wait("v", bmsl)
                            E.r_wait("v", B_h)
                            E.w_wait("v", B_hn)
                            c2 = E.inc("v", v.tensor_tensor(
                                hn[:, base:base + w], h[:, base:base + w],
                                msl[:, 0:w], ALU.subtract))
                            E.note_r("v", bmsl, c2)
                            E.note_r("v", B_h, c2)
                            E.note_w("v", B_hn, c2)
                            E.r_wait("v", brsl)
                            E.r_wait("v", B["consts"])
                            c2 = E.inc("v", v.scalar_tensor_tensor(
                                hn[:, base:base + w], hn[:, base:base + w],
                                ccol("ln_g", k, 1), rsl[:, 0:w],
                                ALU.mult, ALU.mult))
                            E.note_r("v", brsl, c2)
                            E.note_w("v", B_hn, c2)
                        hn_ready[k] = E.cnt["vc"]

                    for m in range(8):
                        ci = half * 20 + 8 + m // 2
                        if m % 2 == 0:
                            ws_ensure(ci)
                        if half == 0:
                            bi = nb()
                            bk = banks[bi]
                            E.w_wait("t", bk)
                            E.r_wait("t", B["lnb"])
                            for k in range(16):
                                ii = t.matmul(pt[bi][:, 0:1],
                                              ws_tile(ci, "fc", ("fc", m % 2, k)),
                                              lnb_bf[:, k:k + 1],
                                              start=(k == 0), stop=(k == 15))
                            cnt = E.inc("t", ii)
                            E.note_w("t", bk, cnt)
                            E.r_wait("v", bk)
                            E.w_wait("v", B["fcb"])
                            E.r_wait("v", B["consts"])
                            c2 = E.inc("v", v.tensor_scalar_add(
                                fcb[:, m:m + 1], pt[bi][:, 0:1],
                                ccol("fc_b", m, 1)))
                            E.note_r("v", bk, c2)
                            E.note_w("v", B["fcb"], c2)
                        for (tcol, w) in OWN_TILES:
                            t0o = tcol - HALO
                            bi = nb()
                            bk = banks[bi]
                            E.w_wait("t", bk)
                            for k in range(16):
                                E.wait("t", "vc", hn_ready[k])
                                ii = t.matmul(pt[bi][:, :w],
                                              ws_tile(ci, "fc", ("fc", m % 2, k)),
                                              hn[:, k * HT + t0o:k * HT + t0o + w],
                                              start=(k == 0), stop=(k == 15))
                            cnt = E.inc("t", ii)
                            E.note_w("t", bk, cnt)
                            E.note_r("t", B_hn, cnt)
                            E.r_wait("s", bk)
                            E.w_wait("s", B_m1)
                            E.r_wait("s", B["fcb"])
                            c2 = E.inc("s", s.activation(
                                m1[:, m * HT + t0o:m * HT + t0o + w],
                                pt[bi][:, :w], AF.Silu,
                                bias=fcb[:, m:m + 1], scale=1.0))
                            E.note_r("s", bk, c2)
                            E.note_w("s", B_m1, c2)
                        if m % 2 == 1:
                            ws_done(ci)

                    for m in range(16):
                        ci = half * 20 + 12 + m // 4
                        if m % 4 == 0:
                            ws_ensure(ci)
                        for (tcol, w) in OWN_TILES:
                            t0o = tcol - HALO
                            bi = nb()
                            bk = banks[bi]
                            E.w_wait("t", bk)
                            for k in range(8):
                                E.r_wait("t", B_m1)
                                ii = t.matmul(pt[bi][:, :w],
                                              ws_tile(ci, "cp", ("cp", m % 4, k)),
                                              m1[:, k * HT + t0o:k * HT + t0o + w],
                                              start=(k == 0), stop=(k == 7))
                            cnt = E.inc("t", ii)
                            E.note_w("t", bk, cnt)
                            E.note_r("t", B_m1, cnt)
                            base = m * HT + t0o
                            E.r_wait("v", bk)
                            E.r_wait("v", B["consts"])
                            E.w_wait("v", B_h)
                            c2 = E.inc("v", v.scalar_tensor_tensor(
                                h[:, base:base + w], pt[bi][:, :w],
                                ccol("cproj_b", m, 1), h[:, base:base + w],
                                ALU.add, ALU.add))
                            E.note_r("v", bk, c2)
                            E.note_w("v", B_h, c2)
                        if m % 4 == 3:
                            ws_done(ci)

                x2 = hs.enter_context(
                    nc.sbuf_tensor(f"x2_{half}", [128, DC * HT], F32, side="left"))
                B_x2 = B["x2"]
                stat_banks = {0: (4, 5), 1: (2, 3)}
                mm_banks = {0: (0, 6), 1: (1, 7)}
                for pr in stat_banks.values():
                    for bi in pr:
                        E.w_wait("t", banks[bi])
                pending3 = []

                def flush3():
                    for fn in pending3:
                        fn()
                    pending3.clear()

                for m in range(8):
                    ci = half * 20 + 16 + m // 2
                    if m % 2 == 0:
                        ws_ensure(ci)
                    for si, (tcol, w) in enumerate(OWN_TILES):
                        t0o = tcol - HALO
                        bi = mm_banks[si][m % 2]
                        bk = banks[bi]
                        E.w_wait("t", bk)
                        for k in range(16):
                            E.r_wait("t", B_h)
                            ii = t.matmul(pt[bi][:, :w],
                                          ws_tile(ci, "ow", ("ow", m % 2, k)),
                                          h[:, k * HT + t0o:k * HT + t0o + w],
                                          start=(k == 0), stop=(k == 15))
                        cnt = E.inc("t", ii)
                        E.note_w("t", bk, cnt)
                        E.note_r("t", B_h, cnt)
                        flush3()
                        base = m * HT + t0o
                        E.r_wait("v", bk)
                        E.r_wait("v", B_xcur)
                        E.w_wait("v", B_x2)
                        c2 = E.inc("v", v.tensor_add(
                            x2[:, base:base + w], pt[bi][:, :w],
                            xcur[:, m * TH + tcol:m * TH + tcol + w]))
                        E.note_r("v", bk, c2)
                        E.note_r("v", B_xcur, c2)
                        E.note_w("v", B_x2, c2)
                        sl = si
                        E.r_wait("s", B_x2)
                        E.w_wait("s", B_xb3[sl])
                        c2 = E.inc("s", s.copy(xb3[:, sl * 512:sl * 512 + w],
                                               x2[:, base:base + w]))
                        E.note_r("s", B_x2, c2)
                        E.note_w("s", B_xb3[sl], c2)
                        E.r_wait("v", B_xb3[sl])
                        E.w_wait("v", B_sq3[sl])
                        c2 = E.inc("v", v.tensor_mul(sq3[:, sl * 512:sl * 512 + w],
                                                     xb3[:, sl * 512:sl * 512 + w],
                                                     xb3[:, sl * 512:sl * 512 + w]))
                        E.note_r("v", B_xb3[sl], c2)
                        E.note_w("v", B_sq3[sl], c2)
                        bmi, bqi = stat_banks[si]

                        def _st3(_bmi=bmi, _bqi=bqi, _sl=sl, _m=m, _w=w):
                            E.r_wait("t", B_xb3[_sl])
                            ii2 = t.matmul(pt[_bmi][0:1, :_w], onec[:, :],
                                           xb3[:, _sl * 512:_sl * 512 + _w],
                                           start=(_m == 0), stop=(_m == 7))
                            cn = E.inc("t", ii2)
                            E.note_r("t", B_xb3[_sl], cn)
                            if _m == 7:
                                E.note_w("t", banks[_bmi], cn)
                            E.r_wait("t", B_sq3[_sl])
                            ii2 = t.matmul(pt[_bqi][0:1, :_w], onec[:, :],
                                           sq3[:, _sl * 512:_sl * 512 + _w],
                                           start=(_m == 0), stop=(_m == 7))
                            cn = E.inc("t", ii2)
                            E.note_r("t", B_sq3[_sl], cn)
                            if _m == 7:
                                E.note_w("t", banks[_bqi], cn)
                        pending3.append(_st3)
                    if m % 2 == 1:
                        ws_done(ci)
                flush3()
                if half == 0:
                    E.w_wait("q", B_xcur)
                    c = E.inc("q", q.dma_start(
                        xcur[:, :], x_h[:, DC * TH:2 * DC * TH]))
                    E.note_w("q", B_xcur, c)
                for si, (tcol, w) in enumerate(OWN_TILES):
                    bmi, bqi = stat_banks[si]
                    for bb, dst, bfd in ((bmi, mA, B["mA"]), (bqi, qA, B["qA"])):
                        E.r_wait("s", banks[bb])
                        E.w_wait("s", bfd)
                        c2 = E.inc("s", s.activation(dst[0:1, tcol:tcol + w],
                                                     pt[bb][0:1, :w], AF.Copy,
                                                     scale=1.0 / D))
                        E.note_r("s", banks[bb], c2)
                        E.note_w("s", bfd, c2)
                rstd_batch(mA, qA, B["mA"], B["qA"])

                xc3 = h[:, 4096:8192]
                HqB2 = h[:, 0:4096]
                HqD[0] = [Hq, HqB2]
                B_HqD[0] = [B_Hq, B["Hq2"]]
                Pb_c[0], Qb_c[0] = Pb, Qb
                B_xc3 = B["xc3"]
                E.barrier(("v", "s"))

                for si, (tcol, w) in enumerate(OWN_TILES):
                    t0o = tcol - HALO
                    bcast_to(mA[0:1, tcol:tcol + w], w, 6, B["mA"], mS, B["mS"])
                    bcast_to(rstd[0:1, tcol:tcol + w], w, 7, B["rstd"], rS, B["rS"])
                    E.r_wait("v", B["mS"])
                    E.r_wait("v", B_x2)
                    E.w_wait("v", B_xc3)
                    xc3d = ap3(xc3, 0, 512, DC, w)
                    c2 = E.inc("v", v.tensor_tensor(
                        xc3d, ap3(x2[:, :], t0o, HT, DC, w),
                        bc_ap(mS[:, 0:w], DC, w), ALU.subtract))
                    E.note_r("v", B["mS"], c2)
                    E.note_r("v", B_x2, c2)
                    E.note_w("v", B_xc3, c2)
                    E.r_wait("v", B["rS"])
                    c2 = E.inc("v", v.tensor_tensor(
                        xc3d, xc3d, bc_ap(rS[:, 0:w], DC, w), ALU.mult))
                    E.note_r("v", B["rS"], c2)
                    E.note_w("v", B_xc3, c2)

                    def dst_add2(m, bi, bk, _si=si, _t0o=t0o, _tcol=tcol, _w=w):
                        sl = m % 2
                        E.r_wait("s", bk)
                        E.w_wait("s", B_tsl[sl])
                        c3 = E.inc("s", s.mul(tsl[:, sl * 512:sl * 512 + _w],
                                              pt[bi][:, :_w], 0.5))
                        E.note_r("s", bk, c3)
                        E.note_w("s", B_tsl[sl], c3)
                        ob = B_outb[sl]
                        E.r_wait("v", B_tsl[sl])
                        E.r_wait("v", B_x2)
                        E.r_wait("v", B["b2h"])
                        E.w_wait("v", ob)
                        c3 = E.inc("v", v.scalar_tensor_tensor(
                            outb[:, sl * 512:sl * 512 + _w],
                            tsl[:, sl * 512:sl * 512 + _w],
                            b2h[:, m:m + 1],
                            x2[:, m * HT + _t0o:m * HT + _t0o + _w],
                            ALU.add, ALU.add))
                        E.note_r("v", B_tsl[sl], c3)
                        E.note_r("v", B_x2, c3)
                        E.note_w("v", ob, c3)
                        E.r_wait("q", ob)
                        c3 = E.inc("q", q.dma_start(
                            y_h[:, m * 2048 + half * HT + _t0o:
                                m * 2048 + half * HT + _t0o + _w],
                            outb[:, sl * 512:sl * 512 + _w]))
                        E.note_r("q", ob, c3)
                        store_last[0] = c3

                    ffn_tile(lambda k: xc3[:, k * 512:k * 512 + w],
                             B_xc3, dst_add2, w, avoid=(6, 7))

        E.wait("q", "sd", store_last[0])
    return nc


def kernel(**inputs):
    blobs, ixs, const_blob, ccols = build_host(inputs)
    x_in = np.asarray(inputs["L0_attn_hidden_states"], np.float32)
    xs = shard_x(x_in)
    nc = build_nc(ixs, ccols, const_blob.shape[1])
    onec = np.ones((128, 1), np.float32)
    oner = np.ones((1, 128), np.float32)
    in_maps = []
    for core in range(NCORES):
        chunk = core % 2
        cb = const_blob.copy()
        cb[:, ccols["mask"] + 0] = 1.0 if chunk == 1 else 0.0
        cb[:, ccols["mask"] + 1] = 1.0
        in_maps.append(dict(x=xs[core], const_blob=cb, onec=onec, oner=oner,
                            **blobs))
    res = run_bass_kernel_spmd(nc, in_maps, core_ids=list(range(NCORES)))
    return gather_y(res.results)


# version 12
# speedup vs baseline: 1.0696x; 1.0071x over previous
import numpy as np
import ml_dtypes
from contextlib import ExitStack

import concourse.bass as bass
import concourse.mybir as mybir
from concourse.ap import AP
from concourse.bass_utils import run_bass_kernel_spmd

BF16 = mybir.dt.bfloat16
F32 = mybir.dt.float32
AF = mybir.ActivationFunctionType
ALU = mybir.AluOpType

D = 1024
I2 = 2048
DFF = 4096
NG = 8
KTAP = 4
HALO = 32
HT = 1024
TH = HALO + HT
NCORES = 8
EPS = 1e-5
TILES = [(0, HALO), (HALO, 512), (HALO + 512, 512)]
OWN_TILES = TILES[1:]
DC, IC = 8, 16


def _tiles(WT):
    return {(k, m): WT[128 * k:128 * (k + 1), 128 * m:128 * (m + 1)]
            for k in range(WT.shape[0] // 128) for m in range(WT.shape[1] // 128)}


def _blob(entries):
    ix, arrs = {}, []
    for i, (key, a) in enumerate(entries):
        ix[key] = i
        arrs.append(np.asarray(a, np.float32))
    return np.ascontiguousarray(np.concatenate(arrs, axis=1)), ix


def _col_pack(vec, nch):
    return np.ascontiguousarray(np.asarray(vec, np.float32).reshape(nch, 128).T)


def build_host(inputs):
    f32 = lambda k: np.ascontiguousarray(np.asarray(inputs[k], np.float32))
    ipw = f32("L0_attn_in_proj_w")
    WhT = np.ascontiguousarray(ipw[:I2, :].T)
    WgT = np.ascontiguousarray(ipw[I2:, :].T)
    cw = f32("L0_attn_conv_w")
    fcT = np.ascontiguousarray(f32("L0_attn_fc_w").T)
    cpT = np.ascontiguousarray(f32("L0_attn_cproj_w").T)
    owT = np.ascontiguousarray(f32("L0_attn_out_w").T)

    tU1, tV1 = _tiles(f32("L0_ffn_U1")), _tiles(f32("L0_ffn_V1"))
    tU2, tV2 = _tiles(f32("L0_ffn_U2")), _tiles(f32("L0_ffn_V2"))
    ents = ([(("U1", k, m), tU1[(k, m)]) for m in range(2) for k in range(8)]
            + [(("V1", k, m), tV1[(k, m)]) for m in range(32) for k in range(2)]
            + [(("U2", k, m), tU2[(k, m)]) for m in range(2) for k in range(32)]
            + [(("V2", k, m), tV2[(k, m)]) for m in range(8) for k in range(2)])
    ffn_blob, ffn_ix = _blob(ents)

    tWh, tWg = _tiles(WhT), _tiles(WgT)
    ip_chunks, ip_ix = [], None
    for g in range(NG):
        ents = ([(("Wh", k, ml), tWh[(k, 2 * g + ml)])
                 for ml in range(2) for k in range(8)]
                + [(("Wg", k, ml), tWg[(k, 2 * g + ml)])
                   for ml in range(2) for k in range(8)])
        for mc in range(2):
            for tap in range(KTAP):
                CT = np.ascontiguousarray(cw[256 * g:256 * (g + 1), :, tap].T)
                for kc in range(2):
                    ents.append((("cv", tap, kc, mc),
                                 CT[128 * kc:128 * (kc + 1),
                                    128 * mc:128 * (mc + 1)]))
        blob, ip_ix = _blob(ents)
        ip_chunks.append(blob)
    ip_blob = np.ascontiguousarray(np.concatenate(ip_chunks, axis=1))

    tfc, tcp, tow = _tiles(fcT), _tiles(cpT), _tiles(owT)
    fc_chunks, cp_chunks, ow_chunks = [], [], []
    for c in range(4):
        b, fc_ix = _blob([(("fc", ml, k), tfc[(k, 2 * c + ml)])
                          for ml in range(2) for k in range(16)])
        fc_chunks.append(b)
    for c in range(4):
        b, cp_ix = _blob([(("cp", ml, k), tcp[(k, 4 * c + ml)])
                          for ml in range(4) for k in range(8)])
        cp_chunks.append(b)
    for c in range(4):
        b, ow_ix = _blob([(("ow", ml, k), tow[(k, 2 * c + ml)])
                          for ml in range(2) for k in range(16)])
        ow_chunks.append(b)
    fc_blob = np.ascontiguousarray(np.concatenate(fc_chunks, axis=1))
    cp_blob = np.ascontiguousarray(np.concatenate(cp_chunks, axis=1))
    ow_blob = np.ascontiguousarray(np.concatenate(ow_chunks, axis=1))

    cols, parts = {}, []

    def add(name, arr):
        cols[name] = sum(p.shape[1] for p in parts)
        parts.append(arr)

    add("b1", _col_pack(inputs["L0_ffn_b1"], 32))
    add("b2", _col_pack(inputs["L0_ffn_b2"], 8))
    add("fc_b", _col_pack(inputs["L0_attn_fc_b"], 8))
    add("cproj_b", _col_pack(inputs["L0_attn_cproj_b"], 16))
    add("conv_b", _col_pack(inputs["L0_attn_conv_b"], 16))
    add("ln_g", _col_pack(inputs["L0_attn_ln_g"], 16))
    add("ln_b", _col_pack(inputs["L0_attn_ln_b"], 16))
    add("eps", np.full((128, 1), EPS, np.float32))
    add("mask", np.zeros((128, 2), np.float32))
    const_blob = np.ascontiguousarray(np.concatenate(parts, axis=1))

    blobs = dict(ffn_blob=ffn_blob, ip_blob=ip_blob, fc_blob=fc_blob,
                 cp_blob=cp_blob, ow_blob=ow_blob)
    ixs = dict(ffn=ffn_ix, ip=ip_ix, fc=fc_ix, cp=cp_ix, ow=ow_ix)
    return blobs, ixs, const_blob, cols


def shard_x(x_in):
    outs = []
    for core in range(NCORES):
        b, chunk = core // 2, core % 2
        t0 = chunk * 2048
        arr = np.zeros((128, 2 * DC * TH), np.float32)
        for half in range(2):
            base = t0 + half * HT
            xs = np.zeros((TH, D), np.float32)
            lo = base - HALO
            if lo >= 0:
                xs[:HALO] = x_in[b, lo:base]
            xs[HALO:] = x_in[b, base:base + HT]
            fm = xs.T.reshape(DC, 128, TH).transpose(1, 0, 2)
            arr[:, half * DC * TH:(half + 1) * DC * TH] = fm.reshape(128, DC * TH)
        outs.append(np.ascontiguousarray(arr))
    return outs


def gather_y(results):
    out = np.zeros((4, 4096, D), np.float32)
    for core in range(NCORES):
        b, chunk = core // 2, core % 2
        a = np.asarray(results[core]["y"], np.float32).reshape(128, DC, 2048)
        out[b, chunk * 2048:(chunk + 1) * 2048] = a.transpose(2, 1, 0).reshape(2048, D)
    return out


class Buf:
    __slots__ = ("name", "w", "r", "psum")

    def __init__(self, name, psum=False):
        self.name, self.w, self.r, self.psum = name, None, {}, psum


class Em:
    ENG_SEM = {"t": "tc", "v": "vc", "s": "sc", "q": "sd", "g": "gd"}

    def __init__(self, nc, es):
        self.nc = nc
        self.eng = {"t": nc.tensor, "v": nc.vector, "s": nc.scalar,
                    "q": nc.sync, "g": nc.gpsimd}
        self.sems, self.cnt, self.wm = {}, {}, {}
        for sn in ("tc", "vc", "sc", "sd", "gd"):
            self.sems[sn] = es.enter_context(nc.semaphore(sn))
            self.cnt[sn] = 0

    def wait(self, en, sn, val):
        if not val or val <= 0:
            return
        assert val <= self.cnt[sn], (en, sn, val, self.cnt[sn])
        if self.wm.get((en, sn), 0) >= val:
            return
        self.eng[en].wait_ge(self.sems[sn], val)
        self.wm[(en, sn)] = val

    def inc(self, en, instr, n=1):
        sn = self.ENG_SEM[en]
        if en in ("q", "g"):
            n = 16
        instr.then_inc(self.sems[sn], n)
        self.cnt[sn] += n
        return self.cnt[sn]

    def r_wait(self, en, buf):
        if buf.w is not None and buf.w[0] != en:
            self.wait(en, self.ENG_SEM[buf.w[0]], buf.w[1])
        if buf.psum and en in ("v", "s"):
            other = "s" if en == "v" else "v"
            if other in buf.r:
                self.wait(en, self.ENG_SEM[other], buf.r[other])

    def w_wait(self, en, buf):
        if buf.w is not None and buf.w[0] != en:
            self.wait(en, self.ENG_SEM[buf.w[0]], buf.w[1])
        for ren, rc in buf.r.items():
            if ren != en:
                self.wait(en, self.ENG_SEM[ren], rc)

    def note_r(self, en, buf, cnt):
        buf.r[en] = max(buf.r.get(en, 0), cnt)

    def note_w(self, en, buf, cnt):
        buf.w = (en, cnt)
        buf.r = {}

    def barrier(self, ens=("t", "v", "s"), dma=False):
        snap = {sn: self.cnt[sn] for sn in ("tc", "vc", "sc")}
        for en in ens:
            for sn, vv in snap.items():
                if self.ENG_SEM[en] != sn:
                    self.wait(en, sn, vv)
            if dma:
                self.wait(en, "sd", self.cnt["sd"])
                self.wait(en, "gd", self.cnt["gd"])


def ap3(base_ap, extra_off, chunk_stride, nch, width):
    apl = [list(p) for p in base_ap.ap]
    return AP(tensor=base_ap.tensor, offset=base_ap.offset + extra_off,
              ap=[apl[0], [chunk_stride, nch], [1, width]])


def bc_ap(psum_ap, reps, width):
    apl = [list(p) for p in psum_ap.ap]
    return AP(tensor=psum_ap.tensor, offset=psum_ap.offset,
              ap=[apl[0], [0, reps], [apl[1][0], width]])


def build_nc(ixs, ccols, nconst):
    nc = bass.Bass("TRN2", target_bir_lowering=False)
    x_h = nc.dram_tensor("x", [128, 2 * DC * TH], F32, kind="ExternalInput").ap()
    ffn_h = nc.dram_tensor("ffn_blob", [128, 160 * 128], F32, kind="ExternalInput").ap()
    ip_h = nc.dram_tensor("ip_blob", [128, NG * 48 * 128], F32, kind="ExternalInput").ap()
    fc_h = nc.dram_tensor("fc_blob", [128, 8 * 16 * 128], F32, kind="ExternalInput").ap()
    cp_h = nc.dram_tensor("cp_blob", [128, 16 * 8 * 128], F32, kind="ExternalInput").ap()
    ow_h = nc.dram_tensor("ow_blob", [128, 8 * 16 * 128], F32, kind="ExternalInput").ap()
    const_h = nc.dram_tensor("const_blob", [128, nconst], F32, kind="ExternalInput").ap()
    onec_h = nc.dram_tensor("onec", [128, 1], F32, kind="ExternalInput").ap()
    oner_h = nc.dram_tensor("oner", [1, 128], F32, kind="ExternalInput").ap()
    y_h = nc.dram_tensor("y", [128, DC * 2048], F32, kind="ExternalOutput").ap()

    with ExitStack() as es:
        E = Em(nc, es)
        t, v, s, q, g = nc.tensor, nc.vector, nc.scalar, nc.sync, nc.gpsimd

        def sbl(name, shape, dt):
            return es.enter_context(nc.sbuf_tensor(name, shape, dt, side="left"))

        consts = sbl("consts_sb", [128, nconst], F32)
        onec = sbl("onec_sb", [128, 1], BF16)
        oner = sbl("oner_sb", [1, 128], BF16)
        ffn_sb = sbl("ffn_sb", [128, 160 * 128], BF16)
        wslot = [sbl(f"wslot{i}", [128, 48 * 128], BF16) for i in range(2)]
        mA = sbl("mA", [1, TH], BF16)
        qA = sbl("qA", [1, TH], BF16)
        mB = sbl("mB", [1, TH], BF16)
        qB = sbl("qB", [1, TH], BF16)
        rstd = sbl("rstd", [1, TH], BF16)
        varb = sbl("varb", [1, TH], F32)
        b2h = sbl("b2h", [128, 8], F32)
        lnb_bf = sbl("lnb_bf", [128, 16], BF16)
        fcb = sbl("fcb", [128, 8], F32)
        tsl = sbl("tsl", [128, 2 * 512], F32)
        xb3 = sbl("xb3", [128, 4 * 512], BF16)
        sq3 = sbl("sq3", [128, 2 * 512], BF16)
        mS = xb3[:, 2 * 512:3 * 512]
        rS = xb3[:, 3 * 512:4 * 512]
        outb = sbl("outb", [128, 2 * 512], F32)

        B = {n: Buf(n) for n in
             ("consts onec oner ffn mA qA mB qB rstd varb b2h lnb fcb "
              "mS rS xcur h x2 xn ph gate hn m1 xb sq xc3 Pb2 Hq2 Qb2 "
              "Pb Hq Qb").split()}
        B_wslot = [Buf("ws0"), Buf("ws1")]
        B_tsl = [Buf("tsl0"), Buf("tsl1")]
        B_xb3 = [Buf(f"xb3{i}") for i in range(2)]
        B_sq3 = [Buf(f"sq3{i}") for i in range(2)]
        B_tail = [Buf(f"tail{i}") for i in range(4)]
        B_outb = [Buf("ob0"), Buf("ob1")]
        banks = [Buf(f"bk{i}", psum=True) for i in range(8)]
        pt = [es.enter_context(nc.psum_tensor(f"pb{i}", [128, 512], F32))
              for i in range(8)]

        def ccol(name, j=0, n=1):
            return consts[:, ccols[name] + j:ccols[name] + j + n]

        def fftile(name, k, m):
            i = ixs["ffn"][(name, k, m)]
            return ffn_sb[:, 128 * i:128 * (i + 1)]

        c = E.inc("q", q.dma_start(consts[:, :], const_h))
        E.note_w("q", B["consts"], c)
        c = E.inc("g", g.dma_start(onec[:, :], onec_h))
        E.note_w("g", B["onec"], c)
        c = E.inc("g", g.dma_start(oner[:, :], oner_h))
        E.note_w("g", B["oner"], c)
        c = E.inc("g", g.dma_start(ffn_sb[:, :], ffn_h))
        E.note_w("g", B["ffn"], c)

        E.r_wait("s", B["consts"])
        c = E.inc("s", s.mul(b2h[:, :], ccol("b2", 0, 8), 0.5))
        E.note_w("s", B["b2h"], c)
        c = E.inc("s", s.copy(lnb_bf[:, :], ccol("ln_b", 0, 16)))
        E.note_w("s", B["lnb"], c)

        sched = []
        for half in range(2):
            sched += [("ip", gi, 6144, ip_h[:, gi * 6144:(gi + 1) * 6144])
                      for gi in range(NG)]
            sched += [("fc", c, 4096, fc_h[:, c * 4096:(c + 1) * 4096])
                      for c in range(4)]
            sched += [("cp", c, 4096, cp_h[:, c * 4096:(c + 1) * 4096])
                      for c in range(4)]
            sched += [("ow", c, 4096, ow_h[:, c * 4096:(c + 1) * 4096])
                      for c in range(4)]
        ws = {"issued": 0, "dma": [None] * len(sched)}

        def ws_issue():
            i = ws["issued"]
            if i >= len(sched):
                return
            _, _, ncols, src = sched[i]
            E.w_wait("g", B_wslot[i % 2])
            c2 = E.inc("g", g.dma_start(wslot[i % 2][:, :ncols], src))
            E.note_w("g", B_wslot[i % 2], c2)
            ws["dma"][i] = c2
            ws["issued"] = i + 1

        def ws_tile(ci, which, key):
            j = ixs[which][key]
            return wslot[ci % 2][:, j * 128:(j + 1) * 128]

        def ws_ensure(ci):
            E.wait("t", "gd", ws["dma"][ci])

        def ws_done(ci):
            E.note_r("t", B_wslot[ci % 2], E.cnt["tc"])
            ws_issue()

        ws_issue()
        ws_issue()

        def bcast_to(src_ap, w, bi, B_src, dst_sb, B_dst):
            bk = banks[bi]
            E.w_wait("t", bk)
            E.r_wait("t", B["oner"])
            E.r_wait("t", B_src)
            ii = t.matmul(pt[bi][:, :w], oner[:, :], src_ap, start=True, stop=True)
            cnt = E.inc("t", ii)
            E.note_w("t", bk, cnt)
            E.note_r("t", B_src, cnt)
            E.r_wait("s", bk)
            E.w_wait("s", B_dst)
            c2 = E.inc("s", s.copy(dst_sb[:, 0:w], pt[bi][:, :w]))
            E.note_r("s", bk, c2)
            E.note_w("s", B_dst, c2)

        def rstd_batch(m_sb, q_sb, B_m, B_q):
            E.r_wait("v", B_m)
            E.w_wait("v", B["varb"])
            c2 = E.inc("v", v.tensor_mul(varb[0:1, :], m_sb[0:1, :], m_sb[0:1, :]))
            E.note_r("v", B_m, c2)
            E.note_w("v", B["varb"], c2)
            E.r_wait("v", B_q)
            c2 = E.inc("v", v.tensor_tensor(varb[0:1, :], q_sb[0:1, :],
                                            varb[0:1, :], ALU.subtract))
            E.note_r("v", B_q, c2)
            E.note_w("v", B["varb"], c2)
            E.r_wait("v", B["consts"])
            c2 = E.inc("v", v.tensor_scalar_add(
                varb[0:1, :], varb[0:1, :],
                consts[0:1, ccols["eps"]:ccols["eps"] + 1]))
            E.note_w("v", B["varb"], c2)
            c2 = E.inc("v", v.reciprocal(varb[0:1, :], varb[0:1, :]))
            E.note_w("v", B["varb"], c2)
            E.r_wait("s", B["varb"])
            E.w_wait("s", B["rstd"])
            c2 = E.inc("s", s.activation(rstd[0:1, :], varb[0:1, :], AF.Sqrt))
            E.note_r("s", B["varb"], c2)
            E.note_w("s", B["rstd"], c2)

        store_last = [0]

        for half in range(2):
            with ExitStack() as hs:
                def rsb(name, shape, dt, stack):
                    return stack.enter_context(
                        nc.sbuf_tensor(f"{name}{half}", shape, dt, side="right"))

                xcur = rsb("xcur", [128, DC * TH], F32, hs)
                h = rsb("h", [128, IC * HT], BF16, hs)
                B_xcur, B_h = B["xcur"], B["h"]

                hap = h[:, :]
                XB, SQ = 0, 4096
                Pb = h[:, 8192:9216]
                Hq = h[:, 9216:13312]
                Qb = h[:, 13312:14336]
                TAIL = 14336
                B_xb, B_sq = B["xb"], B["sq"]
                B_Pb, B_Hq, B_Qb = B["Pb"], B["Hq"], B["Qb"]

                def hfl(base, w):
                    return h[:, base:base + w]

                def xcur_v(tcol, w):
                    return ap3(xcur[:, :], tcol, TH, DC, w)

                E.barrier(("t", "v", "s"), dma=True)
                E.wait("q", "tc", E.cnt["tc"])
                E.wait("q", "vc", E.cnt["vc"])
                E.wait("q", "sc", E.cnt["sc"])
                if half == 0:
                    E.w_wait("q", B_xcur)
                    c = E.inc("q", q.dma_start(
                        xcur[:, :], x_h[:, 0:DC * TH]))
                    E.note_w("q", B_xcur, c)

                rot = [0]

                def nb(held=()):
                    while True:
                        rot[0] = (rot[0] + 1) % 8
                        if rot[0] not in held:
                            return rot[0]

                Pb_c, Qb_c = [Pb], [Qb]
                B_Pb_c, B_Qb_c = [B_Pb], [B_Qb]
                HqD = [[Hq, Hq]]
                B_HqD = [[B_Hq, B_Hq]]

                def ffn_tile(xsrc, B_xsrc, dst_add, w, after_u1=None,
                             avoid=()):
                    for m in range(2):
                        bi = nb(held=avoid)
                        bk = banks[bi]
                        E.w_wait("t", bk)
                        for k in range(8):
                            E.r_wait("t", B_xsrc)
                            ii = t.matmul(pt[bi][:, :w], fftile("U1", k, m),
                                          xsrc(k), start=(k == 0), stop=(k == 7))
                        cnt = E.inc("t", ii)
                        E.note_w("t", bk, cnt)
                        E.note_r("t", B_xsrc, cnt)
                        E.r_wait("s", bk)
                        E.w_wait("s", B_Pb_c[0])
                        c2 = E.inc("s", s.copy(Pb_c[0][:, m * 512:m * 512 + w],
                                               pt[bi][:, :w]))
                        E.note_r("s", bk, c2)
                        E.note_w("s", B_Pb_c[0], c2)
                    if after_u1 is not None:
                        after_u1()
                    u2a = nb(held=avoid)
                    u2b_ = nb(held=(u2a,) + tuple(avoid))
                    u2 = (u2a, u2b_)
                    for mu in range(2):
                        E.w_wait("t", banks[u2[mu]])
                    for qq in range(4):
                        Hqb = HqD[0][qq % 2]
                        B_Hqb = B_HqD[0][qq % 2]
                        for mq in range(8):
                            m = qq * 8 + mq
                            bi = nb(held=u2 + tuple(avoid))
                            bk = banks[bi]
                            E.w_wait("t", bk)
                            for k in range(2):
                                E.r_wait("t", B_Pb_c[0])
                                ii = t.matmul(pt[bi][:, :w], fftile("V1", k, m),
                                              Pb_c[0][:, k * 512:k * 512 + w],
                                              start=(k == 0), stop=(k == 1))
                            cnt = E.inc("t", ii)
                            E.note_w("t", bk, cnt)
                            E.note_r("t", B_Pb_c[0], cnt)
                            E.r_wait("s", bk)
                            E.w_wait("s", B_Hqb)
                            E.r_wait("s", B["consts"])
                            c2 = E.inc("s", s.activation(
                                Hqb[:, mq * 512:mq * 512 + w], pt[bi][:, :w],
                                AF.Gelu_apprx_tanh, bias=ccol("b1", m), scale=1.0))
                            E.note_r("s", bk, c2)
                            E.note_w("s", B_Hqb, c2)
                        for mu in range(2):
                            for kq in range(8):
                                k = qq * 8 + kq
                                E.r_wait("t", B_Hqb)
                                ii = t.matmul(pt[u2[mu]][:, :w], fftile("U2", k, mu),
                                              Hqb[:, kq * 512:kq * 512 + w],
                                              start=(qq == 0 and kq == 0),
                                              stop=(qq == 3 and kq == 7))
                            cnt = E.inc("t", ii)
                            E.note_r("t", B_Hqb, cnt)
                            if qq == 3:
                                E.note_w("t", banks[u2[mu]], cnt)
                    for mu in range(2):
                        bk = banks[u2[mu]]
                        E.r_wait("s", bk)
                        E.w_wait("s", B_Qb_c[0])
                        c2 = E.inc("s", s.copy(Qb_c[0][:, mu * 512:mu * 512 + w],
                                               pt[u2[mu]][:, :w]))
                        E.note_r("s", bk, c2)
                        E.note_w("s", B_Qb_c[0], c2)
                    for m in range(8):
                        bi = nb(held=avoid)
                        bk = banks[bi]
                        E.w_wait("t", bk)
                        for k in range(2):
                            E.r_wait("t", B_Qb_c[0])
                            ii = t.matmul(pt[bi][:, :w], fftile("V2", k, m),
                                          Qb_c[0][:, k * 512:k * 512 + w],
                                          start=(k == 0), stop=(k == 1))
                        cnt = E.inc("t", ii)
                        E.note_w("t", bk, cnt)
                        E.note_r("t", B_Qb_c[0], cnt)
                        dst_add(m, bi, bk)

                for si, (tcol, w) in enumerate(TILES):
                    bm = 4 if si % 2 == 0 else 6
                    bq = bm + 1
                    E.w_wait("t", banks[bm])
                    E.w_wait("t", banks[bq])
                    for cch in range(DC):
                        sl = cch % 2
                        xsl = TAIL + sl * 512
                        qsl = TAIL + 1024 + sl * 512
                        E.r_wait("s", B_xcur)
                        E.w_wait("s", B_tail[sl])
                        c2 = E.inc("s", s.copy(
                            hfl(xsl, w), xcur[:, cch * TH + tcol:cch * TH + tcol + w]))
                        E.note_r("s", B_xcur, c2)
                        E.note_w("s", B_tail[sl], c2)
                        E.r_wait("v", B_tail[sl])
                        E.w_wait("v", B_tail[2 + sl])
                        c2 = E.inc("v", v.tensor_mul(hfl(qsl, w), hfl(xsl, w),
                                                     hfl(xsl, w)))
                        E.note_r("v", B_tail[sl], c2)
                        E.note_w("v", B_tail[2 + sl], c2)
                        E.r_wait("t", B_tail[sl])
                        E.r_wait("t", B["onec"])
                        ii = t.matmul(pt[bm][0:1, :w], onec[:, :], hfl(xsl, w),
                                      start=(cch == 0), stop=(cch == DC - 1))
                        cnt = E.inc("t", ii)
                        E.note_r("t", B_tail[sl], cnt)
                        if cch == DC - 1:
                            E.note_w("t", banks[bm], cnt)
                        E.r_wait("t", B_tail[2 + sl])
                        ii = t.matmul(pt[bq][0:1, :w], onec[:, :], hfl(qsl, w),
                                      start=(cch == 0), stop=(cch == DC - 1))
                        cnt = E.inc("t", ii)
                        E.note_r("t", B_tail[2 + sl], cnt)
                        if cch == DC - 1:
                            E.note_w("t", banks[bq], cnt)
                    for bb, dst, bfd in ((bm, mA, B["mA"]), (bq, qA, B["qA"])):
                        E.r_wait("s", banks[bb])
                        E.w_wait("s", bfd)
                        c2 = E.inc("s", s.activation(dst[0:1, tcol:tcol + w],
                                                     pt[bb][0:1, :w], AF.Copy,
                                                     scale=1.0 / D))
                        E.note_r("s", banks[bb], c2)
                        E.note_w("s", bfd, c2)
                rstd_batch(mA, qA, B["mA"], B["qA"])

                E.barrier(("t", "v", "s"))

                with ExitStack() as p1s:
                    xn = rsb("xn", [128, DC * TH], BF16, p1s)
                    ph = rsb("ph", [128, 2 * (3 + TH)], BF16, p1s)
                    gate = rsb("gate", [128, 2 * 512], BF16, p1s)
                    B_xn, B_ph, B_gate = B["xn"], B["ph"], B["gate"]
                    PHW = 3 + TH
                    def prep_xn1(ti2):
                        tcol2, w2 = TILES[ti2]
                        bcast_to(mA[0:1, tcol2:tcol2 + w2], w2, 6, B["mA"],
                                 mS, B["mS"])
                        bcast_to(rstd[0:1, tcol2:tcol2 + w2], w2, 7, B["rstd"],
                                 rS, B["rS"])
                        E.r_wait("v", B["mS"])
                        E.r_wait("v", B_xcur)
                        E.w_wait("v", B_xb)
                        c4 = E.inc("v", v.tensor_tensor(
                            ap3(hap, XB, 512, DC, w2), xcur_v(tcol2, w2),
                            bc_ap(mS[:, 0:w2], DC, w2), ALU.subtract))
                        E.note_r("v", B["mS"], c4)
                        E.note_r("v", B_xcur, c4)
                        E.note_w("v", B_xb, c4)
                        E.r_wait("v", B["rS"])
                        E.w_wait("v", B_xn)
                        c4 = E.inc("v", v.tensor_tensor(
                            ap3(xn[:, :], tcol2, TH, DC, w2),
                            ap3(hap, XB, 512, DC, w2),
                            bc_ap(rS[:, 0:w2], DC, w2), ALU.mult))
                        E.note_r("v", B["rS"], c4)
                        E.note_r("v", B_xb, c4)
                        E.note_w("v", B_xn, c4)

                    prep_xn1(0)
                    for ti, (tcol, w) in enumerate(TILES):
                        def after_u1(_ti=ti):
                            if _ti + 1 < len(TILES):
                                prep_xn1(_ti + 1)

                        def dst_add(m, bi, bk, _tcol=tcol, _w=w):
                            sl = m % 2
                            E.r_wait("s", bk)
                            E.w_wait("s", B_tsl[sl])
                            c3 = E.inc("s", s.mul(tsl[:, sl * 512:sl * 512 + _w],
                                                  pt[bi][:, :_w], 0.5))
                            E.note_r("s", bk, c3)
                            E.note_w("s", B_tsl[sl], c3)
                            E.r_wait("v", B_tsl[sl])
                            E.r_wait("v", B["b2h"])
                            dstc = xcur[:, m * TH + _tcol:m * TH + _tcol + _w]
                            c3 = E.inc("v", v.scalar_tensor_tensor(
                                dstc, tsl[:, sl * 512:sl * 512 + _w],
                                b2h[:, m:m + 1], dstc, ALU.add, ALU.add))
                            E.note_r("v", B_tsl[sl], c3)
                            E.note_r("v", B["b2h"], c3)
                            E.note_w("v", B_xcur, c3)

                        ffn_tile(lambda k, _t=tcol, _w=w: xn[:, k * TH + _t:k * TH + _t + _w],
                                 B_xn, dst_add, w, after_u1=after_u1,
                                 avoid=(6, 7))

                        E.r_wait("s", B_xcur)
                        E.w_wait("s", B_xb)
                        c2 = E.inc("s", s.copy(ap3(hap, XB, 512, DC, w),
                                               xcur_v(tcol, w)))
                        E.note_r("s", B_xcur, c2)
                        E.note_w("s", B_xb, c2)
                        E.r_wait("v", B_xb)
                        E.w_wait("v", B_sq)
                        c2 = E.inc("v", v.tensor_mul(ap3(hap, SQ, 512, DC, w),
                                                     ap3(hap, XB, 512, DC, w),
                                                     ap3(hap, XB, 512, DC, w)))
                        E.note_r("v", B_xb, c2)
                        E.note_w("v", B_sq, c2)
                        E.r_wait("t", B_xb)
                        E.r_wait("t", B_sq)
                        E.w_wait("t", banks[4])
                        E.r_wait("t", B["onec"])
                        for k in range(DC):
                            ii = t.matmul(pt[4][0:1, :w], onec[:, :], hfl(XB + k * 512, w),
                                          start=(k == 0), stop=(k == DC - 1))
                        cnt = E.inc("t", ii)
                        E.note_w("t", banks[4], cnt)
                        E.note_r("t", B_xb, cnt)
                        E.w_wait("t", banks[5])
                        for k in range(DC):
                            ii = t.matmul(pt[5][0:1, :w], onec[:, :], hfl(SQ + k * 512, w),
                                          start=(k == 0), stop=(k == DC - 1))
                        cnt = E.inc("t", ii)
                        E.note_w("t", banks[5], cnt)
                        E.note_r("t", B_sq, cnt)
                        for bb, dst, bfd in ((4, mB, B["mB"]), (5, qB, B["qB"])):
                            E.r_wait("s", banks[bb])
                            E.w_wait("s", bfd)
                            c2 = E.inc("s", s.activation(dst[0:1, tcol:tcol + w],
                                                         pt[bb][0:1, :w], AF.Copy,
                                                         scale=1.0 / D))
                            E.note_r("s", banks[bb], c2)
                            E.note_w("s", bfd, c2)
                    rstd_batch(mB, qB, B["mB"], B["qB"])

                    for (tcol, w) in TILES:
                        bcast_to(mB[0:1, tcol:tcol + w], w, 6, B["mB"], mS, B["mS"])
                        bcast_to(rstd[0:1, tcol:tcol + w], w, 7, B["rstd"], rS, B["rS"])
                        E.r_wait("v", B["mS"])
                        E.r_wait("v", B_xcur)
                        E.w_wait("v", B_xb)
                        c2 = E.inc("v", v.tensor_tensor(
                            ap3(hap, XB, 512, DC, w), xcur_v(tcol, w),
                            bc_ap(mS[:, 0:w], DC, w), ALU.subtract))
                        E.note_r("v", B["mS"], c2)
                        E.note_r("v", B_xcur, c2)
                        E.note_w("v", B_xb, c2)
                        E.r_wait("v", B["rS"])
                        E.w_wait("v", B_xn)
                        c2 = E.inc("v", v.tensor_tensor(
                            ap3(xn[:, :], tcol, TH, DC, w),
                            ap3(hap, XB, 512, DC, w),
                            bc_ap(rS[:, 0:w], DC, w), ALU.mult))
                        E.note_r("v", B["rS"], c2)
                        E.note_r("v", B_xb, c2)
                        E.note_w("v", B_xn, c2)

                    E.barrier(("v",))

                    HELD = (2, 3, 4, 5)
                    for pr in HELD:
                        E.w_wait("t", banks[pr])
                    pending_stats = []

                    def flush_stats():
                        for fn in pending_stats:
                            fn()
                        pending_stats.clear()

                    for gi in range(NG):
                        ci = half * 20 + gi
                        ws_ensure(ci)
                        for tix, (tcol, w) in enumerate(TILES):
                            for ml in range(2):
                                bi = nb(held=HELD)
                                bk = banks[bi]
                                E.w_wait("t", bk)
                                for k in range(8):
                                    E.r_wait("t", B_xn)
                                    ii = t.matmul(
                                        pt[bi][:, :w],
                                        ws_tile(ci, "ip", ("Wh", k, ml)),
                                        xn[:, k * TH + tcol:k * TH + tcol + w],
                                        start=(k == 0), stop=(k == 7))
                                cnt = E.inc("t", ii)
                                E.note_w("t", bk, cnt)
                                E.note_r("t", B_xn, cnt)
                                if ml == 0:
                                    flush_stats()
                                E.r_wait("s", bk)
                                E.w_wait("s", B_ph)
                                c2 = E.inc("s", s.copy(
                                    ph[:, ml * PHW + 3 + tcol:
                                       ml * PHW + 3 + tcol + w], pt[bi][:, :w]))
                                E.note_r("s", bk, c2)
                                E.note_w("s", B_ph, c2)
                            if tcol == 0:
                                E.r_wait("v", B_ph)
                                E.r_wait("v", B["consts"])
                                mv = ap3(ph[:, :], 32, PHW, 2, 3)
                                c2 = E.inc("v", v.tensor_scalar_mul(
                                    mv, mv, ccol("mask", half, 1)))
                                E.note_r("v", B["consts"], c2)
                                E.note_w("v", B_ph, c2)
                                continue
                            for ml in range(2):
                                bi = nb(held=HELD)
                                bk = banks[bi]
                                E.w_wait("t", bk)
                                for k in range(8):
                                    ii = t.matmul(
                                        pt[bi][:, :w],
                                        ws_tile(ci, "ip", ("Wg", k, ml)),
                                        xn[:, k * TH + tcol:k * TH + tcol + w],
                                        start=(k == 0), stop=(k == 7))
                                cnt = E.inc("t", ii)
                                E.note_w("t", bk, cnt)
                                E.note_r("t", B_xn, cnt)
                                E.r_wait("s", bk)
                                E.w_wait("s", B_gate)
                                c2 = E.inc("s", s.activation(
                                    gate[:, ml * 512:ml * 512 + w],
                                    pt[bi][:, :w], AF.Silu))
                                E.note_r("s", bk, c2)
                                E.note_w("s", B_gate, c2)
                            for mc in range(2):
                                bi = nb(held=HELD)
                                bk = banks[bi]
                                E.w_wait("t", bk)
                                first = True
                                for tap in range(KTAP):
                                    for kc in range(2):
                                        E.r_wait("t", B_ph)
                                        ii = t.matmul(
                                            pt[bi][:, :w],
                                            ws_tile(ci, "ip", ("cv", tap, kc, mc)),
                                            ph[:, kc * PHW + tcol + tap:
                                               kc * PHW + tcol + tap + w],
                                            start=first,
                                            stop=(tap == 3 and kc == 1))
                                        first = False
                                cnt = E.inc("t", ii)
                                E.note_w("t", bk, cnt)
                                E.note_r("t", B_ph, cnt)
                                cch = 2 * gi + mc
                                base = cch * HT + (tcol - HALO)
                                E.r_wait("v", bk)
                                E.r_wait("v", B_gate)
                                E.r_wait("v", B["consts"])
                                E.w_wait("v", B_h)
                                c2 = E.inc("v", v.scalar_tensor_tensor(
                                    h[:, base:base + w], pt[bi][:, :w],
                                    ccol("conv_b", cch, 1),
                                    gate[:, mc * 512:mc * 512 + w],
                                    ALU.add, ALU.mult))
                                E.note_r("v", bk, c2)
                                E.note_r("v", B_gate, c2)
                                E.note_w("v", B_h, c2)
                                si2 = tix - 1
                                bm, bq = (2, 3) if si2 == 0 else (4, 5)
                                gsl = gate[:, mc * 512:mc * 512 + w]
                                E.r_wait("v", B_h)
                                E.w_wait("v", B_gate)
                                c2 = E.inc("v", v.tensor_mul(
                                    gsl, h[:, base:base + w],
                                    h[:, base:base + w]))
                                E.note_r("v", B_h, c2)
                                E.note_w("v", B_gate, c2)

                                def _stats(_bm=bm, _bq=bq, _base=base, _w=w,
                                           _gsl=gsl, _gi=gi, _mc=mc):
                                    E.r_wait("t", B_h)
                                    E.r_wait("t", B["onec"])
                                    ii2 = t.matmul(pt[_bm][0:1, :_w], onec[:, :],
                                                   h[:, _base:_base + _w],
                                                   start=(_gi == 0 and _mc == 0),
                                                   stop=(_gi == NG - 1 and _mc == 1))
                                    cn = E.inc("t", ii2)
                                    E.note_r("t", B_h, cn)
                                    if _gi == NG - 1 and _mc == 1:
                                        E.note_w("t", banks[_bm], cn)
                                    E.r_wait("t", B_gate)
                                    ii2 = t.matmul(pt[_bq][0:1, :_w], onec[:, :],
                                                   _gsl,
                                                   start=(_gi == 0 and _mc == 0),
                                                   stop=(_gi == NG - 1 and _mc == 1))
                                    cn = E.inc("t", ii2)
                                    E.note_r("t", B_gate, cn)
                                    if _gi == NG - 1 and _mc == 1:
                                        E.note_w("t", banks[_bq], cn)
                                pending_stats.append(_stats)
                        ws_done(ci)
                    flush_stats()
                    for si2, (tcol, w) in enumerate(OWN_TILES):
                        bm, bq = (2, 3) if si2 == 0 else (4, 5)
                        for bb, dst, bfd in ((bm, mA, B["mA"]), (bq, qA, B["qA"])):
                            E.r_wait("s", banks[bb])
                            E.w_wait("s", bfd)
                            c2 = E.inc("s", s.activation(
                                dst[0:1, tcol:tcol + w], pt[bb][0:1, :w],
                                AF.Copy, scale=1.0 / I2))
                            E.note_r("s", banks[bb], c2)
                            E.note_w("s", bfd, c2)
                rstd_batch(mA, qA, B["mA"], B["qA"])

                with ExitStack() as p2s:
                    hn = rsb("hn", [128, IC * HT], BF16, p2s)
                    m1 = rsb("m1", [128, DC * HT], BF16, p2s)
                    B_hn, B_m1 = B["hn"], B["m1"]

                    def h_v(t0o, w):
                        return ap3(h[:, :], t0o, HT, IC, w)

                    def hn_v(t0o, w):
                        return ap3(hn[:, :], t0o, HT, IC, w)

                    mS2 = xb3[:, 0:512]
                    rS2 = xb3[:, 512:1024]
                    (t1c, t1w), (t2c, t2w) = OWN_TILES
                    bcast_to(mA[0:1, t1c:t1c + t1w], t1w, 6, B["mA"], mS, B["mS"])
                    bcast_to(rstd[0:1, t1c:t1c + t1w], t1w, 7, B["rstd"], rS, B["rS"])
                    bcast_to(mA[0:1, t2c:t2c + t2w], t2w, 6, B["mA"], mS2, B_xb3[0])
                    bcast_to(rstd[0:1, t2c:t2c + t2w], t2w, 7, B["rstd"], rS2, B_xb3[1])
                    hn_ready = [0] * IC
                    for k in range(IC):
                        for si, (tcol, w) in enumerate(OWN_TILES):
                            t0o = tcol - HALO
                            base = k * HT + t0o
                            msl = mS if si == 0 else mS2
                            rsl = rS if si == 0 else rS2
                            bmsl = B["mS"] if si == 0 else B_xb3[0]
                            brsl = B["rS"] if si == 0 else B_xb3[1]
                            E.r_wait("v", bmsl)
                            E.r_wait("v", B_h)
                            E.w_wait("v", B_hn)
                            c2 = E.inc("v", v.tensor_tensor(
                                hn[:, base:base + w], h[:, base:base + w],
                                msl[:, 0:w], ALU.subtract))
                            E.note_r("v", bmsl, c2)
                            E.note_r("v", B_h, c2)
                            E.note_w("v", B_hn, c2)
                            E.r_wait("v", brsl)
                            E.r_wait("v", B["consts"])
                            c2 = E.inc("v", v.scalar_tensor_tensor(
                                hn[:, base:base + w], hn[:, base:base + w],
                                ccol("ln_g", k, 1), rsl[:, 0:w],
                                ALU.mult, ALU.mult))
                            E.note_r("v", brsl, c2)
                            E.note_w("v", B_hn, c2)
                        hn_ready[k] = E.cnt["vc"]

                    for m in range(8):
                        ci = half * 20 + 8 + m // 2
                        if m % 2 == 0:
                            ws_ensure(ci)
                        if half == 0:
                            bi = nb()
                            bk = banks[bi]
                            E.w_wait("t", bk)
                            E.r_wait("t", B["lnb"])
                            for k in range(16):
                                ii = t.matmul(pt[bi][:, 0:1],
                                              ws_tile(ci, "fc", ("fc", m % 2, k)),
                                              lnb_bf[:, k:k + 1],
                                              start=(k == 0), stop=(k == 15))
                            cnt = E.inc("t", ii)
                            E.note_w("t", bk, cnt)
                            E.r_wait("v", bk)
                            E.w_wait("v", B["fcb"])
                            E.r_wait("v", B["consts"])
                            c2 = E.inc("v", v.tensor_scalar_add(
                                fcb[:, m:m + 1], pt[bi][:, 0:1],
                                ccol("fc_b", m, 1)))
                            E.note_r("v", bk, c2)
                            E.note_w("v", B["fcb"], c2)
                        for (tcol, w) in OWN_TILES:
                            t0o = tcol - HALO
                            bi = nb()
                            bk = banks[bi]
                            E.w_wait("t", bk)
                            for k in range(16):
                                E.wait("t", "vc", hn_ready[k])
                                ii = t.matmul(pt[bi][:, :w],
                                              ws_tile(ci, "fc", ("fc", m % 2, k)),
                                              hn[:, k * HT + t0o:k * HT + t0o + w],
                                              start=(k == 0), stop=(k == 15))
                            cnt = E.inc("t", ii)
                            E.note_w("t", bk, cnt)
                            E.note_r("t", B_hn, cnt)
                            E.r_wait("s", bk)
                            E.w_wait("s", B_m1)
                            E.r_wait("s", B["fcb"])
                            c2 = E.inc("s", s.activation(
                                m1[:, m * HT + t0o:m * HT + t0o + w],
                                pt[bi][:, :w], AF.Silu,
                                bias=fcb[:, m:m + 1], scale=1.0))
                            E.note_r("s", bk, c2)
                            E.note_w("s", B_m1, c2)
                        if m % 2 == 1:
                            ws_done(ci)

                    for m in range(16):
                        ci = half * 20 + 12 + m // 4
                        if m % 4 == 0:
                            ws_ensure(ci)
                        for (tcol, w) in OWN_TILES:
                            t0o = tcol - HALO
                            bi = nb()
                            bk = banks[bi]
                            E.w_wait("t", bk)
                            for k in range(8):
                                E.r_wait("t", B_m1)
                                ii = t.matmul(pt[bi][:, :w],
                                              ws_tile(ci, "cp", ("cp", m % 4, k)),
                                              m1[:, k * HT + t0o:k * HT + t0o + w],
                                              start=(k == 0), stop=(k == 7))
                            cnt = E.inc("t", ii)
                            E.note_w("t", bk, cnt)
                            E.note_r("t", B_m1, cnt)
                            base = m * HT + t0o
                            E.r_wait("v", bk)
                            E.r_wait("v", B["consts"])
                            E.w_wait("v", B_h)
                            c2 = E.inc("v", v.scalar_tensor_tensor(
                                h[:, base:base + w], pt[bi][:, :w],
                                ccol("cproj_b", m, 1), h[:, base:base + w],
                                ALU.add, ALU.add))
                            E.note_r("v", bk, c2)
                            E.note_w("v", B_h, c2)
                        if m % 4 == 3:
                            ws_done(ci)

                x2 = hs.enter_context(
                    nc.sbuf_tensor(f"x2_{half}", [128, DC * HT], F32, side="left"))
                B_x2 = B["x2"]
                stat_banks = {0: (4, 5), 1: (2, 3)}
                mm_banks = {0: (0, 6), 1: (1, 7)}
                for pr in stat_banks.values():
                    for bi in pr:
                        E.w_wait("t", banks[bi])
                pending3 = []

                def flush3():
                    for fn in pending3:
                        fn()
                    pending3.clear()

                for m in range(8):
                    ci = half * 20 + 16 + m // 2
                    if m % 2 == 0:
                        ws_ensure(ci)
                    for si, (tcol, w) in enumerate(OWN_TILES):
                        t0o = tcol - HALO
                        bi = mm_banks[si][m % 2]
                        bk = banks[bi]
                        E.w_wait("t", bk)
                        for k in range(16):
                            E.r_wait("t", B_h)
                            ii = t.matmul(pt[bi][:, :w],
                                          ws_tile(ci, "ow", ("ow", m % 2, k)),
                                          h[:, k * HT + t0o:k * HT + t0o + w],
                                          start=(k == 0), stop=(k == 15))
                        cnt = E.inc("t", ii)
                        E.note_w("t", bk, cnt)
                        E.note_r("t", B_h, cnt)
                        flush3()
                        base = m * HT + t0o
                        E.r_wait("v", bk)
                        E.r_wait("v", B_xcur)
                        E.w_wait("v", B_x2)
                        c2 = E.inc("v", v.tensor_add(
                            x2[:, base:base + w], pt[bi][:, :w],
                            xcur[:, m * TH + tcol:m * TH + tcol + w]))
                        E.note_r("v", bk, c2)
                        E.note_r("v", B_xcur, c2)
                        E.note_w("v", B_x2, c2)
                        sl = si
                        E.r_wait("s", B_x2)
                        E.w_wait("s", B_xb3[sl])
                        c2 = E.inc("s", s.copy(xb3[:, sl * 512:sl * 512 + w],
                                               x2[:, base:base + w]))
                        E.note_r("s", B_x2, c2)
                        E.note_w("s", B_xb3[sl], c2)
                        E.r_wait("v", B_xb3[sl])
                        E.w_wait("v", B_sq3[sl])
                        c2 = E.inc("v", v.tensor_mul(sq3[:, sl * 512:sl * 512 + w],
                                                     xb3[:, sl * 512:sl * 512 + w],
                                                     xb3[:, sl * 512:sl * 512 + w]))
                        E.note_r("v", B_xb3[sl], c2)
                        E.note_w("v", B_sq3[sl], c2)
                        bmi, bqi = stat_banks[si]

                        def _st3(_bmi=bmi, _bqi=bqi, _sl=sl, _m=m, _w=w):
                            E.r_wait("t", B_xb3[_sl])
                            ii2 = t.matmul(pt[_bmi][0:1, :_w], onec[:, :],
                                           xb3[:, _sl * 512:_sl * 512 + _w],
                                           start=(_m == 0), stop=(_m == 7))
                            cn = E.inc("t", ii2)
                            E.note_r("t", B_xb3[_sl], cn)
                            if _m == 7:
                                E.note_w("t", banks[_bmi], cn)
                            E.r_wait("t", B_sq3[_sl])
                            ii2 = t.matmul(pt[_bqi][0:1, :_w], onec[:, :],
                                           sq3[:, _sl * 512:_sl * 512 + _w],
                                           start=(_m == 0), stop=(_m == 7))
                            cn = E.inc("t", ii2)
                            E.note_r("t", B_sq3[_sl], cn)
                            if _m == 7:
                                E.note_w("t", banks[_bqi], cn)
                        pending3.append(_st3)
                    if m % 2 == 1:
                        ws_done(ci)
                flush3()
                if half == 0:
                    E.w_wait("q", B_xcur)
                    c = E.inc("q", q.dma_start(
                        xcur[:, :], x_h[:, DC * TH:2 * DC * TH]))
                    E.note_w("q", B_xcur, c)
                for si, (tcol, w) in enumerate(OWN_TILES):
                    bmi, bqi = stat_banks[si]
                    for bb, dst, bfd in ((bmi, mA, B["mA"]), (bqi, qA, B["qA"])):
                        E.r_wait("s", banks[bb])
                        E.w_wait("s", bfd)
                        c2 = E.inc("s", s.activation(dst[0:1, tcol:tcol + w],
                                                     pt[bb][0:1, :w], AF.Copy,
                                                     scale=1.0 / D))
                        E.note_r("s", banks[bb], c2)
                        E.note_w("s", bfd, c2)
                rstd_batch(mA, qA, B["mA"], B["qA"])

                xc3 = h[:, 4096:8192]
                HqB2 = h[:, 0:4096]
                HqD[0] = [Hq, HqB2]
                B_HqD[0] = [B_Hq, B["Hq2"]]
                Pb_c[0], Qb_c[0] = Pb, Qb
                B_xc3 = B["xc3"]
                E.barrier(("v", "s"))

                for si, (tcol, w) in enumerate(OWN_TILES):
                    t0o = tcol - HALO
                    bcast_to(mA[0:1, tcol:tcol + w], w, 6, B["mA"], mS, B["mS"])
                    bcast_to(rstd[0:1, tcol:tcol + w], w, 7, B["rstd"], rS, B["rS"])
                    E.r_wait("v", B["mS"])
                    E.r_wait("v", B_x2)
                    E.w_wait("v", B_xc3)
                    xc3d = ap3(xc3, 0, 512, DC, w)
                    c2 = E.inc("v", v.tensor_tensor(
                        xc3d, ap3(x2[:, :], t0o, HT, DC, w),
                        bc_ap(mS[:, 0:w], DC, w), ALU.subtract))
                    E.note_r("v", B["mS"], c2)
                    E.note_r("v", B_x2, c2)
                    E.note_w("v", B_xc3, c2)
                    E.r_wait("v", B["rS"])
                    c2 = E.inc("v", v.tensor_tensor(
                        xc3d, xc3d, bc_ap(rS[:, 0:w], DC, w), ALU.mult))
                    E.note_r("v", B["rS"], c2)
                    E.note_w("v", B_xc3, c2)

                    def dst_add2(m, bi, bk, _si=si, _t0o=t0o, _tcol=tcol, _w=w):
                        sl = m % 2
                        E.r_wait("s", bk)
                        E.w_wait("s", B_tsl[sl])
                        c3 = E.inc("s", s.mul(tsl[:, sl * 512:sl * 512 + _w],
                                              pt[bi][:, :_w], 0.5))
                        E.note_r("s", bk, c3)
                        E.note_w("s", B_tsl[sl], c3)
                        ob = B_outb[sl]
                        E.r_wait("v", B_tsl[sl])
                        E.r_wait("v", B_x2)
                        E.r_wait("v", B["b2h"])
                        E.w_wait("v", ob)
                        c3 = E.inc("v", v.scalar_tensor_tensor(
                            outb[:, sl * 512:sl * 512 + _w],
                            tsl[:, sl * 512:sl * 512 + _w],
                            b2h[:, m:m + 1],
                            x2[:, m * HT + _t0o:m * HT + _t0o + _w],
                            ALU.add, ALU.add))
                        E.note_r("v", B_tsl[sl], c3)
                        E.note_r("v", B_x2, c3)
                        E.note_w("v", ob, c3)
                        E.r_wait("q", ob)
                        c3 = E.inc("q", q.dma_start(
                            y_h[:, m * 2048 + half * HT + _t0o:
                                m * 2048 + half * HT + _t0o + _w],
                            outb[:, sl * 512:sl * 512 + _w]))
                        E.note_r("q", ob, c3)
                        store_last[0] = c3

                    ffn_tile(lambda k: xc3[:, k * 512:k * 512 + w],
                             B_xc3, dst_add2, w, avoid=(6, 7))

        E.wait("q", "sd", store_last[0])
    return nc


def kernel(**inputs):
    blobs, ixs, const_blob, ccols = build_host(inputs)
    x_in = np.asarray(inputs["L0_attn_hidden_states"], np.float32)
    xs = shard_x(x_in)
    nc = build_nc(ixs, ccols, const_blob.shape[1])
    onec = np.ones((128, 1), np.float32)
    oner = np.ones((1, 128), np.float32)
    in_maps = []
    for core in range(NCORES):
        chunk = core % 2
        cb = const_blob.copy()
        cb[:, ccols["mask"] + 0] = 1.0 if chunk == 1 else 0.0
        cb[:, ccols["mask"] + 1] = 1.0
        in_maps.append(dict(x=xs[core], const_blob=cb, onec=onec, oner=oner,
                            **blobs))
    res = run_bass_kernel_spmd(nc, in_maps, core_ids=list(range(NCORES)))
    return gather_y(res.results)


# version 14
# speedup vs baseline: 1.1070x; 1.0350x over previous
import numpy as np
import ml_dtypes
from contextlib import ExitStack

import concourse.bass as bass
import concourse.mybir as mybir
from concourse.ap import AP
from concourse.bass_utils import run_bass_kernel_spmd

BF16 = mybir.dt.bfloat16
F32 = mybir.dt.float32
AF = mybir.ActivationFunctionType
ALU = mybir.AluOpType

D = 1024
I2 = 2048
DFF = 4096
NG = 8
KTAP = 4
HALO = 32
HT = 1024
TH = HALO + HT
NCORES = 8
EPS = 1e-5
TILES = [(0, HALO), (HALO, 512), (HALO + 512, 512)]
OWN_TILES = TILES[1:]
DC, IC = 8, 16


def _tiles(WT):
    return {(k, m): WT[128 * k:128 * (k + 1), 128 * m:128 * (m + 1)]
            for k in range(WT.shape[0] // 128) for m in range(WT.shape[1] // 128)}


def _blob(entries):
    ix, arrs = {}, []
    for i, (key, a) in enumerate(entries):
        ix[key] = i
        arrs.append(np.asarray(a, np.float32))
    return np.ascontiguousarray(np.concatenate(arrs, axis=1)), ix


def _col_pack(vec, nch):
    return np.ascontiguousarray(np.asarray(vec, np.float32).reshape(nch, 128).T)


def build_host(inputs):
    f32 = lambda k: np.ascontiguousarray(np.asarray(inputs[k], np.float32))
    ipw = f32("L0_attn_in_proj_w")
    WhT = np.ascontiguousarray(ipw[:I2, :].T)
    WgT = np.ascontiguousarray(ipw[I2:, :].T)
    cw = f32("L0_attn_conv_w")
    fcT = np.ascontiguousarray(f32("L0_attn_fc_w").T)
    cpT = np.ascontiguousarray(f32("L0_attn_cproj_w").T)
    owT = np.ascontiguousarray(f32("L0_attn_out_w").T)

    tU1, tV1 = _tiles(f32("L0_ffn_U1")), _tiles(f32("L0_ffn_V1"))
    tU2, tV2 = _tiles(f32("L0_ffn_U2")), _tiles(f32("L0_ffn_V2"))
    ents = ([(("U1", k, m), tU1[(k, m)]) for m in range(2) for k in range(8)]
            + [(("V1", k, m), tV1[(k, m)]) for m in range(32) for k in range(2)]
            + [(("U2", k, m), tU2[(k, m)]) for m in range(2) for k in range(32)]
            + [(("V2", k, m), tV2[(k, m)]) for m in range(8) for k in range(2)])
    ffn_blob, ffn_ix = _blob(ents)

    tWh, tWg = _tiles(WhT), _tiles(WgT)
    ip_chunks, ip_ix = [], None
    for g in range(NG):
        ents = ([(("Wh", k, ml), tWh[(k, 2 * g + ml)])
                 for ml in range(2) for k in range(8)]
                + [(("Wg", k, ml), tWg[(k, 2 * g + ml)])
                   for ml in range(2) for k in range(8)])
        for mc in range(2):
            for tap in range(KTAP):
                CT = np.ascontiguousarray(cw[256 * g:256 * (g + 1), :, tap].T)
                for kc in range(2):
                    ents.append((("cv", tap, kc, mc),
                                 CT[128 * kc:128 * (kc + 1),
                                    128 * mc:128 * (mc + 1)]))
        blob, ip_ix = _blob(ents)
        ip_chunks.append(blob)
    ip_blob = np.ascontiguousarray(np.concatenate(ip_chunks, axis=1))

    tfc, tcp, tow = _tiles(fcT), _tiles(cpT), _tiles(owT)
    fc_chunks, cp_chunks, ow_chunks = [], [], []
    for c in range(4):
        b, fc_ix = _blob([(("fc", ml, k), tfc[(k, 2 * c + ml)])
                          for ml in range(2) for k in range(16)])
        fc_chunks.append(b)
    for c in range(4):
        b, cp_ix = _blob([(("cp", ml, k), tcp[(k, 4 * c + ml)])
                          for ml in range(4) for k in range(8)])
        cp_chunks.append(b)
    for c in range(4):
        b, ow_ix = _blob([(("ow", ml, k), tow[(k, 2 * c + ml)])
                          for ml in range(2) for k in range(16)])
        ow_chunks.append(b)
    fc_blob = np.ascontiguousarray(np.concatenate(fc_chunks, axis=1))
    cp_blob = np.ascontiguousarray(np.concatenate(cp_chunks, axis=1))
    ow_blob = np.ascontiguousarray(np.concatenate(ow_chunks, axis=1))

    cols, parts = {}, []

    def add(name, arr):
        cols[name] = sum(p.shape[1] for p in parts)
        parts.append(arr)

    add("b1", _col_pack(inputs["L0_ffn_b1"], 32))
    add("b2", _col_pack(inputs["L0_ffn_b2"], 8))
    add("fc_b", _col_pack(inputs["L0_attn_fc_b"], 8))
    add("cproj_b", _col_pack(inputs["L0_attn_cproj_b"], 16))
    add("conv_b", _col_pack(inputs["L0_attn_conv_b"], 16))
    add("ln_g", _col_pack(inputs["L0_attn_ln_g"], 16))
    add("ln_b", _col_pack(inputs["L0_attn_ln_b"], 16))
    add("eps", np.full((128, 1), EPS, np.float32))
    add("mask", np.zeros((128, 2), np.float32))
    const_blob = np.ascontiguousarray(np.concatenate(parts, axis=1))

    blobs = dict(ffn_blob=ffn_blob, ip_blob=ip_blob, fc_blob=fc_blob,
                 cp_blob=cp_blob, ow_blob=ow_blob)
    ixs = dict(ffn=ffn_ix, ip=ip_ix, fc=fc_ix, cp=cp_ix, ow=ow_ix)
    return blobs, ixs, const_blob, cols


def shard_x(x_in):
    outs = []
    for core in range(NCORES):
        b, chunk = core // 2, core % 2
        t0 = chunk * 2048
        arr = np.zeros((128, 2 * DC * TH), np.float32)
        for half in range(2):
            base = t0 + half * HT
            xs = np.zeros((TH, D), np.float32)
            lo = base - HALO
            if lo >= 0:
                xs[:HALO] = x_in[b, lo:base]
            xs[HALO:] = x_in[b, base:base + HT]
            fm = xs.T.reshape(DC, 128, TH).transpose(1, 0, 2)
            arr[:, half * DC * TH:(half + 1) * DC * TH] = fm.reshape(128, DC * TH)
        outs.append(np.ascontiguousarray(arr))
    return outs


def gather_y(results):
    out = np.zeros((4, 4096, D), np.float32)
    for core in range(NCORES):
        b, chunk = core // 2, core % 2
        a = np.asarray(results[core]["y"], np.float32).reshape(128, DC, 2048)
        out[b, chunk * 2048:(chunk + 1) * 2048] = a.transpose(2, 1, 0).reshape(2048, D)
    return out


class Buf:
    __slots__ = ("name", "w", "r", "psum")

    def __init__(self, name, psum=False):
        self.name, self.w, self.r, self.psum = name, None, {}, psum


class Em:
    ENG_SEM = {"t": "tc", "v": "vc", "s": "sc", "q": "sd", "g": "gd"}

    def __init__(self, nc, es):
        self.nc = nc
        self.eng = {"t": nc.tensor, "v": nc.vector, "s": nc.scalar,
                    "q": nc.sync, "g": nc.gpsimd}
        self.sems, self.cnt, self.wm = {}, {}, {}
        for sn in ("tc", "vc", "sc", "sd", "gd"):
            self.sems[sn] = es.enter_context(nc.semaphore(sn))
            self.cnt[sn] = 0

    def wait(self, en, sn, val):
        if not val or val <= 0:
            return
        assert val <= self.cnt[sn], (en, sn, val, self.cnt[sn])
        if self.wm.get((en, sn), 0) >= val:
            return
        self.eng[en].wait_ge(self.sems[sn], val)
        self.wm[(en, sn)] = val

    def inc(self, en, instr, n=1):
        sn = self.ENG_SEM[en]
        if en in ("q", "g"):
            n = 16
        instr.then_inc(self.sems[sn], n)
        self.cnt[sn] += n
        return self.cnt[sn]

    def r_wait(self, en, buf):
        if buf.w is not None and buf.w[0] != en:
            self.wait(en, self.ENG_SEM[buf.w[0]], buf.w[1])
        if buf.psum and en in ("v", "s"):
            other = "s" if en == "v" else "v"
            if other in buf.r:
                self.wait(en, self.ENG_SEM[other], buf.r[other])

    def w_wait(self, en, buf):
        if buf.w is not None and buf.w[0] != en:
            self.wait(en, self.ENG_SEM[buf.w[0]], buf.w[1])
        for ren, rc in buf.r.items():
            if ren != en:
                self.wait(en, self.ENG_SEM[ren], rc)

    def note_r(self, en, buf, cnt):
        buf.r[en] = max(buf.r.get(en, 0), cnt)

    def note_w(self, en, buf, cnt):
        buf.w = (en, cnt)
        buf.r = {}

    def barrier(self, ens=("t", "v", "s"), dma=False):
        snap = {sn: self.cnt[sn] for sn in ("tc", "vc", "sc")}
        for en in ens:
            for sn, vv in snap.items():
                if self.ENG_SEM[en] != sn:
                    self.wait(en, sn, vv)
            if dma:
                self.wait(en, "sd", self.cnt["sd"])
                self.wait(en, "gd", self.cnt["gd"])


def ap3(base_ap, extra_off, chunk_stride, nch, width):
    apl = [list(p) for p in base_ap.ap]
    return AP(tensor=base_ap.tensor, offset=base_ap.offset + extra_off,
              ap=[apl[0], [chunk_stride, nch], [1, width]])


def bc_ap(psum_ap, reps, width):
    apl = [list(p) for p in psum_ap.ap]
    return AP(tensor=psum_ap.tensor, offset=psum_ap.offset,
              ap=[apl[0], [0, reps], [apl[1][0], width]])


def build_nc(ixs, ccols, nconst):
    nc = bass.Bass("TRN2", target_bir_lowering=False)
    x_h = nc.dram_tensor("x", [128, 2 * DC * TH], F32, kind="ExternalInput").ap()
    ffn_h = nc.dram_tensor("ffn_blob", [128, 160 * 128], F32, kind="ExternalInput").ap()
    ip_h = nc.dram_tensor("ip_blob", [128, NG * 48 * 128], F32, kind="ExternalInput").ap()
    fc_h = nc.dram_tensor("fc_blob", [128, 8 * 16 * 128], F32, kind="ExternalInput").ap()
    cp_h = nc.dram_tensor("cp_blob", [128, 16 * 8 * 128], F32, kind="ExternalInput").ap()
    ow_h = nc.dram_tensor("ow_blob", [128, 8 * 16 * 128], F32, kind="ExternalInput").ap()
    const_h = nc.dram_tensor("const_blob", [128, nconst], F32, kind="ExternalInput").ap()
    onec_h = nc.dram_tensor("onec", [128, 1], F32, kind="ExternalInput").ap()
    oner_h = nc.dram_tensor("oner", [1, 128], F32, kind="ExternalInput").ap()
    y_h = nc.dram_tensor("y", [128, DC * 2048], F32, kind="ExternalOutput").ap()

    with ExitStack() as es:
        E = Em(nc, es)
        t, v, s, q, g = nc.tensor, nc.vector, nc.scalar, nc.sync, nc.gpsimd

        def sbl(name, shape, dt):
            return es.enter_context(nc.sbuf_tensor(name, shape, dt, side="left"))

        consts = sbl("consts_sb", [128, nconst], F32)
        onec = sbl("onec_sb", [128, 1], BF16)
        oner = sbl("oner_sb", [1, 128], BF16)
        ffn_sb = sbl("ffn_sb", [128, 160 * 128], BF16)
        wslot = [sbl(f"wslot{i}", [128, 48 * 128], BF16) for i in range(2)]
        mA = sbl("mA", [1, TH], BF16)
        qA = sbl("qA", [1, TH], BF16)
        mB = sbl("mB", [1, TH], BF16)
        qB = qA
        rstd = sbl("rstd", [1, TH], BF16)
        varb = sbl("varb", [1, TH], F32)
        b2h = sbl("b2h", [128, 8], F32)
        lnb_bf = sbl("lnb_bf", [128, 16], BF16)
        fcb = sbl("fcb", [128, 8], F32)
        tsl = sbl("tsl", [128, 2 * 512], F32)
        xb3 = sbl("xb3", [128, 4 * 512], BF16)
        sq3 = sbl("sq3", [128, 2 * 512], BF16)
        mS = xb3[:, 2 * 512:3 * 512]
        rS = xb3[:, 3 * 512:4 * 512]
        mrb = sbl("mrb", [1, TH], BF16)
        outb = sbl("outb", [128, 2 * 512], F32)

        B = {n: Buf(n) for n in
             ("consts onec oner ffn mA qA mB rstd varb b2h lnb fcb "
              "mS rS xcur h x2 xn ph gate hn m1 xb sq xc3 Pb2 Hq2 Qb2 "
              "Pb Hq Qb").split()}
        B_wslot = [Buf("ws0"), Buf("ws1")]
        B_tsl = [Buf("tsl0"), Buf("tsl1")]
        B_mrb = Buf("mrb")
        B_xb3 = [Buf(f"xb3{i}") for i in range(2)]
        B_sq3 = [Buf(f"sq3{i}") for i in range(2)]
        B_tail = [Buf(f"tail{i}") for i in range(4)]
        B_outb = [Buf("ob0"), Buf("ob1")]
        banks = [Buf(f"bk{i}", psum=True) for i in range(8)]
        pt = [es.enter_context(nc.psum_tensor(f"pb{i}", [128, 512], F32))
              for i in range(8)]

        def ccol(name, j=0, n=1):
            return consts[:, ccols[name] + j:ccols[name] + j + n]

        def fftile(name, k, m):
            i = ixs["ffn"][(name, k, m)]
            return ffn_sb[:, 128 * i:128 * (i + 1)]

        c = E.inc("q", q.dma_start(consts[:, :], const_h))
        E.note_w("q", B["consts"], c)
        c = E.inc("g", g.dma_start(onec[:, :], onec_h))
        E.note_w("g", B["onec"], c)
        c = E.inc("g", g.dma_start(oner[:, :], oner_h))
        E.note_w("g", B["oner"], c)
        c = E.inc("g", g.dma_start(ffn_sb[:, :], ffn_h))
        E.note_w("g", B["ffn"], c)

        E.r_wait("s", B["consts"])
        c = E.inc("s", s.mul(b2h[:, :], ccol("b2", 0, 8), 0.5))
        E.note_w("s", B["b2h"], c)
        c = E.inc("s", s.copy(lnb_bf[:, :], ccol("ln_b", 0, 16)))
        E.note_w("s", B["lnb"], c)

        sched = []
        for half in range(2):
            sched += [("ip", gi, 6144, ip_h[:, gi * 6144:(gi + 1) * 6144])
                      for gi in range(NG)]
            sched += [("fc", c, 4096, fc_h[:, c * 4096:(c + 1) * 4096])
                      for c in range(4)]
            sched += [("cp", c, 4096, cp_h[:, c * 4096:(c + 1) * 4096])
                      for c in range(4)]
            sched += [("ow", c, 4096, ow_h[:, c * 4096:(c + 1) * 4096])
                      for c in range(4)]
        ws = {"issued": 0, "dma": [None] * len(sched)}

        def ws_issue():
            i = ws["issued"]
            if i >= len(sched):
                return
            _, _, ncols, src = sched[i]
            E.w_wait("g", B_wslot[i % 2])
            c2 = E.inc("g", g.dma_start(wslot[i % 2][:, :ncols], src))
            E.note_w("g", B_wslot[i % 2], c2)
            ws["dma"][i] = c2
            ws["issued"] = i + 1

        def ws_tile(ci, which, key):
            j = ixs[which][key]
            return wslot[ci % 2][:, j * 128:(j + 1) * 128]

        def ws_ensure(ci):
            E.wait("t", "gd", ws["dma"][ci])

        def ws_done(ci):
            E.note_r("t", B_wslot[ci % 2], E.cnt["tc"])
            ws_issue()

        ws_issue()
        ws_issue()

        def bcast_to(src_ap, w, bi, B_src, dst_sb, B_dst):
            bk = banks[bi]
            E.w_wait("t", bk)
            E.r_wait("t", B["oner"])
            E.r_wait("t", B_src)
            ii = t.matmul(pt[bi][:, :w], oner[:, :], src_ap, start=True, stop=True)
            cnt = E.inc("t", ii)
            E.note_w("t", bk, cnt)
            E.note_r("t", B_src, cnt)
            E.r_wait("s", bk)
            E.w_wait("s", B_dst)
            c2 = E.inc("s", s.copy(dst_sb[:, 0:w], pt[bi][:, :w]))
            E.note_r("s", bk, c2)
            E.note_w("s", B_dst, c2)

        def rstd_batch(m_sb, q_sb, B_m, B_q):
            E.r_wait("v", B_m)
            E.w_wait("v", B["varb"])
            c2 = E.inc("v", v.tensor_mul(varb[0:1, :], m_sb[0:1, :], m_sb[0:1, :]))
            E.note_r("v", B_m, c2)
            E.note_w("v", B["varb"], c2)
            E.r_wait("v", B_q)
            c2 = E.inc("v", v.tensor_tensor(varb[0:1, :], q_sb[0:1, :],
                                            varb[0:1, :], ALU.subtract))
            E.note_r("v", B_q, c2)
            E.note_w("v", B["varb"], c2)
            E.r_wait("v", B["consts"])
            c2 = E.inc("v", v.tensor_scalar_add(
                varb[0:1, :], varb[0:1, :],
                consts[0:1, ccols["eps"]:ccols["eps"] + 1]))
            E.note_w("v", B["varb"], c2)
            c2 = E.inc("v", v.reciprocal(varb[0:1, :], varb[0:1, :]))
            E.note_w("v", B["varb"], c2)
            E.r_wait("s", B["varb"])
            E.w_wait("s", B["rstd"])
            c2 = E.inc("s", s.activation(rstd[0:1, :], varb[0:1, :], AF.Sqrt))
            E.note_r("s", B["varb"], c2)
            E.note_w("s", B["rstd"], c2)

        store_last = [0]

        for half in range(2):
            with ExitStack() as hs:
                def rsb(name, shape, dt, stack):
                    return stack.enter_context(
                        nc.sbuf_tensor(f"{name}{half}", shape, dt, side="right"))

                xcur = rsb("xcur", [128, DC * TH], F32, hs)
                h = rsb("h", [128, IC * HT], BF16, hs)
                B_xcur, B_h = B["xcur"], B["h"]

                hap = h[:, :]
                XB, SQ = 0, 4096
                Pb = h[:, 8192:9216]
                Hq = h[:, 9216:13312]
                Qb = h[:, 13312:14336]
                TAIL = 14336
                B_xb, B_sq = B["xb"], B["sq"]
                B_Pb, B_Hq, B_Qb = B["Pb"], B["Hq"], B["Qb"]

                def hfl(base, w):
                    return h[:, base:base + w]

                def xcur_v(tcol, w):
                    return ap3(xcur[:, :], tcol, TH, DC, w)

                E.barrier(("t", "v", "s"), dma=True)
                E.wait("q", "tc", E.cnt["tc"])
                E.wait("q", "vc", E.cnt["vc"])
                E.wait("q", "sc", E.cnt["sc"])
                if half == 0:
                    E.w_wait("q", B_xcur)
                    c = E.inc("q", q.dma_start(
                        xcur[:, :], x_h[:, 0:DC * TH]))
                    E.note_w("q", B_xcur, c)

                rot = [0]

                def nb(held=()):
                    while True:
                        rot[0] = (rot[0] + 1) % 8
                        if rot[0] not in held:
                            return rot[0]

                Pb_c, Qb_c = [Pb], [Qb]
                B_Pb_c, B_Qb_c = [B_Pb], [B_Qb]
                SQr = h[:, 4096:8192]
                HqD = [[Hq, SQr]]
                B_HqD = [[B_Hq, B_sq]]

                def ffn_tile(xsrc, B_xsrc, dst_add, w, after_u1=None,
                             avoid=()):
                    for m in range(2):
                        bi = nb(held=avoid)
                        bk = banks[bi]
                        E.w_wait("t", bk)
                        for k in range(8):
                            E.r_wait("t", B_xsrc)
                            ii = t.matmul(pt[bi][:, :w], fftile("U1", k, m),
                                          xsrc(k), start=(k == 0), stop=(k == 7))
                        cnt = E.inc("t", ii)
                        E.note_w("t", bk, cnt)
                        E.note_r("t", B_xsrc, cnt)
                        E.r_wait("s", bk)
                        E.w_wait("s", B_Pb_c[0])
                        c2 = E.inc("s", s.copy(Pb_c[0][:, m * 512:m * 512 + w],
                                               pt[bi][:, :w]))
                        E.note_r("s", bk, c2)
                        E.note_w("s", B_Pb_c[0], c2)
                    if after_u1 is not None:
                        after_u1()
                    u2a = nb(held=avoid)
                    u2b_ = nb(held=(u2a,) + tuple(avoid))
                    u2 = (u2a, u2b_)
                    for mu in range(2):
                        E.w_wait("t", banks[u2[mu]])
                    for qq in range(4):
                        Hqb = HqD[0][qq % 2]
                        B_Hqb = B_HqD[0][qq % 2]
                        for mq in range(8):
                            m = qq * 8 + mq
                            bi = nb(held=u2 + tuple(avoid))
                            bk = banks[bi]
                            E.w_wait("t", bk)
                            for k in range(2):
                                E.r_wait("t", B_Pb_c[0])
                                ii = t.matmul(pt[bi][:, :w], fftile("V1", k, m),
                                              Pb_c[0][:, k * 512:k * 512 + w],
                                              start=(k == 0), stop=(k == 1))
                            cnt = E.inc("t", ii)
                            E.note_w("t", bk, cnt)
                            E.note_r("t", B_Pb_c[0], cnt)
                            E.r_wait("s", bk)
                            E.w_wait("s", B_Hqb)
                            E.r_wait("s", B["consts"])
                            c2 = E.inc("s", s.activation(
                                Hqb[:, mq * 512:mq * 512 + w], pt[bi][:, :w],
                                AF.Gelu_apprx_tanh, bias=ccol("b1", m), scale=1.0))
                            E.note_r("s", bk, c2)
                            E.note_w("s", B_Hqb, c2)
                        for mu in range(2):
                            for kq in range(8):
                                k = qq * 8 + kq
                                E.r_wait("t", B_Hqb)
                                ii = t.matmul(pt[u2[mu]][:, :w], fftile("U2", k, mu),
                                              Hqb[:, kq * 512:kq * 512 + w],
                                              start=(qq == 0 and kq == 0),
                                              stop=(qq == 3 and kq == 7))
                            cnt = E.inc("t", ii)
                            E.note_r("t", B_Hqb, cnt)
                            if qq == 3:
                                E.note_w("t", banks[u2[mu]], cnt)
                    for mu in range(2):
                        bk = banks[u2[mu]]
                        E.r_wait("s", bk)
                        E.w_wait("s", B_Qb_c[0])
                        c2 = E.inc("s", s.copy(Qb_c[0][:, mu * 512:mu * 512 + w],
                                               pt[u2[mu]][:, :w]))
                        E.note_r("s", bk, c2)
                        E.note_w("s", B_Qb_c[0], c2)
                    for m in range(8):
                        bi = nb(held=avoid)
                        bk = banks[bi]
                        E.w_wait("t", bk)
                        for k in range(2):
                            E.r_wait("t", B_Qb_c[0])
                            ii = t.matmul(pt[bi][:, :w], fftile("V2", k, m),
                                          Qb_c[0][:, k * 512:k * 512 + w],
                                          start=(k == 0), stop=(k == 1))
                        cnt = E.inc("t", ii)
                        E.note_w("t", bk, cnt)
                        E.note_r("t", B_Qb_c[0], cnt)
                        dst_add(m, bi, bk)

                for si, (tcol, w) in enumerate(TILES):
                    bm = 4 if si % 2 == 0 else 6
                    bq = bm + 1
                    E.w_wait("t", banks[bm])
                    E.w_wait("t", banks[bq])
                    for cch in range(DC):
                        sl = cch % 2
                        xsl = TAIL + sl * 512
                        qsl = TAIL + 1024 + sl * 512
                        E.r_wait("s", B_xcur)
                        E.w_wait("s", B_tail[sl])
                        c2 = E.inc("s", s.copy(
                            hfl(xsl, w), xcur[:, cch * TH + tcol:cch * TH + tcol + w]))
                        E.note_r("s", B_xcur, c2)
                        E.note_w("s", B_tail[sl], c2)
                        E.r_wait("v", B_tail[sl])
                        E.w_wait("v", B_tail[2 + sl])
                        c2 = E.inc("v", v.tensor_mul(hfl(qsl, w), hfl(xsl, w),
                                                     hfl(xsl, w)))
                        E.note_r("v", B_tail[sl], c2)
                        E.note_w("v", B_tail[2 + sl], c2)
                        E.r_wait("t", B_tail[sl])
                        E.r_wait("t", B["onec"])
                        ii = t.matmul(pt[bm][0:1, :w], onec[:, :], hfl(xsl, w),
                                      start=(cch == 0), stop=(cch == DC - 1))
                        cnt = E.inc("t", ii)
                        E.note_r("t", B_tail[sl], cnt)
                        if cch == DC - 1:
                            E.note_w("t", banks[bm], cnt)
                        E.r_wait("t", B_tail[2 + sl])
                        ii = t.matmul(pt[bq][0:1, :w], onec[:, :], hfl(qsl, w),
                                      start=(cch == 0), stop=(cch == DC - 1))
                        cnt = E.inc("t", ii)
                        E.note_r("t", B_tail[2 + sl], cnt)
                        if cch == DC - 1:
                            E.note_w("t", banks[bq], cnt)
                    for bb, dst, bfd in ((bm, mA, B["mA"]), (bq, qA, B["qA"])):
                        E.r_wait("s", banks[bb])
                        E.w_wait("s", bfd)
                        c2 = E.inc("s", s.activation(dst[0:1, tcol:tcol + w],
                                                     pt[bb][0:1, :w], AF.Copy,
                                                     scale=1.0 / D))
                        E.note_r("s", banks[bb], c2)
                        E.note_w("s", bfd, c2)
                rstd_batch(mA, qA, B["mA"], B["qA"])

                E.barrier(("t", "v", "s"))

                with ExitStack() as p1s:
                    xn = rsb("xn", [128, DC * TH], BF16, p1s)
                    ph = rsb("ph", [128, 2 * (3 + TH)], BF16, p1s)
                    gate = rsb("gate", [128, 2 * 512], BF16, p1s)
                    B_xn, B_ph, B_gate = B["xn"], B["ph"], B["gate"]
                    PHW = 3 + TH
                    def prep_xn1(ti2):
                        tcol2, w2 = TILES[ti2]
                        E.r_wait("v", B["mA"])
                        E.r_wait("v", B["rstd"])
                        E.w_wait("v", B_mrb)
                        c4 = E.inc("v", v.tensor_mul(
                            mrb[0:1, tcol2:tcol2 + w2],
                            mA[0:1, tcol2:tcol2 + w2],
                            rstd[0:1, tcol2:tcol2 + w2]))
                        E.note_r("v", B["mA"], c4)
                        E.note_r("v", B["rstd"], c4)
                        E.note_w("v", B_mrb, c4)
                        bcast_to(mrb[0:1, tcol2:tcol2 + w2], w2, 6, B_mrb,
                                 mS, B["mS"])
                        bcast_to(rstd[0:1, tcol2:tcol2 + w2], w2, 7, B["rstd"],
                                 rS, B["rS"])
                        xnv = ap3(xn[:, :], tcol2, TH, DC, w2)
                        E.r_wait("v", B["rS"])
                        E.r_wait("v", B_xcur)
                        E.w_wait("v", B_xn)
                        c4 = E.inc("v", v.tensor_tensor(
                            xnv, xcur_v(tcol2, w2),
                            bc_ap(rS[:, 0:w2], DC, w2), ALU.mult))
                        E.note_r("v", B["rS"], c4)
                        E.note_r("v", B_xcur, c4)
                        E.note_w("v", B_xn, c4)
                        E.r_wait("v", B["mS"])
                        c4 = E.inc("v", v.tensor_tensor(
                            xnv, xnv, bc_ap(mS[:, 0:w2], DC, w2),
                            ALU.subtract))
                        E.note_r("v", B["mS"], c4)
                        E.note_w("v", B_xn, c4)

                    pend_ln2 = []

                    def flush_ln2():
                        for fn in pend_ln2:
                            fn()
                        pend_ln2.clear()

                    prep_xn1(0)
                    for ti, (tcol, w) in enumerate(TILES):
                        def after_u1(_ti=ti):
                            flush_ln2()
                            if _ti + 1 < len(TILES):
                                prep_xn1(_ti + 1)

                        E.r_wait("v", B["b2h"])
                        E.r_wait("v", B_xcur)
                        for m5 in range(DC):
                            dst5 = xcur[:, m5 * TH + tcol:m5 * TH + tcol + w]
                            c5 = E.inc("v", v.tensor_scalar_add(
                                dst5, dst5, b2h[:, m5:m5 + 1]))
                        E.note_r("v", B["b2h"], c5)
                        E.note_w("v", B_xcur, c5)

                        def dst_add(m, bi, bk, _tcol=tcol, _w=w):
                            dstc = xcur[:, m * TH + _tcol:m * TH + _tcol + _w]
                            E.r_wait("v", bk)
                            c3 = E.inc("v", v.scalar_tensor_tensor(
                                dstc, pt[bi][:, :_w], 0.5, dstc,
                                ALU.mult, ALU.add))
                            E.note_r("v", bk, c3)
                            E.note_w("v", B_xcur, c3)

                        ffn_tile(lambda k, _t=tcol, _w=w: xn[:, k * TH + _t:k * TH + _t + _w],
                                 B_xn, dst_add, w, after_u1=after_u1,
                                 avoid=(6, 7))

                        E.r_wait("s", B_xcur)
                        E.w_wait("s", B_xb)
                        c2 = E.inc("s", s.copy(ap3(hap, XB, 512, DC, w),
                                               xcur_v(tcol, w)))
                        E.note_r("s", B_xcur, c2)
                        E.note_w("s", B_xb, c2)
                        E.r_wait("v", B_xb)
                        E.w_wait("v", B_sq)
                        c2 = E.inc("v", v.tensor_mul(ap3(hap, SQ, 512, DC, w),
                                                     ap3(hap, XB, 512, DC, w),
                                                     ap3(hap, XB, 512, DC, w)))
                        E.note_r("v", B_xb, c2)
                        E.note_w("v", B_sq, c2)
                        def _ln2mm(_tcol=tcol, _w=w):
                            E.w_wait("t", banks[4])
                            E.r_wait("t", B_xb)
                            E.r_wait("t", B["onec"])
                            for k in range(DC):
                                ii2 = t.matmul(pt[4][0:1, :_w], onec[:, :],
                                               hfl(XB + k * 512, _w),
                                               start=(k == 0), stop=(k == DC - 1))
                            cn = E.inc("t", ii2)
                            E.note_w("t", banks[4], cn)
                            E.note_r("t", B_xb, cn)
                            E.w_wait("t", banks[5])
                            E.r_wait("t", B_sq)
                            for k in range(DC):
                                ii2 = t.matmul(pt[5][0:1, :_w], onec[:, :],
                                               hfl(SQ + k * 512, _w),
                                               start=(k == 0), stop=(k == DC - 1))
                            cn = E.inc("t", ii2)
                            E.note_w("t", banks[5], cn)
                            E.note_r("t", B_sq, cn)
                            for bb, dst, bfd in ((4, mB, B["mB"]), (5, qB, B["qA"])):
                                E.r_wait("s", banks[bb])
                                E.w_wait("s", bfd)
                                c3 = E.inc("s", s.activation(
                                    dst[0:1, _tcol:_tcol + _w],
                                    pt[bb][0:1, :_w], AF.Copy, scale=1.0 / D))
                                E.note_r("s", banks[bb], c3)
                                E.note_w("s", bfd, c3)
                        pend_ln2.append(_ln2mm)
                    flush_ln2()
                    rstd_batch(mB, qB, B["mB"], B["qA"])

                    def p0c_tile(ti3):
                        tcol3, w3 = TILES[ti3]
                        E.r_wait("v", B["mB"])
                        E.r_wait("v", B["rstd"])
                        E.w_wait("v", B_mrb)
                        c4 = E.inc("v", v.tensor_mul(
                            mrb[0:1, tcol3:tcol3 + w3],
                            mB[0:1, tcol3:tcol3 + w3],
                            rstd[0:1, tcol3:tcol3 + w3]))
                        E.note_r("v", B["mB"], c4)
                        E.note_r("v", B["rstd"], c4)
                        E.note_w("v", B_mrb, c4)
                        bcast_to(mrb[0:1, tcol3:tcol3 + w3], w3, 6, B_mrb,
                                 mS, B["mS"])
                        bcast_to(rstd[0:1, tcol3:tcol3 + w3], w3, 7, B["rstd"],
                                 rS, B["rS"])
                        xnv = ap3(xn[:, :], tcol3, TH, DC, w3)
                        E.r_wait("v", B["rS"])
                        E.r_wait("v", B_xcur)
                        E.w_wait("v", B_xn)
                        c4 = E.inc("v", v.tensor_tensor(
                            xnv, xcur_v(tcol3, w3),
                            bc_ap(rS[:, 0:w3], DC, w3), ALU.mult))
                        E.note_r("v", B["rS"], c4)
                        E.note_r("v", B_xcur, c4)
                        E.note_w("v", B_xn, c4)
                        E.r_wait("v", B["mS"])
                        c4 = E.inc("v", v.tensor_tensor(
                            xnv, xnv, bc_ap(mS[:, 0:w3], DC, w3),
                            ALU.subtract))
                        E.note_r("v", B["mS"], c4)
                        E.note_w("v", B_xn, c4)

                    E.barrier(("v",))

                    HELD = (2, 3, 4, 5)
                    for pr in HELD:
                        E.w_wait("t", banks[pr])
                    pending_stats = []

                    def flush_stats():
                        for fn in pending_stats:
                            fn()
                        pending_stats.clear()

                    for gi in range(NG):
                        ci = half * 20 + gi
                        ws_ensure(ci)
                        for tix, (tcol, w) in enumerate(TILES):
                            if gi == 0:
                                p0c_tile(tix)
                            for ml in range(2):
                                bi = nb(held=HELD)
                                bk = banks[bi]
                                E.w_wait("t", bk)
                                for k in range(8):
                                    E.r_wait("t", B_xn)
                                    ii = t.matmul(
                                        pt[bi][:, :w],
                                        ws_tile(ci, "ip", ("Wh", k, ml)),
                                        xn[:, k * TH + tcol:k * TH + tcol + w],
                                        start=(k == 0), stop=(k == 7))
                                cnt = E.inc("t", ii)
                                E.note_w("t", bk, cnt)
                                E.note_r("t", B_xn, cnt)
                                if ml == 0:
                                    flush_stats()
                                E.r_wait("s", bk)
                                E.w_wait("s", B_ph)
                                c2 = E.inc("s", s.copy(
                                    ph[:, ml * PHW + 3 + tcol:
                                       ml * PHW + 3 + tcol + w], pt[bi][:, :w]))
                                E.note_r("s", bk, c2)
                                E.note_w("s", B_ph, c2)
                            if tcol == 0:
                                E.r_wait("v", B_ph)
                                E.r_wait("v", B["consts"])
                                mv = ap3(ph[:, :], 32, PHW, 2, 3)
                                c2 = E.inc("v", v.tensor_scalar_mul(
                                    mv, mv, ccol("mask", half, 1)))
                                E.note_r("v", B["consts"], c2)
                                E.note_w("v", B_ph, c2)
                                continue
                            for ml in range(2):
                                bi = nb(held=HELD)
                                bk = banks[bi]
                                E.w_wait("t", bk)
                                for k in range(8):
                                    ii = t.matmul(
                                        pt[bi][:, :w],
                                        ws_tile(ci, "ip", ("Wg", k, ml)),
                                        xn[:, k * TH + tcol:k * TH + tcol + w],
                                        start=(k == 0), stop=(k == 7))
                                cnt = E.inc("t", ii)
                                E.note_w("t", bk, cnt)
                                E.note_r("t", B_xn, cnt)
                                E.r_wait("s", bk)
                                E.w_wait("s", B_gate)
                                c2 = E.inc("s", s.activation(
                                    gate[:, ml * 512:ml * 512 + w],
                                    pt[bi][:, :w], AF.Silu))
                                E.note_r("s", bk, c2)
                                E.note_w("s", B_gate, c2)
                            for mc in range(2):
                                bi = nb(held=HELD)
                                bk = banks[bi]
                                E.w_wait("t", bk)
                                first = True
                                for tap in range(KTAP):
                                    for kc in range(2):
                                        E.r_wait("t", B_ph)
                                        ii = t.matmul(
                                            pt[bi][:, :w],
                                            ws_tile(ci, "ip", ("cv", tap, kc, mc)),
                                            ph[:, kc * PHW + tcol + tap:
                                               kc * PHW + tcol + tap + w],
                                            start=first,
                                            stop=(tap == 3 and kc == 1))
                                        first = False
                                cnt = E.inc("t", ii)
                                E.note_w("t", bk, cnt)
                                E.note_r("t", B_ph, cnt)
                                cch = 2 * gi + mc
                                base = cch * HT + (tcol - HALO)
                                E.r_wait("v", bk)
                                E.r_wait("v", B_gate)
                                E.r_wait("v", B["consts"])
                                E.w_wait("v", B_h)
                                c2 = E.inc("v", v.scalar_tensor_tensor(
                                    h[:, base:base + w], pt[bi][:, :w],
                                    ccol("conv_b", cch, 1),
                                    gate[:, mc * 512:mc * 512 + w],
                                    ALU.add, ALU.mult))
                                E.note_r("v", bk, c2)
                                E.note_r("v", B_gate, c2)
                                E.note_w("v", B_h, c2)
                                si2 = tix - 1
                                bm, bq = (2, 3) if si2 == 0 else (4, 5)
                                gsl = gate[:, mc * 512:mc * 512 + w]
                                E.r_wait("v", B_h)
                                E.w_wait("v", B_gate)
                                c2 = E.inc("v", v.tensor_mul(
                                    gsl, h[:, base:base + w],
                                    h[:, base:base + w]))
                                E.note_r("v", B_h, c2)
                                E.note_w("v", B_gate, c2)

                                def _stats(_bm=bm, _bq=bq, _base=base, _w=w,
                                           _gsl=gsl, _gi=gi, _mc=mc):
                                    E.r_wait("t", B_h)
                                    E.r_wait("t", B["onec"])
                                    ii2 = t.matmul(pt[_bm][0:1, :_w], onec[:, :],
                                                   h[:, _base:_base + _w],
                                                   start=(_gi == 0 and _mc == 0),
                                                   stop=(_gi == NG - 1 and _mc == 1))
                                    cn = E.inc("t", ii2)
                                    E.note_r("t", B_h, cn)
                                    if _gi == NG - 1 and _mc == 1:
                                        E.note_w("t", banks[_bm], cn)
                                    E.r_wait("t", B_gate)
                                    ii2 = t.matmul(pt[_bq][0:1, :_w], onec[:, :],
                                                   _gsl,
                                                   start=(_gi == 0 and _mc == 0),
                                                   stop=(_gi == NG - 1 and _mc == 1))
                                    cn = E.inc("t", ii2)
                                    E.note_r("t", B_gate, cn)
                                    if _gi == NG - 1 and _mc == 1:
                                        E.note_w("t", banks[_bq], cn)
                                pending_stats.append(_stats)
                        ws_done(ci)
                    flush_stats()
                    for si2, (tcol, w) in enumerate(OWN_TILES):
                        bm, bq = (2, 3) if si2 == 0 else (4, 5)
                        for bb, dst, bfd in ((bm, mA, B["mA"]), (bq, qA, B["qA"])):
                            E.r_wait("s", banks[bb])
                            E.w_wait("s", bfd)
                            c2 = E.inc("s", s.activation(
                                dst[0:1, tcol:tcol + w], pt[bb][0:1, :w],
                                AF.Copy, scale=1.0 / I2))
                            E.note_r("s", banks[bb], c2)
                            E.note_w("s", bfd, c2)
                rstd_batch(mA, qA, B["mA"], B["qA"])

                with ExitStack() as p2s:
                    hn = rsb("hn", [128, IC * HT], BF16, p2s)
                    m1 = rsb("m1", [128, DC * HT], BF16, p2s)
                    B_hn, B_m1 = B["hn"], B["m1"]

                    def h_v(t0o, w):
                        return ap3(h[:, :], t0o, HT, IC, w)

                    def hn_v(t0o, w):
                        return ap3(hn[:, :], t0o, HT, IC, w)

                    mS2 = xb3[:, 0:512]
                    rS2 = xb3[:, 512:1024]
                    (t1c, t1w), (t2c, t2w) = OWN_TILES
                    bcast_to(mA[0:1, t1c:t1c + t1w], t1w, 6, B["mA"], mS, B["mS"])
                    bcast_to(rstd[0:1, t1c:t1c + t1w], t1w, 7, B["rstd"], rS, B["rS"])
                    bcast_to(mA[0:1, t2c:t2c + t2w], t2w, 6, B["mA"], mS2, B_xb3[0])
                    bcast_to(rstd[0:1, t2c:t2c + t2w], t2w, 7, B["rstd"], rS2, B_xb3[1])
                    hn_ready = [0] * IC
                    for k in range(IC):
                        for si, (tcol, w) in enumerate(OWN_TILES):
                            t0o = tcol - HALO
                            base = k * HT + t0o
                            msl = mS if si == 0 else mS2
                            rsl = rS if si == 0 else rS2
                            bmsl = B["mS"] if si == 0 else B_xb3[0]
                            brsl = B["rS"] if si == 0 else B_xb3[1]
                            E.r_wait("v", bmsl)
                            E.r_wait("v", B_h)
                            E.w_wait("v", B_hn)
                            c2 = E.inc("v", v.tensor_tensor(
                                hn[:, base:base + w], h[:, base:base + w],
                                msl[:, 0:w], ALU.subtract))
                            E.note_r("v", bmsl, c2)
                            E.note_r("v", B_h, c2)
                            E.note_w("v", B_hn, c2)
                            E.r_wait("v", brsl)
                            E.r_wait("v", B["consts"])
                            c2 = E.inc("v", v.scalar_tensor_tensor(
                                hn[:, base:base + w], hn[:, base:base + w],
                                ccol("ln_g", k, 1), rsl[:, 0:w],
                                ALU.mult, ALU.mult))
                            E.note_r("v", brsl, c2)
                            E.note_w("v", B_hn, c2)
                        hn_ready[k] = E.cnt["vc"]

                    for m in range(8):
                        ci = half * 20 + 8 + m // 2
                        if m % 2 == 0:
                            ws_ensure(ci)
                        if half == 0:
                            bi = nb()
                            bk = banks[bi]
                            E.w_wait("t", bk)
                            E.r_wait("t", B["lnb"])
                            for k in range(16):
                                ii = t.matmul(pt[bi][:, 0:1],
                                              ws_tile(ci, "fc", ("fc", m % 2, k)),
                                              lnb_bf[:, k:k + 1],
                                              start=(k == 0), stop=(k == 15))
                            cnt = E.inc("t", ii)
                            E.note_w("t", bk, cnt)
                            E.r_wait("v", bk)
                            E.w_wait("v", B["fcb"])
                            E.r_wait("v", B["consts"])
                            c2 = E.inc("v", v.tensor_scalar_add(
                                fcb[:, m:m + 1], pt[bi][:, 0:1],
                                ccol("fc_b", m, 1)))
                            E.note_r("v", bk, c2)
                            E.note_w("v", B["fcb"], c2)
                        for (tcol, w) in OWN_TILES:
                            t0o = tcol - HALO
                            bi = nb()
                            bk = banks[bi]
                            E.w_wait("t", bk)
                            for k in range(16):
                                E.wait("t", "vc", hn_ready[k])
                                ii = t.matmul(pt[bi][:, :w],
                                              ws_tile(ci, "fc", ("fc", m % 2, k)),
                                              hn[:, k * HT + t0o:k * HT + t0o + w],
                                              start=(k == 0), stop=(k == 15))
                            cnt = E.inc("t", ii)
                            E.note_w("t", bk, cnt)
                            E.note_r("t", B_hn, cnt)
                            E.r_wait("s", bk)
                            E.w_wait("s", B_m1)
                            E.r_wait("s", B["fcb"])
                            c2 = E.inc("s", s.activation(
                                m1[:, m * HT + t0o:m * HT + t0o + w],
                                pt[bi][:, :w], AF.Silu,
                                bias=fcb[:, m:m + 1], scale=1.0))
                            E.note_r("s", bk, c2)
                            E.note_w("s", B_m1, c2)
                        if m % 2 == 1:
                            ws_done(ci)

                    for m in range(16):
                        ci = half * 20 + 12 + m // 4
                        if m % 4 == 0:
                            ws_ensure(ci)
                        for (tcol, w) in OWN_TILES:
                            t0o = tcol - HALO
                            bi = nb()
                            bk = banks[bi]
                            E.w_wait("t", bk)
                            for k in range(8):
                                E.r_wait("t", B_m1)
                                ii = t.matmul(pt[bi][:, :w],
                                              ws_tile(ci, "cp", ("cp", m % 4, k)),
                                              m1[:, k * HT + t0o:k * HT + t0o + w],
                                              start=(k == 0), stop=(k == 7))
                            cnt = E.inc("t", ii)
                            E.note_w("t", bk, cnt)
                            E.note_r("t", B_m1, cnt)
                            base = m * HT + t0o
                            E.r_wait("v", bk)
                            E.r_wait("v", B["consts"])
                            E.w_wait("v", B_h)
                            c2 = E.inc("v", v.scalar_tensor_tensor(
                                h[:, base:base + w], pt[bi][:, :w],
                                ccol("cproj_b", m, 1), h[:, base:base + w],
                                ALU.add, ALU.add))
                            E.note_r("v", bk, c2)
                            E.note_w("v", B_h, c2)
                        if m % 4 == 3:
                            ws_done(ci)

                x2 = hs.enter_context(
                    nc.sbuf_tensor(f"x2_{half}", [128, DC * HT], F32, side="left"))
                B_x2 = B["x2"]
                stat_banks = {0: (4, 5), 1: (2, 3)}
                mm_banks = {0: (0, 6), 1: (1, 7)}
                for pr in stat_banks.values():
                    for bi in pr:
                        E.w_wait("t", banks[bi])
                pending3 = []

                def flush3():
                    for fn in pending3:
                        fn()
                    pending3.clear()

                for m in range(8):
                    ci = half * 20 + 16 + m // 2
                    if m % 2 == 0:
                        ws_ensure(ci)
                    for si, (tcol, w) in enumerate(OWN_TILES):
                        t0o = tcol - HALO
                        bi = mm_banks[si][m % 2]
                        bk = banks[bi]
                        E.w_wait("t", bk)
                        for k in range(16):
                            E.r_wait("t", B_h)
                            ii = t.matmul(pt[bi][:, :w],
                                          ws_tile(ci, "ow", ("ow", m % 2, k)),
                                          h[:, k * HT + t0o:k * HT + t0o + w],
                                          start=(k == 0), stop=(k == 15))
                        cnt = E.inc("t", ii)
                        E.note_w("t", bk, cnt)
                        E.note_r("t", B_h, cnt)
                        flush3()
                        base = m * HT + t0o
                        E.r_wait("v", bk)
                        E.r_wait("v", B_xcur)
                        E.w_wait("v", B_x2)
                        c2 = E.inc("v", v.tensor_add(
                            x2[:, base:base + w], pt[bi][:, :w],
                            xcur[:, m * TH + tcol:m * TH + tcol + w]))
                        E.note_r("v", bk, c2)
                        E.note_r("v", B_xcur, c2)
                        E.note_w("v", B_x2, c2)
                        sl = si
                        E.r_wait("s", B_x2)
                        E.w_wait("s", B_xb3[sl])
                        c2 = E.inc("s", s.copy(xb3[:, sl * 512:sl * 512 + w],
                                               x2[:, base:base + w]))
                        E.note_r("s", B_x2, c2)
                        E.note_w("s", B_xb3[sl], c2)
                        E.r_wait("v", B_xb3[sl])
                        E.w_wait("v", B_sq3[sl])
                        c2 = E.inc("v", v.tensor_mul(sq3[:, sl * 512:sl * 512 + w],
                                                     xb3[:, sl * 512:sl * 512 + w],
                                                     xb3[:, sl * 512:sl * 512 + w]))
                        E.note_r("v", B_xb3[sl], c2)
                        E.note_w("v", B_sq3[sl], c2)
                        bmi, bqi = stat_banks[si]

                        def _st3(_bmi=bmi, _bqi=bqi, _sl=sl, _m=m, _w=w):
                            E.r_wait("t", B_xb3[_sl])
                            ii2 = t.matmul(pt[_bmi][0:1, :_w], onec[:, :],
                                           xb3[:, _sl * 512:_sl * 512 + _w],
                                           start=(_m == 0), stop=(_m == 7))
                            cn = E.inc("t", ii2)
                            E.note_r("t", B_xb3[_sl], cn)
                            if _m == 7:
                                E.note_w("t", banks[_bmi], cn)
                            E.r_wait("t", B_sq3[_sl])
                            ii2 = t.matmul(pt[_bqi][0:1, :_w], onec[:, :],
                                           sq3[:, _sl * 512:_sl * 512 + _w],
                                           start=(_m == 0), stop=(_m == 7))
                            cn = E.inc("t", ii2)
                            E.note_r("t", B_sq3[_sl], cn)
                            if _m == 7:
                                E.note_w("t", banks[_bqi], cn)
                        pending3.append(_st3)
                    if m % 2 == 1:
                        ws_done(ci)
                flush3()
                if half == 0:
                    E.w_wait("q", B_xcur)
                    c = E.inc("q", q.dma_start(
                        xcur[:, :], x_h[:, DC * TH:2 * DC * TH]))
                    E.note_w("q", B_xcur, c)
                for si, (tcol, w) in enumerate(OWN_TILES):
                    bmi, bqi = stat_banks[si]
                    for bb, dst, bfd in ((bmi, mA, B["mA"]), (bqi, qA, B["qA"])):
                        E.r_wait("s", banks[bb])
                        E.w_wait("s", bfd)
                        c2 = E.inc("s", s.activation(dst[0:1, tcol:tcol + w],
                                                     pt[bb][0:1, :w], AF.Copy,
                                                     scale=1.0 / D))
                        E.note_r("s", banks[bb], c2)
                        E.note_w("s", bfd, c2)
                rstd_batch(mA, qA, B["mA"], B["qA"])

                xc3 = h[:, 4096:8192]
                HqB2 = h[:, 0:4096]
                HqD[0] = [Hq, HqB2]
                B_HqD[0] = [B_Hq, B["Hq2"]]
                Pb_c[0], Qb_c[0] = Pb, Qb
                B_xc3 = B["xc3"]
                E.barrier(("v", "s"))

                def prep_xc3(si4):
                    tcol4, w4 = OWN_TILES[si4]
                    t0o4 = tcol4 - HALO
                    bcast_to(mA[0:1, tcol4:tcol4 + w4], w4, 6, B["mA"],
                             mS, B["mS"])
                    bcast_to(rstd[0:1, tcol4:tcol4 + w4], w4, 7, B["rstd"],
                             rS, B["rS"])
                    E.r_wait("v", B["mS"])
                    E.r_wait("v", B_x2)
                    E.w_wait("v", B_xc3)
                    xc3d = ap3(xc3, 0, 512, DC, w4)
                    c2 = E.inc("v", v.tensor_tensor(
                        xc3d, ap3(x2[:, :], t0o4, HT, DC, w4),
                        bc_ap(mS[:, 0:w4], DC, w4), ALU.subtract))
                    E.note_r("v", B["mS"], c2)
                    E.note_r("v", B_x2, c2)
                    E.note_w("v", B_xc3, c2)
                    E.r_wait("v", B["rS"])
                    c2 = E.inc("v", v.tensor_tensor(
                        xc3d, xc3d, bc_ap(rS[:, 0:w4], DC, w4), ALU.mult))
                    E.note_r("v", B["rS"], c2)
                    E.note_w("v", B_xc3, c2)

                prep_xc3(0)
                for si, (tcol, w) in enumerate(OWN_TILES):
                    t0o = tcol - HALO

                    def dst_add2(m, bi, bk, _si=si, _t0o=t0o, _tcol=tcol, _w=w):
                        sl = m % 2
                        E.r_wait("s", bk)
                        E.w_wait("s", B_tsl[sl])
                        c3 = E.inc("s", s.mul(tsl[:, sl * 512:sl * 512 + _w],
                                              pt[bi][:, :_w], 0.5))
                        E.note_r("s", bk, c3)
                        E.note_w("s", B_tsl[sl], c3)
                        ob = B_outb[sl]
                        E.r_wait("v", B_tsl[sl])
                        E.r_wait("v", B_x2)
                        E.r_wait("v", B["b2h"])
                        E.w_wait("v", ob)
                        c3 = E.inc("v", v.scalar_tensor_tensor(
                            outb[:, sl * 512:sl * 512 + _w],
                            tsl[:, sl * 512:sl * 512 + _w],
                            b2h[:, m:m + 1],
                            x2[:, m * HT + _t0o:m * HT + _t0o + _w],
                            ALU.add, ALU.add))
                        E.note_r("v", B_tsl[sl], c3)
                        E.note_r("v", B_x2, c3)
                        E.note_w("v", ob, c3)
                        E.r_wait("q", ob)
                        c3 = E.inc("q", q.dma_start(
                            y_h[:, m * 2048 + half * HT + _t0o:
                                m * 2048 + half * HT + _t0o + _w],
                            outb[:, sl * 512:sl * 512 + _w]))
                        E.note_r("q", ob, c3)
                        store_last[0] = c3

                    ffn_tile(lambda k: xc3[:, k * 512:k * 512 + w],
                             B_xc3, dst_add2, w, avoid=(6, 7),
                             after_u1=(lambda _si=si: prep_xc3(_si + 1)
                                       if _si + 1 < len(OWN_TILES) else None))

        E.wait("q", "sd", store_last[0])
    return nc


def kernel(**inputs):
    blobs, ixs, const_blob, ccols = build_host(inputs)
    x_in = np.asarray(inputs["L0_attn_hidden_states"], np.float32)
    xs = shard_x(x_in)
    nc = build_nc(ixs, ccols, const_blob.shape[1])
    onec = np.ones((128, 1), np.float32)
    oner = np.ones((1, 128), np.float32)
    in_maps = []
    for core in range(NCORES):
        chunk = core % 2
        cb = const_blob.copy()
        cb[:, ccols["mask"] + 0] = 1.0 if chunk == 1 else 0.0
        cb[:, ccols["mask"] + 1] = 1.0
        in_maps.append(dict(x=xs[core], const_blob=cb, onec=onec, oner=oner,
                            **blobs))
    res = run_bass_kernel_spmd(nc, in_maps, core_ids=list(range(NCORES)))
    return gather_y(res.results)


# version 15
# speedup vs baseline: 1.1118x; 1.0044x over previous
import numpy as np
import ml_dtypes
from contextlib import ExitStack

import concourse.bass as bass
import concourse.mybir as mybir
from concourse.ap import AP
from concourse.bass_utils import run_bass_kernel_spmd

BF16 = mybir.dt.bfloat16
F32 = mybir.dt.float32
AF = mybir.ActivationFunctionType
ALU = mybir.AluOpType

D = 1024
I2 = 2048
DFF = 4096
NG = 8
KTAP = 4
HALO = 32
HT = 1024
TH = HALO + HT
NCORES = 8
EPS = 1e-5
TILES = [(0, HALO), (HALO, 512), (HALO + 512, 512)]
OWN_TILES = TILES[1:]
DC, IC = 8, 16


def _tiles(WT):
    return {(k, m): WT[128 * k:128 * (k + 1), 128 * m:128 * (m + 1)]
            for k in range(WT.shape[0] // 128) for m in range(WT.shape[1] // 128)}


def _blob(entries):
    ix, arrs = {}, []
    for i, (key, a) in enumerate(entries):
        ix[key] = i
        arrs.append(np.asarray(a, np.float32))
    return np.ascontiguousarray(np.concatenate(arrs, axis=1)), ix


def _col_pack(vec, nch):
    return np.ascontiguousarray(np.asarray(vec, np.float32).reshape(nch, 128).T)


def build_host(inputs):
    f32 = lambda k: np.ascontiguousarray(np.asarray(inputs[k], np.float32))
    ipw = f32("L0_attn_in_proj_w")
    WhT = np.ascontiguousarray(ipw[:I2, :].T)
    WgT = np.ascontiguousarray(ipw[I2:, :].T)
    cw = f32("L0_attn_conv_w")
    fcT = np.ascontiguousarray(f32("L0_attn_fc_w").T)
    cpT = np.ascontiguousarray(f32("L0_attn_cproj_w").T)
    owT = np.ascontiguousarray(f32("L0_attn_out_w").T)

    tU1, tV1 = _tiles(f32("L0_ffn_U1")), _tiles(f32("L0_ffn_V1"))
    tU2, tV2 = _tiles(f32("L0_ffn_U2")), _tiles(f32("L0_ffn_V2"))
    ents = ([(("U1", k, m), tU1[(k, m)]) for m in range(2) for k in range(8)]
            + [(("V1", k, m), tV1[(k, m)]) for m in range(32) for k in range(2)]
            + [(("U2", k, m), tU2[(k, m)]) for m in range(2) for k in range(32)]
            + [(("V2", k, m), tV2[(k, m)]) for m in range(8) for k in range(2)])
    ffn_blob, ffn_ix = _blob(ents)

    tWh, tWg = _tiles(WhT), _tiles(WgT)
    ip_chunks, ip_ix = [], None
    for g in range(NG):
        ents = ([(("Wh", k, ml), tWh[(k, 2 * g + ml)])
                 for ml in range(2) for k in range(8)]
                + [(("Wg", k, ml), tWg[(k, 2 * g + ml)])
                   for ml in range(2) for k in range(8)])
        for mc in range(2):
            for tap in range(KTAP):
                CT = np.ascontiguousarray(cw[256 * g:256 * (g + 1), :, tap].T)
                for kc in range(2):
                    ents.append((("cv", tap, kc, mc),
                                 CT[128 * kc:128 * (kc + 1),
                                    128 * mc:128 * (mc + 1)]))
        blob, ip_ix = _blob(ents)
        ip_chunks.append(blob)
    ip_blob = np.ascontiguousarray(np.concatenate(ip_chunks, axis=1))

    tfc, tcp, tow = _tiles(fcT), _tiles(cpT), _tiles(owT)
    fc_chunks, cp_chunks, ow_chunks = [], [], []
    for c in range(4):
        b, fc_ix = _blob([(("fc", ml, k), tfc[(k, 2 * c + ml)])
                          for ml in range(2) for k in range(16)])
        fc_chunks.append(b)
    for c in range(4):
        b, cp_ix = _blob([(("cp", ml, k), tcp[(k, 4 * c + ml)])
                          for ml in range(4) for k in range(8)])
        cp_chunks.append(b)
    for c in range(4):
        b, ow_ix = _blob([(("ow", ml, k), tow[(k, 2 * c + ml)])
                          for ml in range(2) for k in range(16)])
        ow_chunks.append(b)
    fc_blob = np.ascontiguousarray(np.concatenate(fc_chunks, axis=1))
    cp_blob = np.ascontiguousarray(np.concatenate(cp_chunks, axis=1))
    ow_blob = np.ascontiguousarray(np.concatenate(ow_chunks, axis=1))

    cols, parts = {}, []

    def add(name, arr):
        cols[name] = sum(p.shape[1] for p in parts)
        parts.append(arr)

    add("b1", _col_pack(inputs["L0_ffn_b1"], 32))
    add("b2", _col_pack(inputs["L0_ffn_b2"], 8))
    add("fc_b", _col_pack(inputs["L0_attn_fc_b"], 8))
    add("cproj_b", _col_pack(inputs["L0_attn_cproj_b"], 16))
    add("conv_b", _col_pack(inputs["L0_attn_conv_b"], 16))
    add("ln_g", _col_pack(inputs["L0_attn_ln_g"], 16))
    add("ln_b", _col_pack(inputs["L0_attn_ln_b"], 16))
    add("eps", np.full((128, 1), EPS, np.float32))
    add("mask", np.zeros((128, 2), np.float32))
    const_blob = np.ascontiguousarray(np.concatenate(parts, axis=1))

    blobs = dict(ffn_blob=ffn_blob, ip_blob=ip_blob, fc_blob=fc_blob,
                 cp_blob=cp_blob, ow_blob=ow_blob)
    ixs = dict(ffn=ffn_ix, ip=ip_ix, fc=fc_ix, cp=cp_ix, ow=ow_ix)
    return blobs, ixs, const_blob, cols


def shard_x(x_in):
    outs = []
    for core in range(NCORES):
        b, chunk = core // 2, core % 2
        t0 = chunk * 2048
        arr = np.zeros((128, 2 * DC * TH), np.float32)
        for half in range(2):
            base = t0 + half * HT
            xs = np.zeros((TH, D), np.float32)
            lo = base - HALO
            if lo >= 0:
                xs[:HALO] = x_in[b, lo:base]
            xs[HALO:] = x_in[b, base:base + HT]
            fm = xs.T.reshape(DC, 128, TH).transpose(1, 0, 2)
            arr[:, half * DC * TH:(half + 1) * DC * TH] = fm.reshape(128, DC * TH)
        outs.append(np.ascontiguousarray(arr))
    return outs


def gather_y(results):
    out = np.zeros((4, 4096, D), np.float32)
    for core in range(NCORES):
        b, chunk = core // 2, core % 2
        a = np.asarray(results[core]["y"], np.float32).reshape(128, DC, 2048)
        out[b, chunk * 2048:(chunk + 1) * 2048] = a.transpose(2, 1, 0).reshape(2048, D)
    return out


class Buf:
    __slots__ = ("name", "w", "r", "psum")

    def __init__(self, name, psum=False):
        self.name, self.w, self.r, self.psum = name, None, {}, psum


class Em:
    ENG_SEM = {"t": "tc", "v": "vc", "s": "sc", "q": "sd", "g": "gd"}

    def __init__(self, nc, es):
        self.nc = nc
        self.eng = {"t": nc.tensor, "v": nc.vector, "s": nc.scalar,
                    "q": nc.sync, "g": nc.gpsimd}
        self.sems, self.cnt, self.wm = {}, {}, {}
        for sn in ("tc", "vc", "sc", "sd", "gd"):
            self.sems[sn] = es.enter_context(nc.semaphore(sn))
            self.cnt[sn] = 0

    def wait(self, en, sn, val):
        if not val or val <= 0:
            return
        assert val <= self.cnt[sn], (en, sn, val, self.cnt[sn])
        if self.wm.get((en, sn), 0) >= val:
            return
        self.eng[en].wait_ge(self.sems[sn], val)
        self.wm[(en, sn)] = val

    def inc(self, en, instr, n=1):
        sn = self.ENG_SEM[en]
        if en in ("q", "g"):
            n = 16
        instr.then_inc(self.sems[sn], n)
        self.cnt[sn] += n
        return self.cnt[sn]

    def r_wait(self, en, buf):
        if buf.w is not None and buf.w[0] != en:
            self.wait(en, self.ENG_SEM[buf.w[0]], buf.w[1])
        if buf.psum and en in ("v", "s"):
            other = "s" if en == "v" else "v"
            if other in buf.r:
                self.wait(en, self.ENG_SEM[other], buf.r[other])

    def w_wait(self, en, buf):
        if buf.w is not None and buf.w[0] != en:
            self.wait(en, self.ENG_SEM[buf.w[0]], buf.w[1])
        for ren, rc in buf.r.items():
            if ren != en:
                self.wait(en, self.ENG_SEM[ren], rc)

    def note_r(self, en, buf, cnt):
        buf.r[en] = max(buf.r.get(en, 0), cnt)

    def note_w(self, en, buf, cnt):
        buf.w = (en, cnt)
        buf.r = {}

    def barrier(self, ens=("t", "v", "s"), dma=False):
        snap = {sn: self.cnt[sn] for sn in ("tc", "vc", "sc")}
        for en in ens:
            for sn, vv in snap.items():
                if self.ENG_SEM[en] != sn:
                    self.wait(en, sn, vv)
            if dma:
                self.wait(en, "sd", self.cnt["sd"])
                self.wait(en, "gd", self.cnt["gd"])


def ap3(base_ap, extra_off, chunk_stride, nch, width):
    apl = [list(p) for p in base_ap.ap]
    return AP(tensor=base_ap.tensor, offset=base_ap.offset + extra_off,
              ap=[apl[0], [chunk_stride, nch], [1, width]])


def bc_ap(psum_ap, reps, width):
    apl = [list(p) for p in psum_ap.ap]
    return AP(tensor=psum_ap.tensor, offset=psum_ap.offset,
              ap=[apl[0], [0, reps], [apl[1][0], width]])


def build_nc(ixs, ccols, nconst):
    nc = bass.Bass("TRN2", target_bir_lowering=False)
    x_h = nc.dram_tensor("x", [128, 2 * DC * TH], F32, kind="ExternalInput").ap()
    ffn_h = nc.dram_tensor("ffn_blob", [128, 160 * 128], F32, kind="ExternalInput").ap()
    ip_h = nc.dram_tensor("ip_blob", [128, NG * 48 * 128], F32, kind="ExternalInput").ap()
    fc_h = nc.dram_tensor("fc_blob", [128, 8 * 16 * 128], F32, kind="ExternalInput").ap()
    cp_h = nc.dram_tensor("cp_blob", [128, 16 * 8 * 128], F32, kind="ExternalInput").ap()
    ow_h = nc.dram_tensor("ow_blob", [128, 8 * 16 * 128], F32, kind="ExternalInput").ap()
    const_h = nc.dram_tensor("const_blob", [128, nconst], F32, kind="ExternalInput").ap()
    onec_h = nc.dram_tensor("onec", [128, 1], F32, kind="ExternalInput").ap()
    oner_h = nc.dram_tensor("oner", [1, 128], F32, kind="ExternalInput").ap()
    y_h = nc.dram_tensor("y", [128, DC * 2048], F32, kind="ExternalOutput").ap()

    with ExitStack() as es:
        E = Em(nc, es)
        t, v, s, q, g = nc.tensor, nc.vector, nc.scalar, nc.sync, nc.gpsimd

        def sbl(name, shape, dt):
            return es.enter_context(nc.sbuf_tensor(name, shape, dt, side="left"))

        consts = sbl("consts_sb", [128, nconst], F32)
        onec = sbl("onec_sb", [128, 1], BF16)
        oner = sbl("oner_sb", [1, 128], BF16)
        ffn_sb = sbl("ffn_sb", [128, 160 * 128], BF16)
        wslot = [sbl(f"wslot{i}", [128, 48 * 128], BF16) for i in range(2)]
        mA = sbl("mA", [1, TH], BF16)
        qA = sbl("qA", [1, TH], BF16)
        mB = sbl("mB", [1, TH], BF16)
        qB = qA
        rstd = sbl("rstd", [1, TH], BF16)
        varb = sbl("varb", [1, TH], F32)
        b2h = sbl("b2h", [128, 8], F32)
        lnb_bf = sbl("lnb_bf", [128, 16], BF16)
        fcb = sbl("fcb", [128, 8], F32)
        tsl = sbl("tsl", [128, 2 * 512], F32)
        xb3 = sbl("xb3", [128, 4 * 512], BF16)
        sq3 = sbl("sq3", [128, 2 * 512], BF16)
        mS = xb3[:, 2 * 512:3 * 512]
        rS = xb3[:, 3 * 512:4 * 512]
        mrb = sbl("mrb", [1, TH], BF16)
        outb = sbl("outb", [128, 2 * 512], F32)

        B = {n: Buf(n) for n in
             ("consts onec oner ffn mA qA mB rstd varb b2h lnb fcb "
              "mS rS xcur h x2 xn ph gate hn m1 xb sq xc3 Pb2 Hq2 Qb2 "
              "Pb Hq Qb").split()}
        B_wslot = [Buf("ws0"), Buf("ws1")]
        B_tsl = [Buf("tsl0"), Buf("tsl1")]
        B_mrb = Buf("mrb")
        B_xb3 = [Buf(f"xb3{i}") for i in range(2)]
        B_sq3 = [Buf(f"sq3{i}") for i in range(2)]
        B_tail = [Buf(f"tail{i}") for i in range(4)]
        B_outb = [Buf("ob0"), Buf("ob1")]
        banks = [Buf(f"bk{i}", psum=True) for i in range(8)]
        pt = [es.enter_context(nc.psum_tensor(f"pb{i}", [128, 512], F32))
              for i in range(8)]

        def ccol(name, j=0, n=1):
            return consts[:, ccols[name] + j:ccols[name] + j + n]

        def fftile(name, k, m):
            i = ixs["ffn"][(name, k, m)]
            return ffn_sb[:, 128 * i:128 * (i + 1)]

        c = E.inc("q", q.dma_start(consts[:, :], const_h))
        E.note_w("q", B["consts"], c)
        c = E.inc("g", g.dma_start(onec[:, :], onec_h))
        E.note_w("g", B["onec"], c)
        c = E.inc("g", g.dma_start(oner[:, :], oner_h))
        E.note_w("g", B["oner"], c)
        c = E.inc("g", g.dma_start(ffn_sb[:, :], ffn_h))
        E.note_w("g", B["ffn"], c)

        E.r_wait("s", B["consts"])
        c = E.inc("s", s.mul(b2h[:, :], ccol("b2", 0, 8), 0.5))
        E.note_w("s", B["b2h"], c)
        c = E.inc("s", s.copy(lnb_bf[:, :], ccol("ln_b", 0, 16)))
        E.note_w("s", B["lnb"], c)

        sched = []
        for half in range(2):
            sched += [("ip", gi, 6144, ip_h[:, gi * 6144:(gi + 1) * 6144])
                      for gi in range(NG)]
            sched += [("fc", c, 4096, fc_h[:, c * 4096:(c + 1) * 4096])
                      for c in range(4)]
            sched += [("cp", c, 4096, cp_h[:, c * 4096:(c + 1) * 4096])
                      for c in range(4)]
            sched += [("ow", c, 4096, ow_h[:, c * 4096:(c + 1) * 4096])
                      for c in range(4)]
        ws = {"issued": 0, "dma": [None] * len(sched)}

        def ws_issue():
            i = ws["issued"]
            if i >= len(sched):
                return
            _, _, ncols, src = sched[i]
            E.w_wait("g", B_wslot[i % 2])
            c2 = E.inc("g", g.dma_start(wslot[i % 2][:, :ncols], src))
            E.note_w("g", B_wslot[i % 2], c2)
            ws["dma"][i] = c2
            ws["issued"] = i + 1

        def ws_tile(ci, which, key):
            j = ixs[which][key]
            return wslot[ci % 2][:, j * 128:(j + 1) * 128]

        def ws_ensure(ci):
            E.wait("t", "gd", ws["dma"][ci])

        def ws_done(ci):
            E.note_r("t", B_wslot[ci % 2], E.cnt["tc"])
            ws_issue()

        ws_issue()
        ws_issue()

        def bcast_to(src_ap, w, bi, B_src, dst_sb, B_dst):
            bk = banks[bi]
            E.w_wait("t", bk)
            E.r_wait("t", B["oner"])
            E.r_wait("t", B_src)
            ii = t.matmul(pt[bi][:, :w], oner[:, :], src_ap, start=True, stop=True)
            cnt = E.inc("t", ii)
            E.note_w("t", bk, cnt)
            E.note_r("t", B_src, cnt)
            E.r_wait("s", bk)
            E.w_wait("s", B_dst)
            c2 = E.inc("s", s.copy(dst_sb[:, 0:w], pt[bi][:, :w]))
            E.note_r("s", bk, c2)
            E.note_w("s", B_dst, c2)

        def preload_sqrt():
            pass

        def rstd_batch(m_sb, q_sb, B_m, B_q):
            E.r_wait("v", B_m)
            E.w_wait("v", B["varb"])
            c2 = E.inc("v", v.tensor_mul(varb[0:1, :], m_sb[0:1, :], m_sb[0:1, :]))
            E.note_r("v", B_m, c2)
            E.note_w("v", B["varb"], c2)
            E.r_wait("v", B_q)
            c2 = E.inc("v", v.tensor_tensor(varb[0:1, :], q_sb[0:1, :],
                                            varb[0:1, :], ALU.subtract))
            E.note_r("v", B_q, c2)
            E.note_w("v", B["varb"], c2)
            E.r_wait("v", B["consts"])
            c2 = E.inc("v", v.tensor_scalar_add(
                varb[0:1, :], varb[0:1, :],
                consts[0:1, ccols["eps"]:ccols["eps"] + 1]))
            E.note_w("v", B["varb"], c2)
            c2 = E.inc("v", v.reciprocal(varb[0:1, :], varb[0:1, :]))
            E.note_w("v", B["varb"], c2)
            E.r_wait("s", B["varb"])
            E.w_wait("s", B["rstd"])
            c2 = E.inc("s", s.activation(rstd[0:1, :], varb[0:1, :], AF.Sqrt))
            E.note_r("s", B["varb"], c2)
            E.note_w("s", B["rstd"], c2)

        store_last = [0]

        for half in range(2):
            with ExitStack() as hs:
                def rsb(name, shape, dt, stack):
                    return stack.enter_context(
                        nc.sbuf_tensor(f"{name}{half}", shape, dt, side="right"))

                xcur = rsb("xcur", [128, DC * TH], F32, hs)
                h = rsb("h", [128, IC * HT], BF16, hs)
                B_xcur, B_h = B["xcur"], B["h"]

                hap = h[:, :]
                XB, SQ = 0, 4096
                Pb = h[:, 8192:9216]
                Hq = h[:, 9216:13312]
                Qb = h[:, 13312:14336]
                TAIL = 14336
                B_xb, B_sq = B["xb"], B["sq"]
                B_Pb, B_Hq, B_Qb = B["Pb"], B["Hq"], B["Qb"]

                def hfl(base, w):
                    return h[:, base:base + w]

                def xcur_v(tcol, w):
                    return ap3(xcur[:, :], tcol, TH, DC, w)

                E.barrier(("t", "v", "s"), dma=True)
                E.wait("q", "tc", E.cnt["tc"])
                E.wait("q", "vc", E.cnt["vc"])
                E.wait("q", "sc", E.cnt["sc"])
                xcur_dma = [0, 0, 0]
                if half == 0:
                    E.w_wait("q", B_xcur)
                    for ti0, (tc0, w0) in enumerate(TILES):
                        c = E.inc("q", q.dma_start(
                            ap3(xcur[:, :], tc0, TH, DC, w0),
                            ap3(x_h, tc0, TH, DC, w0)))
                        xcur_dma[ti0] = c
                    E.note_w("q", B_xcur, c)

                rot = [0]

                def nb(held=()):
                    while True:
                        rot[0] = (rot[0] + 1) % 8
                        if rot[0] not in held:
                            return rot[0]

                Pb_c, Qb_c = [Pb], [Qb]
                B_Pb_c, B_Qb_c = [B_Pb], [B_Qb]
                SQr = h[:, 4096:8192]
                HqD = [[Hq, SQr]]
                B_HqD = [[B_Hq, B_sq]]

                def ffn_tile(xsrc, B_xsrc, dst_add, w, after_u1=None,
                             avoid=()):
                    for m in range(2):
                        bi = nb(held=avoid)
                        bk = banks[bi]
                        E.w_wait("t", bk)
                        for k in range(8):
                            E.r_wait("t", B_xsrc)
                            ii = t.matmul(pt[bi][:, :w], fftile("U1", k, m),
                                          xsrc(k), start=(k == 0), stop=(k == 7))
                        cnt = E.inc("t", ii)
                        E.note_w("t", bk, cnt)
                        E.note_r("t", B_xsrc, cnt)
                        E.r_wait("s", bk)
                        E.w_wait("s", B_Pb_c[0])
                        c2 = E.inc("s", s.copy(Pb_c[0][:, m * 512:m * 512 + w],
                                               pt[bi][:, :w]))
                        E.note_r("s", bk, c2)
                        E.note_w("s", B_Pb_c[0], c2)
                    if after_u1 is not None:
                        after_u1()
                    u2a = nb(held=avoid)
                    u2b_ = nb(held=(u2a,) + tuple(avoid))
                    u2 = (u2a, u2b_)
                    for mu in range(2):
                        E.w_wait("t", banks[u2[mu]])
                    for qq in range(4):
                        Hqb = HqD[0][qq % 2]
                        B_Hqb = B_HqD[0][qq % 2]
                        for mq in range(8):
                            m = qq * 8 + mq
                            bi = nb(held=u2 + tuple(avoid))
                            bk = banks[bi]
                            E.w_wait("t", bk)
                            for k in range(2):
                                E.r_wait("t", B_Pb_c[0])
                                ii = t.matmul(pt[bi][:, :w], fftile("V1", k, m),
                                              Pb_c[0][:, k * 512:k * 512 + w],
                                              start=(k == 0), stop=(k == 1))
                            cnt = E.inc("t", ii)
                            E.note_w("t", bk, cnt)
                            E.note_r("t", B_Pb_c[0], cnt)
                            E.r_wait("s", bk)
                            E.w_wait("s", B_Hqb)
                            E.r_wait("s", B["consts"])
                            c2 = E.inc("s", s.activation(
                                Hqb[:, mq * 512:mq * 512 + w], pt[bi][:, :w],
                                AF.Gelu_apprx_tanh, bias=ccol("b1", m), scale=1.0))
                            E.note_r("s", bk, c2)
                            E.note_w("s", B_Hqb, c2)
                        for mu in range(2):
                            for kq in range(8):
                                k = qq * 8 + kq
                                E.r_wait("t", B_Hqb)
                                ii = t.matmul(pt[u2[mu]][:, :w], fftile("U2", k, mu),
                                              Hqb[:, kq * 512:kq * 512 + w],
                                              start=(qq == 0 and kq == 0),
                                              stop=(qq == 3 and kq == 7))
                            cnt = E.inc("t", ii)
                            E.note_r("t", B_Hqb, cnt)
                            if qq == 3:
                                E.note_w("t", banks[u2[mu]], cnt)
                    for mu in range(2):
                        bk = banks[u2[mu]]
                        E.r_wait("s", bk)
                        E.w_wait("s", B_Qb_c[0])
                        c2 = E.inc("s", s.copy(Qb_c[0][:, mu * 512:mu * 512 + w],
                                               pt[u2[mu]][:, :w]))
                        E.note_r("s", bk, c2)
                        E.note_w("s", B_Qb_c[0], c2)
                    for m in range(8):
                        bi = nb(held=avoid)
                        bk = banks[bi]
                        E.w_wait("t", bk)
                        for k in range(2):
                            E.r_wait("t", B_Qb_c[0])
                            ii = t.matmul(pt[bi][:, :w], fftile("V2", k, m),
                                          Qb_c[0][:, k * 512:k * 512 + w],
                                          start=(k == 0), stop=(k == 1))
                        cnt = E.inc("t", ii)
                        E.note_w("t", bk, cnt)
                        E.note_r("t", B_Qb_c[0], cnt)
                        dst_add(m, bi, bk)

                preload_sqrt()
                for si, (tcol, w) in enumerate(TILES):
                    bm = 4 if si % 2 == 0 else 6
                    bq = bm + 1
                    E.w_wait("t", banks[bm])
                    E.w_wait("t", banks[bq])
                    for cch in range(DC):
                        sl = cch % 2
                        xsl = TAIL + sl * 512
                        qsl = TAIL + 1024 + sl * 512
                        if half == 0 and xcur_dma[si]:
                            E.wait("s", "sd", xcur_dma[si])
                        else:
                            E.r_wait("s", B_xcur)
                        E.w_wait("s", B_tail[sl])
                        c2 = E.inc("s", s.copy(
                            hfl(xsl, w), xcur[:, cch * TH + tcol:cch * TH + tcol + w]))
                        E.note_r("s", B_xcur, c2)
                        E.note_w("s", B_tail[sl], c2)
                        E.r_wait("v", B_tail[sl])
                        E.w_wait("v", B_tail[2 + sl])
                        c2 = E.inc("v", v.tensor_mul(hfl(qsl, w), hfl(xsl, w),
                                                     hfl(xsl, w)))
                        E.note_r("v", B_tail[sl], c2)
                        E.note_w("v", B_tail[2 + sl], c2)
                        E.r_wait("t", B_tail[sl])
                        E.r_wait("t", B["onec"])
                        ii = t.matmul(pt[bm][0:1, :w], onec[:, :], hfl(xsl, w),
                                      start=(cch == 0), stop=(cch == DC - 1))
                        cnt = E.inc("t", ii)
                        E.note_r("t", B_tail[sl], cnt)
                        if cch == DC - 1:
                            E.note_w("t", banks[bm], cnt)
                        E.r_wait("t", B_tail[2 + sl])
                        ii = t.matmul(pt[bq][0:1, :w], onec[:, :], hfl(qsl, w),
                                      start=(cch == 0), stop=(cch == DC - 1))
                        cnt = E.inc("t", ii)
                        E.note_r("t", B_tail[2 + sl], cnt)
                        if cch == DC - 1:
                            E.note_w("t", banks[bq], cnt)
                    for bb, dst, bfd in ((bm, mA, B["mA"]), (bq, qA, B["qA"])):
                        E.r_wait("s", banks[bb])
                        E.w_wait("s", bfd)
                        c2 = E.inc("s", s.activation(dst[0:1, tcol:tcol + w],
                                                     pt[bb][0:1, :w], AF.Copy,
                                                     scale=1.0 / D))
                        E.note_r("s", banks[bb], c2)
                        E.note_w("s", bfd, c2)
                rstd_batch(mA, qA, B["mA"], B["qA"])

                E.barrier(("t", "v", "s"))

                with ExitStack() as p1s:
                    xn = rsb("xn", [128, DC * TH], BF16, p1s)
                    ph = rsb("ph", [128, 2 * (3 + TH)], BF16, p1s)
                    gate = rsb("gate", [128, 2 * 512], BF16, p1s)
                    B_xn, B_ph, B_gate = B["xn"], B["ph"], B["gate"]
                    PHW = 3 + TH
                    def prep_xn1(ti2):
                        tcol2, w2 = TILES[ti2]
                        E.r_wait("v", B["mA"])
                        E.r_wait("v", B["rstd"])
                        E.w_wait("v", B_mrb)
                        c4 = E.inc("v", v.tensor_mul(
                            mrb[0:1, tcol2:tcol2 + w2],
                            mA[0:1, tcol2:tcol2 + w2],
                            rstd[0:1, tcol2:tcol2 + w2]))
                        E.note_r("v", B["mA"], c4)
                        E.note_r("v", B["rstd"], c4)
                        E.note_w("v", B_mrb, c4)
                        bcast_to(mrb[0:1, tcol2:tcol2 + w2], w2, 6, B_mrb,
                                 mS, B["mS"])
                        bcast_to(rstd[0:1, tcol2:tcol2 + w2], w2, 7, B["rstd"],
                                 rS, B["rS"])
                        xnv = ap3(xn[:, :], tcol2, TH, DC, w2)
                        E.r_wait("v", B["rS"])
                        E.r_wait("v", B_xcur)
                        E.w_wait("v", B_xn)
                        c4 = E.inc("v", v.tensor_tensor(
                            xnv, xcur_v(tcol2, w2),
                            bc_ap(rS[:, 0:w2], DC, w2), ALU.mult))
                        E.note_r("v", B["rS"], c4)
                        E.note_r("v", B_xcur, c4)
                        E.note_w("v", B_xn, c4)
                        E.r_wait("v", B["mS"])
                        c4 = E.inc("v", v.tensor_tensor(
                            xnv, xnv, bc_ap(mS[:, 0:w2], DC, w2),
                            ALU.subtract))
                        E.note_r("v", B["mS"], c4)
                        E.note_w("v", B_xn, c4)

                    pend_ln2 = []

                    def flush_ln2():
                        for fn in pend_ln2:
                            fn()
                        pend_ln2.clear()

                    prep_xn1(0)
                    for ti, (tcol, w) in enumerate(TILES):
                        def after_u1(_ti=ti):
                            flush_ln2()
                            if _ti + 1 < len(TILES):
                                prep_xn1(_ti + 1)

                        E.r_wait("v", B["b2h"])
                        E.r_wait("v", B_xcur)
                        for m5 in range(DC):
                            dst5 = xcur[:, m5 * TH + tcol:m5 * TH + tcol + w]
                            c5 = E.inc("v", v.tensor_scalar_add(
                                dst5, dst5, b2h[:, m5:m5 + 1]))
                        E.note_r("v", B["b2h"], c5)
                        E.note_w("v", B_xcur, c5)

                        def dst_add(m, bi, bk, _tcol=tcol, _w=w):
                            dstc = xcur[:, m * TH + _tcol:m * TH + _tcol + _w]
                            E.r_wait("v", bk)
                            c3 = E.inc("v", v.scalar_tensor_tensor(
                                dstc, pt[bi][:, :_w], 0.5, dstc,
                                ALU.mult, ALU.add))
                            E.note_r("v", bk, c3)
                            E.note_w("v", B_xcur, c3)

                        ffn_tile(lambda k, _t=tcol, _w=w: xn[:, k * TH + _t:k * TH + _t + _w],
                                 B_xn, dst_add, w, after_u1=after_u1,
                                 avoid=(6, 7))

                        E.r_wait("s", B_xcur)
                        E.w_wait("s", B_xb)
                        c2 = E.inc("s", s.copy(ap3(hap, XB, 512, DC, w),
                                               xcur_v(tcol, w)))
                        E.note_r("s", B_xcur, c2)
                        E.note_w("s", B_xb, c2)
                        E.r_wait("v", B_xb)
                        E.w_wait("v", B_sq)
                        c2 = E.inc("v", v.tensor_mul(ap3(hap, SQ, 512, DC, w),
                                                     ap3(hap, XB, 512, DC, w),
                                                     ap3(hap, XB, 512, DC, w)))
                        E.note_r("v", B_xb, c2)
                        E.note_w("v", B_sq, c2)
                        def _ln2mm(_tcol=tcol, _w=w):
                            E.w_wait("t", banks[4])
                            E.r_wait("t", B_xb)
                            E.r_wait("t", B["onec"])
                            for k in range(DC):
                                ii2 = t.matmul(pt[4][0:1, :_w], onec[:, :],
                                               hfl(XB + k * 512, _w),
                                               start=(k == 0), stop=(k == DC - 1))
                            cn = E.inc("t", ii2)
                            E.note_w("t", banks[4], cn)
                            E.note_r("t", B_xb, cn)
                            E.w_wait("t", banks[5])
                            E.r_wait("t", B_sq)
                            for k in range(DC):
                                ii2 = t.matmul(pt[5][0:1, :_w], onec[:, :],
                                               hfl(SQ + k * 512, _w),
                                               start=(k == 0), stop=(k == DC - 1))
                            cn = E.inc("t", ii2)
                            E.note_w("t", banks[5], cn)
                            E.note_r("t", B_sq, cn)
                            for bb, dst, bfd in ((4, mB, B["mB"]), (5, qB, B["qA"])):
                                E.r_wait("s", banks[bb])
                                E.w_wait("s", bfd)
                                c3 = E.inc("s", s.activation(
                                    dst[0:1, _tcol:_tcol + _w],
                                    pt[bb][0:1, :_w], AF.Copy, scale=1.0 / D))
                                E.note_r("s", banks[bb], c3)
                                E.note_w("s", bfd, c3)
                        pend_ln2.append(_ln2mm)
                    preload_sqrt()
                    flush_ln2()
                    rstd_batch(mB, qB, B["mB"], B["qA"])

                    def p0c_tile(ti3):
                        tcol3, w3 = TILES[ti3]
                        E.r_wait("v", B["mB"])
                        E.r_wait("v", B["rstd"])
                        E.w_wait("v", B_mrb)
                        c4 = E.inc("v", v.tensor_mul(
                            mrb[0:1, tcol3:tcol3 + w3],
                            mB[0:1, tcol3:tcol3 + w3],
                            rstd[0:1, tcol3:tcol3 + w3]))
                        E.note_r("v", B["mB"], c4)
                        E.note_r("v", B["rstd"], c4)
                        E.note_w("v", B_mrb, c4)
                        bcast_to(mrb[0:1, tcol3:tcol3 + w3], w3, 6, B_mrb,
                                 mS, B["mS"])
                        bcast_to(rstd[0:1, tcol3:tcol3 + w3], w3, 7, B["rstd"],
                                 rS, B["rS"])
                        xnv = ap3(xn[:, :], tcol3, TH, DC, w3)
                        E.r_wait("v", B["rS"])
                        E.r_wait("v", B_xcur)
                        E.w_wait("v", B_xn)
                        c4 = E.inc("v", v.tensor_tensor(
                            xnv, xcur_v(tcol3, w3),
                            bc_ap(rS[:, 0:w3], DC, w3), ALU.mult))
                        E.note_r("v", B["rS"], c4)
                        E.note_r("v", B_xcur, c4)
                        E.note_w("v", B_xn, c4)
                        E.r_wait("v", B["mS"])
                        c4 = E.inc("v", v.tensor_tensor(
                            xnv, xnv, bc_ap(mS[:, 0:w3], DC, w3),
                            ALU.subtract))
                        E.note_r("v", B["mS"], c4)
                        E.note_w("v", B_xn, c4)

                    E.barrier(("v",))

                    HELD = (2, 3, 4, 5)
                    for pr in HELD:
                        E.w_wait("t", banks[pr])
                    pending_stats = []

                    def flush_stats():
                        for fn in pending_stats:
                            fn()
                        pending_stats.clear()

                    for gi in range(NG):
                        ci = half * 20 + gi
                        ws_ensure(ci)
                        for tix, (tcol, w) in enumerate(TILES):
                            if gi == 0:
                                p0c_tile(tix)
                            for ml in range(2):
                                bi = nb(held=HELD)
                                bk = banks[bi]
                                E.w_wait("t", bk)
                                for k in range(8):
                                    E.r_wait("t", B_xn)
                                    ii = t.matmul(
                                        pt[bi][:, :w],
                                        ws_tile(ci, "ip", ("Wh", k, ml)),
                                        xn[:, k * TH + tcol:k * TH + tcol + w],
                                        start=(k == 0), stop=(k == 7))
                                cnt = E.inc("t", ii)
                                E.note_w("t", bk, cnt)
                                E.note_r("t", B_xn, cnt)
                                if ml == 0:
                                    flush_stats()
                                E.r_wait("s", bk)
                                E.w_wait("s", B_ph)
                                c2 = E.inc("s", s.copy(
                                    ph[:, ml * PHW + 3 + tcol:
                                       ml * PHW + 3 + tcol + w], pt[bi][:, :w]))
                                E.note_r("s", bk, c2)
                                E.note_w("s", B_ph, c2)
                            if tcol == 0:
                                E.r_wait("v", B_ph)
                                E.r_wait("v", B["consts"])
                                mv = ap3(ph[:, :], 32, PHW, 2, 3)
                                c2 = E.inc("v", v.tensor_scalar_mul(
                                    mv, mv, ccol("mask", half, 1)))
                                E.note_r("v", B["consts"], c2)
                                E.note_w("v", B_ph, c2)
                                continue
                            for ml in range(2):
                                bi = nb(held=HELD)
                                bk = banks[bi]
                                E.w_wait("t", bk)
                                for k in range(8):
                                    ii = t.matmul(
                                        pt[bi][:, :w],
                                        ws_tile(ci, "ip", ("Wg", k, ml)),
                                        xn[:, k * TH + tcol:k * TH + tcol + w],
                                        start=(k == 0), stop=(k == 7))
                                cnt = E.inc("t", ii)
                                E.note_w("t", bk, cnt)
                                E.note_r("t", B_xn, cnt)
                                E.r_wait("s", bk)
                                E.w_wait("s", B_gate)
                                c2 = E.inc("s", s.activation(
                                    gate[:, ml * 512:ml * 512 + w],
                                    pt[bi][:, :w], AF.Silu))
                                E.note_r("s", bk, c2)
                                E.note_w("s", B_gate, c2)
                            for mc in range(2):
                                bi = nb(held=HELD)
                                bk = banks[bi]
                                E.w_wait("t", bk)
                                first = True
                                for tap in range(KTAP):
                                    for kc in range(2):
                                        E.r_wait("t", B_ph)
                                        ii = t.matmul(
                                            pt[bi][:, :w],
                                            ws_tile(ci, "ip", ("cv", tap, kc, mc)),
                                            ph[:, kc * PHW + tcol + tap:
                                               kc * PHW + tcol + tap + w],
                                            start=first,
                                            stop=(tap == 3 and kc == 1))
                                        first = False
                                cnt = E.inc("t", ii)
                                E.note_w("t", bk, cnt)
                                E.note_r("t", B_ph, cnt)
                                cch = 2 * gi + mc
                                base = cch * HT + (tcol - HALO)
                                E.r_wait("v", bk)
                                E.r_wait("v", B_gate)
                                E.r_wait("v", B["consts"])
                                E.w_wait("v", B_h)
                                c2 = E.inc("v", v.scalar_tensor_tensor(
                                    h[:, base:base + w], pt[bi][:, :w],
                                    ccol("conv_b", cch, 1),
                                    gate[:, mc * 512:mc * 512 + w],
                                    ALU.add, ALU.mult))
                                E.note_r("v", bk, c2)
                                E.note_r("v", B_gate, c2)
                                E.note_w("v", B_h, c2)
                                si2 = tix - 1
                                bm, bq = (2, 3) if si2 == 0 else (4, 5)
                                gsl = gate[:, mc * 512:mc * 512 + w]
                                E.r_wait("v", B_h)
                                E.w_wait("v", B_gate)
                                c2 = E.inc("v", v.tensor_mul(
                                    gsl, h[:, base:base + w],
                                    h[:, base:base + w]))
                                E.note_r("v", B_h, c2)
                                E.note_w("v", B_gate, c2)

                                def _stats(_bm=bm, _bq=bq, _base=base, _w=w,
                                           _gsl=gsl, _gi=gi, _mc=mc):
                                    E.r_wait("t", B_h)
                                    E.r_wait("t", B["onec"])
                                    ii2 = t.matmul(pt[_bm][0:1, :_w], onec[:, :],
                                                   h[:, _base:_base + _w],
                                                   start=(_gi == 0 and _mc == 0),
                                                   stop=(_gi == NG - 1 and _mc == 1))
                                    cn = E.inc("t", ii2)
                                    E.note_r("t", B_h, cn)
                                    if _gi == NG - 1 and _mc == 1:
                                        E.note_w("t", banks[_bm], cn)
                                    E.r_wait("t", B_gate)
                                    ii2 = t.matmul(pt[_bq][0:1, :_w], onec[:, :],
                                                   _gsl,
                                                   start=(_gi == 0 and _mc == 0),
                                                   stop=(_gi == NG - 1 and _mc == 1))
                                    cn = E.inc("t", ii2)
                                    E.note_r("t", B_gate, cn)
                                    if _gi == NG - 1 and _mc == 1:
                                        E.note_w("t", banks[_bq], cn)
                                pending_stats.append(_stats)
                        ws_done(ci)
                    flush_stats()
                    preload_sqrt()
                    for si2, (tcol, w) in enumerate(OWN_TILES):
                        bm, bq = (2, 3) if si2 == 0 else (4, 5)
                        for bb, dst, bfd in ((bm, mA, B["mA"]), (bq, qA, B["qA"])):
                            E.r_wait("s", banks[bb])
                            E.w_wait("s", bfd)
                            c2 = E.inc("s", s.activation(
                                dst[0:1, tcol:tcol + w], pt[bb][0:1, :w],
                                AF.Copy, scale=1.0 / I2))
                            E.note_r("s", banks[bb], c2)
                            E.note_w("s", bfd, c2)
                rstd_batch(mA, qA, B["mA"], B["qA"])

                with ExitStack() as p2s:
                    hn = rsb("hn", [128, IC * HT], BF16, p2s)
                    m1 = rsb("m1", [128, DC * HT], BF16, p2s)
                    B_hn, B_m1 = B["hn"], B["m1"]

                    def h_v(t0o, w):
                        return ap3(h[:, :], t0o, HT, IC, w)

                    def hn_v(t0o, w):
                        return ap3(hn[:, :], t0o, HT, IC, w)

                    mS2 = xb3[:, 0:512]
                    rS2 = xb3[:, 512:1024]
                    (t1c, t1w), (t2c, t2w) = OWN_TILES
                    bcast_to(mA[0:1, t1c:t1c + t1w], t1w, 6, B["mA"], mS, B["mS"])
                    bcast_to(rstd[0:1, t1c:t1c + t1w], t1w, 7, B["rstd"], rS, B["rS"])
                    bcast_to(mA[0:1, t2c:t2c + t2w], t2w, 6, B["mA"], mS2, B_xb3[0])
                    bcast_to(rstd[0:1, t2c:t2c + t2w], t2w, 7, B["rstd"], rS2, B_xb3[1])
                    hn_ready = [0] * IC
                    for k in range(IC):
                        for si, (tcol, w) in enumerate(OWN_TILES):
                            t0o = tcol - HALO
                            base = k * HT + t0o
                            msl = mS if si == 0 else mS2
                            rsl = rS if si == 0 else rS2
                            bmsl = B["mS"] if si == 0 else B_xb3[0]
                            brsl = B["rS"] if si == 0 else B_xb3[1]
                            E.r_wait("v", bmsl)
                            E.r_wait("v", B_h)
                            E.w_wait("v", B_hn)
                            c2 = E.inc("v", v.tensor_tensor(
                                hn[:, base:base + w], h[:, base:base + w],
                                msl[:, 0:w], ALU.subtract))
                            E.note_r("v", bmsl, c2)
                            E.note_r("v", B_h, c2)
                            E.note_w("v", B_hn, c2)
                            E.r_wait("v", brsl)
                            E.r_wait("v", B["consts"])
                            c2 = E.inc("v", v.scalar_tensor_tensor(
                                hn[:, base:base + w], hn[:, base:base + w],
                                ccol("ln_g", k, 1), rsl[:, 0:w],
                                ALU.mult, ALU.mult))
                            E.note_r("v", brsl, c2)
                            E.note_w("v", B_hn, c2)
                        hn_ready[k] = E.cnt["vc"]

                    for m in range(8):
                        ci = half * 20 + 8 + m // 2
                        if m % 2 == 0:
                            ws_ensure(ci)
                        if half == 0:
                            bi = nb()
                            bk = banks[bi]
                            E.w_wait("t", bk)
                            E.r_wait("t", B["lnb"])
                            for k in range(16):
                                ii = t.matmul(pt[bi][:, 0:1],
                                              ws_tile(ci, "fc", ("fc", m % 2, k)),
                                              lnb_bf[:, k:k + 1],
                                              start=(k == 0), stop=(k == 15))
                            cnt = E.inc("t", ii)
                            E.note_w("t", bk, cnt)
                            E.r_wait("v", bk)
                            E.w_wait("v", B["fcb"])
                            E.r_wait("v", B["consts"])
                            c2 = E.inc("v", v.tensor_scalar_add(
                                fcb[:, m:m + 1], pt[bi][:, 0:1],
                                ccol("fc_b", m, 1)))
                            E.note_r("v", bk, c2)
                            E.note_w("v", B["fcb"], c2)
                        for (tcol, w) in OWN_TILES:
                            t0o = tcol - HALO
                            bi = nb()
                            bk = banks[bi]
                            E.w_wait("t", bk)
                            for k in range(16):
                                E.wait("t", "vc", hn_ready[k])
                                ii = t.matmul(pt[bi][:, :w],
                                              ws_tile(ci, "fc", ("fc", m % 2, k)),
                                              hn[:, k * HT + t0o:k * HT + t0o + w],
                                              start=(k == 0), stop=(k == 15))
                            cnt = E.inc("t", ii)
                            E.note_w("t", bk, cnt)
                            E.note_r("t", B_hn, cnt)
                            E.r_wait("s", bk)
                            E.w_wait("s", B_m1)
                            E.r_wait("s", B["fcb"])
                            c2 = E.inc("s", s.activation(
                                m1[:, m * HT + t0o:m * HT + t0o + w],
                                pt[bi][:, :w], AF.Silu,
                                bias=fcb[:, m:m + 1], scale=1.0))
                            E.note_r("s", bk, c2)
                            E.note_w("s", B_m1, c2)
                        if m % 2 == 1:
                            ws_done(ci)

                    for m in range(16):
                        ci = half * 20 + 12 + m // 4
                        if m % 4 == 0:
                            ws_ensure(ci)
                        for (tcol, w) in OWN_TILES:
                            t0o = tcol - HALO
                            bi = nb()
                            bk = banks[bi]
                            E.w_wait("t", bk)
                            for k in range(8):
                                E.r_wait("t", B_m1)
                                ii = t.matmul(pt[bi][:, :w],
                                              ws_tile(ci, "cp", ("cp", m % 4, k)),
                                              m1[:, k * HT + t0o:k * HT + t0o + w],
                                              start=(k == 0), stop=(k == 7))
                            cnt = E.inc("t", ii)
                            E.note_w("t", bk, cnt)
                            E.note_r("t", B_m1, cnt)
                            base = m * HT + t0o
                            E.r_wait("v", bk)
                            E.r_wait("v", B["consts"])
                            E.w_wait("v", B_h)
                            c2 = E.inc("v", v.scalar_tensor_tensor(
                                h[:, base:base + w], pt[bi][:, :w],
                                ccol("cproj_b", m, 1), h[:, base:base + w],
                                ALU.add, ALU.add))
                            E.note_r("v", bk, c2)
                            E.note_w("v", B_h, c2)
                        if m % 4 == 3:
                            ws_done(ci)

                x2 = hs.enter_context(
                    nc.sbuf_tensor(f"x2_{half}", [128, DC * HT], F32, side="left"))
                B_x2 = B["x2"]
                preload_sqrt()
                stat_banks = {0: (4, 5), 1: (2, 3)}
                mm_banks = {0: (0, 6), 1: (1, 7)}
                for pr in stat_banks.values():
                    for bi in pr:
                        E.w_wait("t", banks[bi])
                pending3 = []

                def flush3():
                    for fn in pending3:
                        fn()
                    pending3.clear()

                for m in range(8):
                    ci = half * 20 + 16 + m // 2
                    if m % 2 == 0:
                        ws_ensure(ci)
                    for si, (tcol, w) in enumerate(OWN_TILES):
                        t0o = tcol - HALO
                        bi = mm_banks[si][m % 2]
                        bk = banks[bi]
                        E.w_wait("t", bk)
                        for k in range(16):
                            E.r_wait("t", B_h)
                            ii = t.matmul(pt[bi][:, :w],
                                          ws_tile(ci, "ow", ("ow", m % 2, k)),
                                          h[:, k * HT + t0o:k * HT + t0o + w],
                                          start=(k == 0), stop=(k == 15))
                        cnt = E.inc("t", ii)
                        E.note_w("t", bk, cnt)
                        E.note_r("t", B_h, cnt)
                        flush3()
                        base = m * HT + t0o
                        E.r_wait("v", bk)
                        E.r_wait("v", B_xcur)
                        E.w_wait("v", B_x2)
                        c2 = E.inc("v", v.tensor_add(
                            x2[:, base:base + w], pt[bi][:, :w],
                            xcur[:, m * TH + tcol:m * TH + tcol + w]))
                        E.note_r("v", bk, c2)
                        E.note_r("v", B_xcur, c2)
                        E.note_w("v", B_x2, c2)
                        sl = si
                        E.r_wait("s", B_x2)
                        E.w_wait("s", B_xb3[sl])
                        c2 = E.inc("s", s.copy(xb3[:, sl * 512:sl * 512 + w],
                                               x2[:, base:base + w]))
                        E.note_r("s", B_x2, c2)
                        E.note_w("s", B_xb3[sl], c2)
                        E.r_wait("v", B_xb3[sl])
                        E.w_wait("v", B_sq3[sl])
                        c2 = E.inc("v", v.tensor_mul(sq3[:, sl * 512:sl * 512 + w],
                                                     xb3[:, sl * 512:sl * 512 + w],
                                                     xb3[:, sl * 512:sl * 512 + w]))
                        E.note_r("v", B_xb3[sl], c2)
                        E.note_w("v", B_sq3[sl], c2)
                        bmi, bqi = stat_banks[si]

                        def _st3(_bmi=bmi, _bqi=bqi, _sl=sl, _m=m, _w=w):
                            E.r_wait("t", B_xb3[_sl])
                            ii2 = t.matmul(pt[_bmi][0:1, :_w], onec[:, :],
                                           xb3[:, _sl * 512:_sl * 512 + _w],
                                           start=(_m == 0), stop=(_m == 7))
                            cn = E.inc("t", ii2)
                            E.note_r("t", B_xb3[_sl], cn)
                            if _m == 7:
                                E.note_w("t", banks[_bmi], cn)
                            E.r_wait("t", B_sq3[_sl])
                            ii2 = t.matmul(pt[_bqi][0:1, :_w], onec[:, :],
                                           sq3[:, _sl * 512:_sl * 512 + _w],
                                           start=(_m == 0), stop=(_m == 7))
                            cn = E.inc("t", ii2)
                            E.note_r("t", B_sq3[_sl], cn)
                            if _m == 7:
                                E.note_w("t", banks[_bqi], cn)
                        pending3.append(_st3)
                    if m % 2 == 1:
                        ws_done(ci)
                flush3()
                if half == 0:
                    E.w_wait("q", B_xcur)
                    c = E.inc("q", q.dma_start(
                        xcur[:, :], x_h[:, DC * TH:2 * DC * TH]))
                    E.note_w("q", B_xcur, c)
                for si, (tcol, w) in enumerate(OWN_TILES):
                    bmi, bqi = stat_banks[si]
                    for bb, dst, bfd in ((bmi, mA, B["mA"]), (bqi, qA, B["qA"])):
                        E.r_wait("s", banks[bb])
                        E.w_wait("s", bfd)
                        c2 = E.inc("s", s.activation(dst[0:1, tcol:tcol + w],
                                                     pt[bb][0:1, :w], AF.Copy,
                                                     scale=1.0 / D))
                        E.note_r("s", banks[bb], c2)
                        E.note_w("s", bfd, c2)
                rstd_batch(mA, qA, B["mA"], B["qA"])

                xc3 = h[:, 4096:8192]
                HqB2 = h[:, 0:4096]
                HqD[0] = [Hq, HqB2]
                B_HqD[0] = [B_Hq, B["Hq2"]]
                Pb_c[0], Qb_c[0] = Pb, Qb
                B_xc3 = B["xc3"]
                E.barrier(("v", "s"))

                def prep_xc3(si4):
                    tcol4, w4 = OWN_TILES[si4]
                    t0o4 = tcol4 - HALO
                    bcast_to(mA[0:1, tcol4:tcol4 + w4], w4, 6, B["mA"],
                             mS, B["mS"])
                    bcast_to(rstd[0:1, tcol4:tcol4 + w4], w4, 7, B["rstd"],
                             rS, B["rS"])
                    E.r_wait("v", B["mS"])
                    E.r_wait("v", B_x2)
                    E.w_wait("v", B_xc3)
                    xc3d = ap3(xc3, 0, 512, DC, w4)
                    c2 = E.inc("v", v.tensor_tensor(
                        xc3d, ap3(x2[:, :], t0o4, HT, DC, w4),
                        bc_ap(mS[:, 0:w4], DC, w4), ALU.subtract))
                    E.note_r("v", B["mS"], c2)
                    E.note_r("v", B_x2, c2)
                    E.note_w("v", B_xc3, c2)
                    E.r_wait("v", B["rS"])
                    c2 = E.inc("v", v.tensor_tensor(
                        xc3d, xc3d, bc_ap(rS[:, 0:w4], DC, w4), ALU.mult))
                    E.note_r("v", B["rS"], c2)
                    E.note_w("v", B_xc3, c2)

                prep_xc3(0)
                for si, (tcol, w) in enumerate(OWN_TILES):
                    t0o = tcol - HALO

                    def dst_add2(m, bi, bk, _si=si, _t0o=t0o, _tcol=tcol, _w=w):
                        sl = m % 2
                        E.r_wait("s", bk)
                        E.w_wait("s", B_tsl[sl])
                        c3 = E.inc("s", s.mul(tsl[:, sl * 512:sl * 512 + _w],
                                              pt[bi][:, :_w], 0.5))
                        E.note_r("s", bk, c3)
                        E.note_w("s", B_tsl[sl], c3)
                        ob = B_outb[sl]
                        E.r_wait("v", B_tsl[sl])
                        E.r_wait("v", B_x2)
                        E.r_wait("v", B["b2h"])
                        E.w_wait("v", ob)
                        c3 = E.inc("v", v.scalar_tensor_tensor(
                            outb[:, sl * 512:sl * 512 + _w],
                            tsl[:, sl * 512:sl * 512 + _w],
                            b2h[:, m:m + 1],
                            x2[:, m * HT + _t0o:m * HT + _t0o + _w],
                            ALU.add, ALU.add))
                        E.note_r("v", B_tsl[sl], c3)
                        E.note_r("v", B_x2, c3)
                        E.note_w("v", ob, c3)
                        E.r_wait("q", ob)
                        c3 = E.inc("q", q.dma_start(
                            y_h[:, m * 2048 + half * HT + _t0o:
                                m * 2048 + half * HT + _t0o + _w],
                            outb[:, sl * 512:sl * 512 + _w]))
                        E.note_r("q", ob, c3)
                        store_last[0] = c3

                    ffn_tile(lambda k: xc3[:, k * 512:k * 512 + w],
                             B_xc3, dst_add2, w, avoid=(6, 7),
                             after_u1=(lambda _si=si: prep_xc3(_si + 1)
                                       if _si + 1 < len(OWN_TILES) else None))

        E.wait("q", "sd", store_last[0])
    return nc


def kernel(**inputs):
    blobs, ixs, const_blob, ccols = build_host(inputs)
    x_in = np.asarray(inputs["L0_attn_hidden_states"], np.float32)
    xs = shard_x(x_in)
    nc = build_nc(ixs, ccols, const_blob.shape[1])
    onec = np.ones((128, 1), np.float32)
    oner = np.ones((1, 128), np.float32)
    in_maps = []
    for core in range(NCORES):
        chunk = core % 2
        cb = const_blob.copy()
        cb[:, ccols["mask"] + 0] = 1.0 if chunk == 1 else 0.0
        cb[:, ccols["mask"] + 1] = 1.0
        in_maps.append(dict(x=xs[core], const_blob=cb, onec=onec, oner=oner,
                            **blobs))
    res = run_bass_kernel_spmd(nc, in_maps, core_ids=list(range(NCORES)))
    return gather_y(res.results)
